# version 28
# baseline (speedup 1.0000x reference)
import contextlib
import numpy as np
import ml_dtypes
import concourse.bass as bass
import concourse.mybir as mybir
from concourse.bass_utils import run_bass_kernel_spmd

F32 = mybir.dt.float32
BF16 = mybir.dt.bfloat16
FP8 = mybir.dt.float8e4
AF = mybir.ActivationFunctionType
ALU = mybir.AluOpType
AX = mybir.AxisListType

M32 = float(np.float32(1.5 * 2 ** 23))
M52 = float(np.float32(1.5 * 2 ** 52))
EPS = 1e-5


class CFG:
    def __init__(self, T=2048, C=1024, HID=4096, D=64, TC=256):
        self.T = T
        self.C = C
        self.HID = HID
        self.D = D
        self.H = C // D
        self.TC = TC
        self.NT = T // 128
        self.NCC = C // 128
        self.NHC = HID // 128
        self.NFC = T // TC


class EZ:
    CENG = ("tensor", "vector", "scalar", "gpsimd")

    def __init__(self, nc, es):
        self.nc = nc
        self.osem = {}
        self.ocnt = {}
        for e in self.CENG:
            self.osem[e] = es.enter_context(nc.semaphore("o_" + e))
            self.ocnt[e] = 0
        self.dsem = {}
        self.wm = {}
        self.wr = {}
        self.rd = {}
        self.pend = {e: [] for e in self.CENG + ("sync",)}
        self.pend_rd = {e: [] for e in self.CENG + ("sync",)}
        self.es = es

    def mksem(self, name):
        self.dsem[name] = [self.es.enter_context(self.nc.semaphore(name)), 0]

    def _emit_wait(self, engine, evt):
        if evt is None:
            return
        if evt[0] == "e":
            _, peng, val = evt
            key = (engine, "o", peng)
            if self.wm.get(key, 0) >= val:
                return
            getattr(self.nc, engine).wait_ge(self.osem[peng], val)
            self.wm[key] = val
        else:
            _, sname, val = evt
            key = (engine, "d", sname)
            if self.wm.get(key, 0) >= val:
                return
            getattr(self.nc, engine).wait_ge(self.dsem[sname][0], val)
            self.wm[key] = val

    def _pre(self, engine, rd, wr):
        need = {}

        def add(evt):
            if evt is None:
                return
            key = ("o", evt[1]) if evt[0] == "e" else ("d", evt[1])
            if need.get(key, 0) < evt[2]:
                need[key] = evt[2]

        for k in rd:
            add(self.wr.get(k))
        for k in wr:
            add(self.wr.get(k))
            for evt in self.rd.get(k, ()):
                add(evt)
        for key, val in need.items():
            self._emit_wait(engine, ("e", key[1], val) if key[0] == "o"
                            else ("d", key[1], val))

    def _post(self, engine, evt, rd, wr, sig):
        if sig:
            for k in self.pend[engine]:
                self.wr[k] = evt
                self.rd[k] = []
            self.pend[engine] = []
            for k in wr:
                self.wr[k] = evt
                self.rd[k] = []
            for k in self.pend_rd[engine]:
                self.rd.setdefault(k, []).append(evt)
            self.pend_rd[engine] = []
            for k in rd:
                self.rd.setdefault(k, []).append(evt)
        else:
            self.pend[engine].extend(wr)
            self.pend_rd[engine].extend(rd)

    def op(self, engine, fname, *args, rd=(), wr=(), sig=True, **kw):
        self._pre(engine, rd, wr)
        instr = getattr(getattr(self.nc, engine), fname)(*args, **kw)
        evt = None
        if sig:
            self.ocnt[engine] += 1
            instr.then_inc(self.osem[engine], 1)
            evt = ("e", engine, self.ocnt[engine])
        self._post(engine, evt, rd, wr, sig)
        return instr

    def dma(self, engine, out, in_, sname, rd=(), wr=(), transpose=False, **dkw):
        self._pre(engine, rd, wr)
        kw = dict(dkw)
        if transpose:
            kw["transpose"] = True
        instr = getattr(self.nc, engine).dma_start(out=out, in_=in_, **kw)
        s = self.dsem[sname]
        s[1] += 16
        instr.then_inc(s[0], 16)
        self._post(engine, ("d", sname, s[1]), rd, wr, True)
        return instr

    def seal(self, sname):
        tot = self.dsem[sname][1]
        for k, evt in self.wr.items():
            if evt is not None and evt[0] == "d" and evt[1] == sname:
                self.wr[k] = ("d", sname, tot)

    def barrier(self, dmas=()):
        for e in ("tensor", "scalar", "gpsimd"):
            if self.ocnt[e] > 0:
                self._emit_wait("vector", ("e", e, self.ocnt[e]))
        for dn in dmas:
            if self.dsem[dn][1] > 0:
                self._emit_wait("vector", ("d", dn, self.dsem[dn][1]))
        self.ocnt["vector"] += 1
        self.nc.vector.engine_nop().then_inc(self.osem["vector"], 1)
        evt = ("e", "vector", self.ocnt["vector"])
        for e in ("tensor", "scalar", "gpsimd", "sync"):
            self._emit_wait(e, evt)


def rnd32(x):
    return (x + 31) // 32 * 32


def build_nc(cfg: CFG):
    nc = bass.Bass("TRN2", target_bir_lowering=False)
    T, C, HID, D, H = cfg.T, cfg.C, cfg.HID, cfg.D, cfg.H
    NT, NCC, NHC, TC, NFC = cfg.NT, cfg.NCC, cfg.NHC, cfg.TC, cfg.NFC
    HP = H // 2
    TT = TC // 128
    OW = min(512, C)
    NO2 = C // OW
    TQW = min(512, T)
    NT4 = T // TQW
    HW_ = TC + 2
    inv_C = 1.0 / C
    NJ = 6 * NCC
    THALF = min(1024, T)
    NKH = T // THALF
    KHT = THALF // 128
    PW1 = min(1024, HID)
    PW2 = min(1024, C)
    NQ = max(1, NHC // 8)
    QH = NHC // NQ
    QW1 = NCC * HID // NQ
    QW2 = NHC * C // NQ
    GRP = 8 if NHC >= 8 else NHC
    NG = NHC // GRP

    d = {}

    def din(name, shape, dtype=F32):
        d[name] = nc.dram_tensor(name, shape, dtype, kind="ExternalInput").ap()

    din("x", [T, C])
    din("wqT", [C, C]); din("wkT", [C, C]); din("wvT", [C, C]); din("woT", [C, C])
    din("w1T", [C, HID]); din("w2T", [HID, C])
    din("wmod", [C, 6 * C])
    din("c_fm", [128, NCC])
    din("bq_row", [1, C]); din("bv_row", [1, C]); din("bo_row", [1, C])
    din("lng_fm", [128, NCC]); din("lnb_fm", [128, NCC])
    din("convw_fm", [128, 3 * NHC]); din("convb_fm", [128, NHC])
    din("ident_bf", [128, 128], BF16)
    din("ident_f32", [128, 128])
    din("ones_row_bf", [1, 128], BF16)
    din("ones_row_f32", [1, 128])
    din("onesmat_bf", [128, 128], BF16)
    out_hbm = nc.dram_tensor("out", [T, C], F32, kind="ExternalOutput").ap()
    wq1h = nc.dram_tensor("wq1h", [128, NCC * HID], FP8,
                          kind="ExternalOutput").ap()
    wq2h = nc.dram_tensor("wq2h", [128, NHC * C], FP8,
                          kind="ExternalOutput").ap()

    es = contextlib.ExitStack()
    ez = EZ(nc, es)
    A = {}

    def sb(name, shape, dtype):
        A[name] = es.enter_context(nc.sbuf_tensor("s_" + name, shape, dtype))
        return A[name]

    NW1P = NCC * (HID // PW1)
    NW2P = NHC * (C // PW2)
    for nm, shp, dt_ in [
        ("ident_bf", [128, 128], BF16), ("ident_f32", [128, 128], F32),
        ("ones_row_bf", [1, 128], BF16), ("ones_row_f32", [1, 128], F32),
        ("onesmat_bf", [128, 128], BF16),
        ("c_fm", [128, NCC], F32), ("c_sil", [128, NCC], BF16),
        ("bq_row", [1, C], BF16), ("bv_row", [1, C], BF16),
        ("bo_row", [1, C], BF16),
        ("lng_fm", [128, NCC], F32), ("lnb_fm", [128, NCC], F32),
        ("convw_fm", [128, 3 * NHC], F32), ("convb_fm", [128, NHC], F32),
        ("stats", [128, 3 * NT], F32), ("sgn32", [128, NCC], F32),
        ("mod_col", [128, 6 * NCC], F32), ("modpart", [128, 6 * NCC], F32),
        ("ca1_col", [128, NCC], F32), ("sa_col", [128, NCC], F32),
        ("ga_col", [128, NCC], F32), ("gb_col", [128, NCC], F32),
        ("cb1_rep", [128, C], F32), ("sb_rep", [128, C], F32),
        ("ga_rep", [128, C], BF16), ("gb_rep", [128, C], BF16),
        ("kstats", [128, 2 * NCC], F32),
        ("epsv", [128, 1], F32), ("qstat", [128, 2 * H + 1], F32),
        ("state_sb", [128, HP * D], F32), ("state_bf", [128, HP * D], BF16),
        ("fstats", [128, 5 * TT], F32), ("s2c", [128, 3 * TT], F32),
        ("rowtc", [1, TC], F32), ("wsc", [128, 8], F32),
        ("wsums", [128, NW1P + NW2P + 8], F32), ("wrow", [1, 8], F32),
        ("tapsf", [128, 3 * NHC], F32),
        ("s2rep", [128, TC], BF16),
        ("hsel", [128, 1], F32), ("nm32c", [128, 1], F32),
        ("m52c", [128, 1], F32),
    ]:
        sb(nm, shp, dt_)

    RB1E = max(16 * C, NT * C, NCC * T,
               TT * C + 2 * rnd32(NCC * HW_) + NHC * TC + 64)
    RB2E = NCC * T
    SLOT3 = max(NCC * C, rnd32(NHC * HW_))
    RB3E = 2 * SLOT3
    SCBE = max(2 * T, 2 * C)
    ABE = RB1E + RB2E + RB3E + SCBE
    arB = es.enter_context(nc.sbuf_tensor("arB", [128, ABE], BF16))

    RF1E = max(2 * T + 4 * C, 3 * TT * C + 2 * C, 3 * C,
               max(2 * T, 2 * C + 2 * PW1) + 2 * PW1)
    SCW = max(C, NG * TC, PW1)
    SCW2 = SCW
    SCFE = 3 * SCW
    AFE = RF1E + SCFE
    arF = es.enter_context(nc.sbuf_tensor("arF", [128, AFE], F32))

    APE = 2 * QW1 + 2 * QW2
    arP = es.enter_context(nc.sbuf_tensor("arP", [128, APE], FP8))

    def vB(off, n, rows=128):
        assert off + n <= ABE, (off, n, ABE)
        return arB[0:rows, off:off + n]

    def vF(off, n, rows=128):
        assert off + n <= AFE, (off, n, AFE)
        return arF[0:rows, off:off + n]

    def vP(off, n):
        assert off + n <= APE, (off, n, APE)
        return arP[:, off:off + n]

    O1 = 0
    xn4 = [vB(O1 + i * C, C) for i in range(4)]
    wmst_v = [vB(O1 + 4 * C, 6 * C), vB(O1 + 10 * C, 6 * C)]
    v_tm = vB(O1, NT * C)
    q_fm = vB(O1, NCC * T)
    hqs_tm = vB(O1, TT * C)
    _o = O1 + TT * C
    hqs_v = [vB(_o, NCC * HW_), vB(_o + rnd32(NCC * HW_), NCC * HW_)]
    _o += 2 * rnd32(NCC * HW_)
    convB = vB(_o, NHC * TC)
    hq2 = convB
    O2 = RB1E
    h_fm = vB(O2, NCC * T)
    attn_fm = vB(O2, NCC * T)
    O3 = RB1E + RB2E
    wslot = [vB(O3, NCC * C), vB(O3 + SLOT3, NCC * C)]
    h1pad_v = [vB(O3, NHC * HW_), vB(O3 + SLOT3, NHC * HW_)]
    OS = RB1E + RB2E + RB3E
    kexp_fm = vB(OS, T)
    kslice_v = [vB(OS + T, 128), vB(OS + T + 128, 128)]
    sqb_v = [vB(OS, T), vB(OS + T, T)]
    b_rep = vB(OS, T)
    ascr_v = vB(OS + T, T)
    qtb_s = vB(OS, 2 * C)
    a_rep = vB(OS + T, T)

    OF = 0
    WSB2 = max(2 * T, 2 * C + 2 * PW1)
    wstage_v = [vF(OF + 2 * C, PW1), vF(OF + 2 * C + PW1, PW1),
                vF(OF + WSB2, PW1), vF(OF + WSB2 + PW1, PW1)]
    xt_v = [vF(OF + i * C, C) for i in range(4)]
    qe_v = [vF(OF, C), vF(OF + C, C)]
    m_rep = vF(OF, T)
    v_rep = vF(OF + T, T)
    xres_v = [vF(OF + 2 * T, C), vF(OF + 2 * T + C, C)]
    xout_v = [vF(OF + 2 * T + 2 * C, C), vF(OF + 2 * T + 3 * C, C)]
    xc_v = [vF(OF, TT * C), vF(OF + TT * C, TT * C)]
    xn2c = vF(OF + 2 * TT * C, TT * C)
    fout_v = [vF(OF + 3 * TT * C, C), vF(OF + 3 * TT * C + C, C)]
    OSF = OF + RF1E
    runmax_s = vF(OSF, NG * TC)
    f2tmp_v = vF(OSF + SCW, C)
    lnA = vF(OSF + 2 * SCW, C)
    rowtmp = vF(OSF + 2 * SCW, C, rows=1)[0:1, :]
    wqstg = vF(OSF + 2 * SCW, PW1)
    wsgn = vF(OSF + SCW, PW1)
    tern8 = None

    wq1r = [vP(0, QW1), vP(QW1, QW1)]
    wq2r = [vP(2 * QW1, QW2), vP(2 * QW1 + QW2, QW2)]
    tern8 = vP(0, PW1)

    psA = es.enter_context(nc.psum_tensor("psA", [128, 2048], F32))
    psB = es.enter_context(nc.psum_tensor("psB", [128, 2048], F32))
    PS = {"psA": psA, "psB": psB}

    def pk(psn, col0, width):
        b0 = col0 // 512
        b1 = (col0 + width - 1) // 512
        return tuple((psn, b) for b in range(b0, b1 + 1))

    for s in ["cst", "cstg", "xt0", "xt1", "xt2", "xt3", "wm0", "wm1", "wk", "wv",
              "wq", "wo", "wst0", "wst1", "wst2", "wst3", "xr0", "xr1", "xm0", "xm1",
              "so0", "so1", "xc0", "xc1", "wqh", "wr1a", "wr1b", "wr2a", "wr2b"]:
        ez.mksem(s)

    with es:
        es.enter_context(nc.Block())

        ez.op("vector", "memset", A["epsv"][:], EPS, wr=(("epsv",),))
        ez.op("vector", "memset", A["hsel"][:], 1.0, wr=(("hsel", 0),))
        ez.op("vector", "memset", A["nm32c"][:], -M32, wr=(("nm32c",),))
        ez.op("vector", "memset", A["m52c"][:], M52, wr=(("m52c",),))
        ez.op("vector", "tensor_scalar", A["hsel"][:], A["hsel"][:],
              M52, M52, ALU.add, ALU.subtract,
              rd=(("hsel", 0),), wr=(("hsel", 1),))
        ez.op("vector", "tensor_scalar", A["hsel"][:], A["hsel"][:],
              -1.0, 1.0, ALU.mult, ALU.add,
              rd=(("hsel", 1),), wr=(("hsel",),))

        for nm in ["ident_bf", "ident_f32", "ones_row_bf", "ones_row_f32",
                   "onesmat_bf", "c_fm", "lng_fm", "lnb_fm", "convw_fm",
                   "convb_fm"]:
            ez.dma("sync", A[nm][:], d[nm], "cst", wr=((nm,),))
        for nm in ["bq_row", "bv_row", "bo_row"]:
            ez.dma("gpsimd", A[nm][:], d[nm], "cstg", wr=((nm,),))
        ez.seal("cst")
        ez.seal("cstg")

        def load_w(nm, slot, sem, claims=()):
            ez.dma("gpsimd", wslot[slot].rearrange("p (j c) -> p j c", c=C),
                   d[nm].rearrange("(j p) c -> p j c", p=128), sem,
                   wr=((nm,),) + claims)

        load_w("wvT", 0, "wv")
        load_w("wkT", 1, "wk")

        def load_wmod(i):
            ez.dma("gpsimd", wmst_v[i % 2][:], d["wmod"][i * 128:(i + 1) * 128, :],
                   "wm%d" % (i % 2), wr=(("wmst", i % 2),))

        for _i in range(min(2, NCC)):
            load_wmod(_i)

        st = A["stats"]

        def load_xt(i):
            ez.dma("sync", xt_v[i % 4], d["x"][i * 128:(i + 1) * 128, :],
                   "xt%d" % (i % 4), wr=(("xt", i % 4),))

        for _i in range(min(4, NT)):
            load_xt(_i)
        sq_out_bf = vB(OS, C)
        for g0 in range(0, NT, 4):
            ng = min(4, NT - g0)
            for gi in range(ng):
                i = g0 + gi
                xt = xt_v[i % 4]
                xk = ("xt", i % 4)
                ez.op("vector", "reduce_sum", st[:, i:i + 1], xt, axis=AX.X,
                      rd=(xk,), wr=(("st", "sum", i),))
                ez.op("scalar", "activation", sq_out_bf, xt, AF.Square,
                      accum_out=st[:, NT + i:NT + i + 1], rd=(xk,),
                      wr=(("st", "sq", i), ("scrb0",)))
            sm = st[:, g0:g0 + ng]
            sq = st[:, NT + g0:NT + g0 + ng]
            c2 = st[:, 2 * NT + g0:2 * NT + g0 + ng]
            sumk = tuple(("st", "sum", g0 + z) for z in range(ng))
            sqk = tuple(("st", "sq", g0 + z) for z in range(ng))
            ez.op("vector", "tensor_scalar", sm, sm, inv_C, None, ALU.mult,
                  rd=sumk, wr=(("st", "m", g0),))
            ez.op("vector", "tensor_scalar", sq, sq, inv_C, None, ALU.mult,
                  rd=sqk, wr=(("st", "v0", g0),))
            ez.op("vector", "tensor_tensor", c2, sm, sm, ALU.mult,
                  rd=(("st", "m", g0),), wr=(("st", "m2", g0),))
            ez.op("vector", "tensor_tensor", sq, sq, c2, ALU.subtract,
                  rd=(("st", "v0", g0), ("st", "m2", g0)),
                  wr=(("st", "var", g0),))
            ez.op("scalar", "activation", sq, sq, AF.Sqrt,
                  bias=A["epsv"][:, 0:1],
                  rd=(("st", "var", g0), ("epsv",)), wr=(("st", "rs", g0),))
            ez.op("vector", "reciprocal", sq, sq,
                  rd=(("st", "rs", g0),), wr=(("st", "rs2", g0),))
            ez.op("vector", "tensor_tensor", c2, sm, sq, ALU.mult,
                  rd=(("st", "m", g0), ("st", "rs2", g0)),
                  wr=(("st", "nb0", g0),))
            ez.op("vector", "tensor_scalar", c2, c2, -1.0, None, ALU.mult,
                  rd=(("st", "nb0", g0),), wr=(("st", "nb", g0),))
            for gi in range(ng):
                i = g0 + gi
                ez.op("scalar", "activation", xn4[i % 4], xt_v[i % 4],
                      AF.Identity, scale=st[:, NT + i:NT + i + 1],
                      bias=st[:, 2 * NT + i:2 * NT + i + 1],
                      rd=(("xt", i % 4), ("st", "rs2", g0), ("st", "nb", g0)),
                      wr=(("xn4", i % 4),))
                if i + 4 < NT:
                    load_xt(i + 4)
            for cc in range(NCC):
                bank = cc % 4
                keys = pk("psA", bank * 512, ng * 128)
                for gi in range(ng):
                    ez.op("tensor", "matmul",
                          psA[:, bank * 512 + gi * 128:
                              bank * 512 + (gi + 1) * 128],
                          xn4[(g0 + gi) % 4][:, cc * 128:(cc + 1) * 128],
                          A["ident_bf"][:], start=True, stop=True,
                          rd=(("xn4", (g0 + gi) % 4), ("ident_bf",)), wr=keys,
                          sig=(gi == ng - 1))
                ez.op("vector", "tensor_copy",
                      h_fm[:, cc * T + g0 * 128: cc * T + (g0 + ng) * 128],
                      psA[:, bank * 512: bank * 512 + ng * 128],
                      rd=keys,
                      wr=tuple(("xn_fm", cc, g0 + gi) for gi in range(ng)))

        ez.op("scalar", "activation", A["c_sil"][:], A["c_fm"][:], AF.Silu,
              rd=(("c_fm",),), wr=(("c_sil",),))
        ez.op("vector", "memset", A["mod_col"][:], 0.0, wr=(("mod_col", 0),))
        for i in range(NCC):
            for jb in range(NJ):
                bank = (i % 2) * 2 + (jb % 2)
                keys = pk("psB", bank * 512, 512)
                slot = jb // 2
                ez.op("tensor", "matmul",
                      psB[:, bank * 512 + slot:bank * 512 + slot + 1],
                      wmst_v[i % 2][:, jb * 128:(jb + 1) * 128],
                      A["c_sil"][:, i:i + 1],
                      start=True, stop=True,
                      rd=(("wmst", i % 2), ("c_sil",)), wr=keys,
                      sig=(jb >= NJ - 2))
            for b2 in range(2):
                bank = (i % 2) * 2 + b2
                keys = pk("psB", bank * 512, 512)
                dst = A["modpart"][:].rearrange("p (sl two) -> p sl two", two=2) \
                    [:, :, b2:b2 + 1]
                ez.op("vector", "tensor_copy", dst,
                      psB[:, bank * 512:bank * 512 + NJ // 2].unsqueeze(2),
                      rd=keys, wr=(("modpart", b2),))
            if i + 2 < NCC:
                load_wmod(i + 2)
            ez.op("vector", "tensor_tensor", A["mod_col"][:], A["mod_col"][:],
                  A["modpart"][:], ALU.add,
                  rd=tuple(("modpart", b2) for b2 in range(2)) +
                     (("mod_col", i),),
                  wr=(("mod_col", i + 1),))
        MODK = ("mod_col", NCC)
        mc = A["mod_col"]
        g_sa, g_ca, g_ga, g_sb, g_cb, g_gb = range(6)
        ez.op("vector", "tensor_scalar", A["ca1_col"][:],
              mc[:, g_ca * NCC:(g_ca + 1) * NCC], 1.0, None, ALU.add,
              rd=(MODK,), wr=(("ca1_col",),))
        ez.op("vector", "tensor_copy", A["sa_col"][:],
              mc[:, g_sa * NCC:(g_sa + 1) * NCC], rd=(MODK,), wr=(("sa_col",),))
        for gsl, nm in ((g_ga, "ga_col"), (g_gb, "gb_col")):
            sl = mc[:, gsl * NCC:(gsl + 1) * NCC]
            ez.op("scalar", "sign", A["sgn32"][:], sl, rd=(MODK,),
                  wr=(("sgn32",),))
            ez.op("scalar", "activation", A[nm][:], sl, AF.Abs,
                  rd=(MODK,), wr=((nm, 0),))
            ez.op("vector", "tensor_scalar", A[nm][:], A[nm][:], 0.25, None,
                  ALU.max, rd=((nm, 0),), wr=((nm, 1),))
            ez.op("vector", "tensor_tensor", A[nm][:], A[nm][:], A["sgn32"][:],
                  ALU.mult, rd=((nm, 1), ("sgn32",)), wr=((nm,),))

        def col_to_rep(col_ap, col_key, rep_name, add_one):
            keys = pk("psB", 0, C)
            for cc in range(NCC):
                ez.op("tensor", "matmul", psB[0:1, cc * 128:(cc + 1) * 128],
                      col_ap[:, cc:cc + 1], A["ident_f32"][:],
                      start=True, stop=True,
                      rd=(col_key, ("ident_f32",)), wr=keys,
                      sig=(cc == NCC - 1))
            ez.op("vector", "tensor_copy", rowtmp[0:1, :], psB[0:1, 0:C],
                  rd=keys, wr=(("rowtmp",),))
            keys2 = pk("psB", 1024, C)
            for h2 in range(NO2):
                ez.op("tensor", "matmul",
                      psB[:, 1024 + h2 * OW:1024 + (h2 + 1) * OW],
                      A["ones_row_f32"][:], rowtmp[0:1, h2 * OW:(h2 + 1) * OW],
                      start=True, stop=True,
                      rd=(("rowtmp",), ("ones_row_f32",)), wr=keys2,
                      sig=(h2 == NO2 - 1))
            ez.op("vector", "tensor_scalar", A[rep_name][:],
                  psB[:, 1024:1024 + C], 1.0 if add_one else 0.0, None, ALU.add,
                  rd=keys2, wr=((rep_name,),))

        col_to_rep(mc[:, g_cb * NCC:(g_cb + 1) * NCC], MODK, "cb1_rep", True)
        col_to_rep(mc[:, g_sb * NCC:(g_sb + 1) * NCC], MODK, "sb_rep", False)
        col_to_rep(A["ga_col"][:], ("ga_col",), "ga_rep", False)
        col_to_rep(A["gb_col"][:], ("gb_col",), "gb_rep", False)

        for cc in range(NCC):
            ez.op("vector", "tensor_scalar",
                  h_fm[:, cc * T:(cc + 1) * T], h_fm[:, cc * T:(cc + 1) * T],
                  A["ca1_col"][:, cc:cc + 1], A["sa_col"][:, cc:cc + 1],
                  ALU.mult, ALU.add,
                  rd=tuple(("xn_fm", cc, i) for i in range(NT)) +
                     (("ca1_col",), ("sa_col",)),
                  wr=(("h_fm", cc),))

        ez.barrier()

        wq_work = []

        def wqd(n=1):
            for _ in range(n):
                if wq_work:
                    wq_work.pop(0)()

        st_i = [0]

        def quant_weights_pieces(dname, n_chunks, row_len, sumoff, scol,
                                 sinvcol, negcol, dst_hbm, dst_name, mean_n,
                                 auxcol, pw):
            npc = row_len // pw
            np_tot = n_chunks * npc

            def stage(pi):
                ci, pj = divmod(pi, npc)
                bidx = st_i[0] % 4
                buf = wstage_v[bidx]
                src = d[dname][ci * 128:(ci + 1) * 128, pj * pw:(pj + 1) * pw]
                ez.dma("gpsimd", buf[:, 0:pw], src, "wst%d" % bidx,
                       wr=(("wstage", bidx),))
                st_i[0] += 1
                return buf, ("wstage", bidx)

            def mk_mean(pi):
                def go():
                    buf, bk = stage(pi)
                    ez.op("vector", "tensor_reduce",
                          A["wsums"][:, sumoff + pi:sumoff + pi + 1],
                          buf[:, 0:pw], axis=AX.X, op=ALU.add,
                          apply_absolute_value=True,
                          rd=(bk,), wr=(("wsums", dname, pi),))
                return go

            for pi in range(np_tot):
                wq_work.append(mk_mean(pi))

            def finalize():
                allk = tuple(("wsums", dname, pi) for pi in range(np_tot))
                ez.op("vector", "reduce_sum", A["wsc"][:, auxcol:auxcol + 1],
                      A["wsums"][:, sumoff:sumoff + np_tot], axis=AX.X,
                      rd=allk, wr=(("wsc", dname, "ps"),))
                keys = pk("psA", 1024, 128)
                ez.op("tensor", "matmul", psA[0:1, 1024:1152],
                      A["wsc"][:, auxcol:auxcol + 1], A["ident_f32"][:],
                      start=True, stop=True,
                      rd=(("wsc", dname, "ps"), ("ident_f32",)), wr=keys,
                      sig=True)
                ez.op("vector", "tensor_reduce", A["wrow"][0:1, 0:1],
                      psA[0:1, 1024:1152], axis=AX.X, op=ALU.add,
                      rd=keys, wr=(("wrow", dname, "tot"),))
                ez.op("vector", "tensor_scalar", A["wrow"][0:1, 1:2],
                      A["wrow"][0:1, 0:1], 1.0 / mean_n, 1e-5, ALU.mult,
                      ALU.max,
                      rd=(("wrow", dname, "tot"),),
                      wr=(("wrow", dname, "mean"),))
                ez.op("vector", "reciprocal", A["wrow"][0:1, 2:3],
                      A["wrow"][0:1, 1:2],
                      rd=(("wrow", dname, "mean"),), wr=(("wrow", dname, "s"),))
                keys2 = pk("psA", 1280, 2)
                ez.op("tensor", "matmul", psA[:, 1280:1282],
                      A["ones_row_f32"][:], A["wrow"][0:1, 1:3],
                      start=True, stop=True,
                      rd=(("wrow", dname, "mean"), ("wrow", dname, "s"),
                          ("ones_row_f32",)),
                      wr=keys2, sig=True)
                ez.op("vector", "tensor_copy",
                      A["wsc"][:, sinvcol:sinvcol + 1], psA[:, 1280:1281],
                      rd=keys2, wr=(("wsc", "sinv", dname),))
                ez.op("vector", "tensor_copy", A["wsc"][:, scol:scol + 1],
                      psA[:, 1281:1282], rd=keys2, wr=(("wsc", "s", dname),))
                ez.op("vector", "tensor_scalar", A["wsc"][:, negcol:negcol + 1],
                      A["wsc"][:, scol:scol + 1], -1.0, None, ALU.mult,
                      rd=(("wsc", "s", dname),), wr=(("wsc", "negs", dname),))

            wq_work.append(finalize)

            def mk_tern(pi):
                def go():
                    ci, pj = divmod(pi, npc)
                    buf, bk = stage(pi)
                    w_sl = buf[:, 0:pw]
                    ez.op("vector", "tensor_scalar", wqstg[:, 0:pw], w_sl,
                          A["wsc"][:, scol:scol + 1], 0.5, ALU.mult, ALU.is_ge,
                          rd=(bk, ("wsc", "s", dname)),
                          wr=(("wqstg",), ("rowtmp",), ("lnA", 0),
                              ("lnA", 1)))
                    ez.op("vector", "tensor_scalar", w_sl, w_sl,
                          A["wsc"][:, negcol:negcol + 1], 0.5, ALU.mult,
                          ALU.is_ge,
                          rd=(bk, ("wsc", "negs", dname)), wr=(bk,))
                    ez.op("gpsimd", "tensor_tensor", tern8[:, 0:pw],
                          wqstg[:, 0:pw], w_sl, ALU.subtract,
                          rd=(("wqstg",), bk), wr=(("tern8",),))
                    ez.dma("sync",
                           dst_hbm[:, ci * row_len + pj * pw:
                                   ci * row_len + (pj + 1) * pw],
                           tern8[:, 0:pw], "wqh",
                           rd=(("tern8",),), wr=((dst_name, pi),))
                return go

            for pi in range(np_tot):
                wq_work.append(mk_tern(pi))

        quant_weights_pieces("w1T", NCC, HID, 0, 0, 1, 6, wq1h, "wq1h",
                             C * HID, 4, PW1)
        quant_weights_pieces("w2T", NHC, C, NW1P, 2, 3, 7, wq2h, "wq2h",
                             C * HID, 5, PW2)

        def proj_tm_tile(i, wname, wslot_i, brow, psn):
            ps = PS[psn]
            keys = pk(psn, 0, C)
            for ci in range(NCC):
                lhsT = h_fm[:, ci * T + i * 128: ci * T + (i + 1) * 128]
                for oh in range(NO2):
                    ez.op("tensor", "matmul",
                          ps[:, oh * OW:(oh + 1) * OW], lhsT,
                          wslot[wslot_i][:, ci * C + oh * OW:
                                         ci * C + (oh + 1) * OW],
                          start=(ci == 0), stop=False,
                          rd=(("h_fm", ci), (wname,)), wr=keys, sig=False)
            for oh in range(NO2):
                ez.op("tensor", "matmul", ps[:, oh * OW:(oh + 1) * OW],
                      A["ones_row_bf"][:],
                      A[brow][0:1, oh * OW:(oh + 1) * OW],
                      start=False, stop=True,
                      rd=(("ones_row_bf",), (brow,)), wr=keys,
                      sig=(oh == NO2 - 1))
            return ps, keys

        for i in range(NT):
            psn = "psA" if i % 2 == 0 else "psB"
            ps, keys = proj_tm_tile(i, "wvT", 0, "bv_row", psn)
            ez.op("scalar", "activation", v_tm[:, i * C:(i + 1) * C],
                  ps[:, 0:C], AF.Identity, rd=keys, wr=(("v_tm", i),))
            wqd(3)
        load_w("wqT", 0, "wq", claims=(("wvT",),))

        kst = A["kstats"]
        for oc in range(NCC):
            psn = "psA" if oc % 2 == 0 else "psB"
            opn = "psB" if oc % 2 == 0 else "psA"
            ps = PS[psn]
            stk_e = pk(psn, 1536, 64)
            stk_o = pk(opn, 1536, 64)
            for half in range(NKH):
                keys = pk(psn, 0, THALF)
                for ci in range(NCC):
                    lhsT = wslot[1][:, ci * C + oc * 128: ci * C + (oc + 1) * 128]
                    for tq in range(THALF // TQW):
                        t0 = half * THALF + tq * TQW
                        ez.op("tensor", "matmul",
                              ps[:, tq * TQW:(tq + 1) * TQW], lhsT,
                              h_fm[:, ci * T + t0: ci * T + t0 + TQW],
                              start=(ci == 0), stop=(ci == NCC - 1),
                              rd=(("wkT",), ("h_fm", ci)), wr=keys,
                              sig=(ci == NCC - 1 and tq == THALF // TQW - 1))
                scol = (NCC + oc) if half == 0 else oc
                ez.op("scalar", "activation", kexp_fm[:, 0:THALF],
                      ps[:, 0:THALF], AF.Exp,
                      accum_out=kst[:, scol:scol + 1],
                      rd=keys, wr=(("kexp",), ("kst", "sum", oc, half),
                                   ("scrb0",)))
                for it in range(KHT):
                    i = half * KHT + it
                    bank = it % 2
                    keys2 = pk(psn, bank * 512, 128)
                    ksl = kslice_v[i % 2]
                    ez.op("tensor", "matmul",
                          ps[:, bank * 512:bank * 512 + 128],
                          kexp_fm[:, it * 128:(it + 1) * 128],
                          A["ident_bf"][:], start=True, stop=True,
                          rd=(("kexp",), ("ident_bf",)), wr=keys2, sig=True)
                    ez.op("scalar", "activation", ksl,
                          ps[:, bank * 512:bank * 512 + 128], AF.Identity,
                          rd=keys2, wr=(("ksl", i % 2),))
                    ez.op("tensor", "matmul",
                          ps[0:64, 1536:1600],
                          ksl[:, 0:64],
                          v_tm[:, i * C + oc * 128: i * C + oc * 128 + 64],
                          start=(i == 0), stop=(i == NT - 1),
                          tile_position=(0, 0),
                          rd=(("ksl", i % 2), ("v_tm", i)),
                          wr=stk_e, sig=True)
                    ez.op("tensor", "matmul",
                          PS[opn][64:128, 1536:1600],
                          ksl[:, 64:128],
                          v_tm[:, i * C + oc * 128 + 64:
                               i * C + oc * 128 + 128],
                          start=(i == 0), stop=(i == NT - 1),
                          tile_position=(0, 64),
                          rd=(("ksl", i % 2), ("v_tm", i)),
                          wr=stk_o, sig=True)
                wqd(3)
            if NKH > 1:
                ez.op("vector", "tensor_tensor", kst[:, NCC + oc:NCC + oc + 1],
                      kst[:, NCC + oc:NCC + oc + 1], kst[:, oc:oc + 1],
                      ALU.add,
                      rd=(("kst", "sum", oc, 0), ("kst", "sum", oc, 1)),
                      wr=(("kst", "tot", oc),))
            ez.op("scalar", "activation",
                  A["state_sb"][0:64, oc * D:(oc + 1) * D],
                  PS[psn][0:64, 1536:1600], AF.Identity,
                  rd=stk_e, wr=(("state0", oc),))
            ez.op("scalar", "activation",
                  A["state_sb"][64:128, oc * D:(oc + 1) * D],
                  PS[opn][64:128, 1536:1600], AF.Identity,
                  rd=stk_o, wr=(("state1", oc),))
            wqd(2)
        ez.op("vector", "reciprocal", kst[:, NCC:2 * NCC], kst[:, NCC:2 * NCC],
              rd=tuple(("kst", "tot" if NKH > 1 else "sum", oc) +
                       (() if NKH > 1 else (0,))
                       for oc in range(NCC)),
              wr=(("kst", "recip"),))
        for oc in range(NCC):
            for pb in (0, 64):
                ez.op("scalar", "activation",
                      A["state_bf"][pb:pb + 64, oc * D:(oc + 1) * D],
                      A["state_sb"][pb:pb + 64, oc * D:(oc + 1) * D],
                      AF.Identity,
                      scale=kst[pb:pb + 64, NCC + oc:NCC + oc + 1],
                      rd=(("state%d" % (pb // 64), oc), ("kst", "recip")),
                      wr=(("state%d" % (pb // 64), oc, "s"),))
        load_w("woT", 1, "wo", claims=(("wkT",),))

        ez.barrier()

        for i in range(NT):
            psn = "psA" if i % 2 == 0 else "psB"
            ps, keys = proj_tm_tile(i, "wqT", 0, "bq_row", psn)
            qe = qe_v[i % 2]
            qek = ("qe", i % 2)
            qs = A["qstat"]
            ez.op("vector", "tensor_reduce", qs[:, 2 * H:2 * H + 1],
                  ps[:, 0:C], axis=AX.X, op=ALU.max, negate=True,
                  rd=keys, wr=(("qmx", i),))
            ez.op("scalar", "activation", qe, ps[:, 0:C], AF.Exp,
                  bias=qs[:, 2 * H:2 * H + 1],
                  rd=keys + (("qmx", i),), wr=(qek,))
            ez.op("vector", "tensor_reduce", qs[:, 0:H].unsqueeze(2),
                  qe.rearrange("p (h dd) -> p h dd", dd=D), axis=AX.X,
                  op=ALU.add, rd=(qek,), wr=(("qs", i),))
            ez.op("vector", "reciprocal", qs[:, H:2 * H], qs[:, 0:H],
                  rd=(("qs", i),), wr=(("qsr", i),))
            qtb = qtb_s[:, (i % 2) * C:(i % 2 + 1) * C]
            ez.op("vector", "tensor_tensor",
                  qtb.rearrange("p (h dd) -> p h dd", dd=D),
                  qe.rearrange("p (h dd) -> p h dd", dd=D),
                  qs[:, H:2 * H].unsqueeze(2).broadcast_to([128, H, D]),
                  ALU.mult,
                  rd=(qek, ("qsr", i)), wr=(("qtb", i % 2),))
            for gq in range(0, NCC, 4):
                bank = 2 + (gq // 4) % 2
                ngq = min(4, NCC - gq)
                keys2 = pk(psn, bank * 512, ngq * 128)
                for gi in range(ngq):
                    cc = gq + gi
                    ez.op("tensor", "matmul",
                          ps[:, bank * 512 + gi * 128:
                             bank * 512 + (gi + 1) * 128],
                          qtb[:, cc * 128:(cc + 1) * 128],
                          A["ident_bf"][:], start=True, stop=True,
                          rd=(("qtb", i % 2), ("ident_bf",)), wr=keys2,
                          sig=(gi == ngq - 1))
                dst = q_fm.rearrange("p (hp t) -> p hp t", t=T) \
                    [:, gq:gq + ngq, i * 128:(i + 1) * 128]
                ez.op("scalar", "activation", dst,
                      ps[:, bank * 512:bank * 512 + ngq * 128]
                      .rearrange("p (n w) -> p n w", w=128), AF.Identity,
                      rd=keys2,
                      wr=tuple(("q_fm", gq + gi, i) for gi in range(ngq)))
            wqd(3)

        ez.barrier()

        for hp in range(HP):
            psn = "psA" if hp % 2 == 0 else "psB"
            ps = PS[psn]
            keys = pk(psn, 0, T)
            for tq in range(NT4):
                for pb in (0, 64):
                    ez.op("tensor", "matmul",
                          ps[pb:pb + 64, tq * TQW:(tq + 1) * TQW],
                          A["state_bf"][pb:pb + 64, hp * D:(hp + 1) * D],
                          q_fm[pb:pb + 64, hp * T + tq * TQW:
                               hp * T + (tq + 1) * TQW],
                          start=True, stop=True, tile_position=(pb, pb),
                          rd=(("state%d" % (pb // 64), hp, "s"),
                              ("q_fm", hp, 4 * tq)) +
                             tuple(("q_fm", hp, 4 * tq + z)
                                   for z in range(1, TQW // 128)),
                          wr=keys, sig=(pb == 64))
            ez.op("scalar", "activation", attn_fm[:, hp * T:(hp + 1) * T],
                  ps[:, 0:T], AF.Identity, rd=keys, wr=(("attn_fm", hp),))
            wqd(2)
        wqd(len(wq_work))

        for hp in range(HP):
            sl = attn_fm[:, hp * T:(hp + 1) * T]
            sq = sqb_v[hp % 2]
            ez.op("scalar", "activation", sq, sl, AF.Square,
                  rd=(("attn_fm", hp),), wr=(("sqbuf", hp % 2),))
            psn, psq = ("psA", "psB")
            for tq in range(NT4):
                keysS = pk(psn, tq * TQW, TQW)
                keysQ = pk(psq, tq * TQW, TQW)
                ez.op("tensor", "matmul",
                      PS[psn][:, tq * TQW:(tq + 1) * TQW],
                      A["onesmat_bf"][:], sl[:, tq * TQW:(tq + 1) * TQW],
                      start=(hp == 0), stop=(hp == HP - 1),
                      rd=(("attn_fm", hp), ("onesmat_bf",)), wr=keysS,
                      sig=(tq == NT4 - 1))
                ez.op("tensor", "matmul",
                      PS[psq][:, tq * TQW:(tq + 1) * TQW],
                      A["onesmat_bf"][:], sq[:, tq * TQW:(tq + 1) * TQW],
                      start=(hp == 0), stop=(hp == HP - 1),
                      rd=(("sqbuf", hp % 2), ("onesmat_bf",)), wr=keysQ,
                      sig=(tq == NT4 - 1))
        pkS = pk("psA", 0, T)
        pkQ = pk("psB", 0, T)
        ez.op("vector", "tensor_scalar", m_rep, psA[:, 0:T], inv_C, None,
              ALU.mult, rd=pkS, wr=(("m_rep",),))
        ez.op("vector", "tensor_scalar", v_rep, psB[:, 0:T], inv_C, None,
              ALU.mult, rd=pkQ, wr=(("v_rep", 0), ("wstage", 0),
                                    ("wstage", 1)))
        ez.op("vector", "scalar_tensor_tensor", ascr_v, m_rep, -1.0, m_rep,
              ALU.mult, ALU.mult,
              rd=(("m_rep",),),
              wr=(("ascr",), ("sqbuf", 1)))
        ez.op("vector", "tensor_tensor", v_rep, v_rep, ascr_v, ALU.add,
              rd=(("v_rep", 0), ("ascr",)), wr=(("v_rep", 1),))
        ez.op("scalar", "activation", v_rep, v_rep, AF.Sqrt,
              bias=A["epsv"][:, 0:1],
              rd=(("v_rep", 1), ("epsv",)), wr=(("v_rep", "sd"),))
        ez.op("vector", "reciprocal", v_rep, v_rep,
              rd=(("v_rep", "sd"),), wr=(("v_rep", "rs"),))
        ez.op("vector", "tensor_copy", a_rep, v_rep,
              rd=(("v_rep", "rs"),), wr=(("a_rep",), ("ascr",),
                                         ("sqbuf", 1)))
        ez.op("vector", "tensor_tensor", m_rep, m_rep, v_rep, ALU.mult,
              rd=(("m_rep",), ("v_rep", "rs")), wr=(("m_rep", "mrs"),))
        ez.op("vector", "tensor_scalar", b_rep, m_rep, -1.0, None, ALU.mult,
              rd=(("m_rep", "mrs"),), wr=(("b_rep",), ("sqbuf", 0),
                                          ("kexp",)))

        for hp in range(HP):
            sl = attn_fm[:, hp * T:(hp + 1) * T]
            ez.op("vector", "tensor_tensor", sl, sl, a_rep, ALU.mult,
                  rd=(("attn_fm", hp), ("a_rep",)), wr=(("attn_fm", hp, "s"),))
            ez.op("vector", "tensor_tensor", sl, sl, b_rep, ALU.add,
                  rd=(("attn_fm", hp, "s"), ("b_rep",)),
                  wr=(("attn_fm", hp, "b"),))
            ez.op("scalar", "activation", sl, sl, AF.Identity,
                  scale=A["lng_fm"][:, hp:hp + 1], bias=A["lnb_fm"][:, hp:hp + 1],
                  rd=(("attn_fm", hp, "b"), ("lng_fm",), ("lnb_fm",)),
                  wr=(("lnout", hp),))

        def load_xres(i):
            ez.dma("sync", xres_v[i % 2], d["x"][i * 128:(i + 1) * 128, :],
                   "xr%d" % (i % 2),
                   wr=(("xres", i % 2), ("wstage", 0), ("wstage", 1),
                       ("wstage", 2), ("wstage", 3)))

        for _i in range(min(2, NT)):
            load_xres(_i)
        for i in range(NT):
            psn = "psA" if i % 2 == 0 else "psB"
            ps = PS[psn]
            keys = pk(psn, 0, C)
            for ci in range(NCC):
                lhsT = attn_fm[:, ci * T + i * 128: ci * T + (i + 1) * 128]
                for oh in range(NO2):
                    ez.op("tensor", "matmul", ps[:, oh * OW:(oh + 1) * OW],
                          lhsT,
                          wslot[1][:, ci * C + oh * OW: ci * C + (oh + 1) * OW],
                          start=(ci == 0), stop=False,
                          rd=(("lnout", ci), ("woT",)), wr=keys, sig=False)
            for oh in range(NO2):
                ez.op("tensor", "matmul", ps[:, oh * OW:(oh + 1) * OW],
                      A["ones_row_bf"][:], A["bo_row"][0:1, oh * OW:(oh + 1) * OW],
                      start=False, stop=True,
                      rd=(("ones_row_bf",), ("bo_row",)), wr=keys,
                      sig=(oh == NO2 - 1))
            xrk = ("xres", i % 2)
            xok = ("xout", i % 2)
            ez.op("vector", "tensor_tensor", f2tmp_v, ps[:, 0:C],
                  A["ga_rep"][:], ALU.mult,
                  rd=keys + (("ga_rep",),), wr=(("f2s",), ("wsgn",)))
            ez.op("gpsimd", "tensor_tensor", xout_v[i % 2], f2tmp_v,
                  xres_v[i % 2], ALU.add,
                  rd=(("f2s",), xrk), wr=(xok,))
            ez.dma("sync", out_hbm[i * 128:(i + 1) * 128, :], xout_v[i % 2],
                   "xm%d" % (i % 2), rd=(xok,), wr=(("xmid", i),))
            if i + 2 < NT:
                load_xres(i + 2)

        ez.barrier(dmas=("xm0", "xm1", "wqh"))

        ez.op("vector", "tensor_scalar", A["tapsf"][:], A["convw_fm"][:],
              A["wsc"][:, 1:2], None, ALU.mult,
              rd=(("convw_fm",), ("wsc", "sinv", "w1T")), wr=(("tapsf",),))

        q1_idx = [0]
        q2_idx = [0]

        def load_q1():
            idx = q1_idx[0]
            if idx >= NFC * NQ:
                return
            q = idx % NQ
            slot = idx % 2
            src = wq1h.rearrange("p (cc z) -> p cc z", z=HID) \
                [:, :, q * (HID // NQ):(q + 1) * (HID // NQ)]
            ez.dma("sync", wq1r[slot].rearrange("p (cc z) -> p cc z",
                                                z=HID // NQ),
                   src, "wr1a" if slot == 0 else "wr1b",
                   wr=(("wq1r", slot),))
            q1_idx[0] += 1

        def load_q2():
            idx = q2_idx[0]
            if idx >= NFC * NQ:
                return
            q = idx % NQ
            slot = idx % 2
            src = wq2h[:, q * QW2:(q + 1) * QW2]
            ez.dma("sync", wq2r[slot], src, "wr2a" if slot == 0 else "wr2b",
                   wr=(("wq2r", slot),))
            q2_idx[0] += 1

        load_q1()
        load_q1()
        load_q2()
        load_q2()

        def ln2_quant(j):
            xcb = xc_v[j % 2]
            xck = ("xc", j % 2)
            src = out_hbm[j * TC:(j + 1) * TC, :] \
                .rearrange("(j2 p) c -> p j2 c", p=128)
            fstk = tuple(("fst", t, t2) for t in ("sum", "sq", "amax")
                         for t2 in range(TT)) + \
                tuple(("fst", t) for t in ("m", "v0", "m2", "var", "sd", "rs",
                                           "nb0", "nb", "s1inv", "s1"))
            ez.dma("sync", xcb.rearrange("p (j2 c) -> p j2 c", c=C), src,
                   "xc%d" % (j % 2),
                   rd=tuple(("xmid", jj) for jj in range(NT)),
                   wr=(xck,) + fstk)
            fst = A["fstats"]
            for t2 in range(TT):
                xt = xcb[:, t2 * C:(t2 + 1) * C]
                ez.op("vector", "reduce_sum", fst[:, t2:t2 + 1], xt, axis=AX.X,
                      rd=(xck,), wr=(("fst", "sum", t2),))
                ez.op("scalar", "activation", sq_out_bf, xt, AF.Square,
                      accum_out=fst[:, TT + t2:TT + t2 + 1],
                      rd=(xck,),
                      wr=(("fst", "sq", t2), ("scrb0",)) +
                         tuple(("run", g) for g in range(NG)))
            sm_ = fst[:, 0:TT]; sq_ = fst[:, TT:2 * TT]; ax_ = fst[:, 2 * TT:3 * TT]
            kk = tuple(("fst", t, t2) for t in ("sum", "sq") for t2 in range(TT))
            ez.op("vector", "tensor_scalar", sm_, sm_, inv_C, None, ALU.mult,
                  rd=kk, wr=(("fst", "m"),))
            ez.op("vector", "tensor_scalar", sq_, sq_, inv_C, None, ALU.mult,
                  rd=kk + (("fst", "m"),), wr=(("fst", "v0"),))
            ez.op("vector", "tensor_tensor", ax_, sm_, sm_, ALU.mult,
                  rd=(("fst", "m"),), wr=(("fst", "m2"),))
            ez.op("vector", "tensor_tensor", sq_, sq_, ax_, ALU.subtract,
                  rd=(("fst", "v0"), ("fst", "m2")), wr=(("fst", "var"),))
            ez.op("scalar", "activation", sq_, sq_, AF.Sqrt,
                  bias=A["epsv"][:, 0:1],
                  rd=(("fst", "var"), ("epsv",)), wr=(("fst", "sd"),))
            ez.op("vector", "reciprocal", sq_, sq_,
                  rd=(("fst", "sd"),), wr=(("fst", "rs"),))
            ez.op("vector", "tensor_tensor", ax_, sm_, sq_, ALU.mult,
                  rd=(("fst", "m"), ("fst", "rs")), wr=(("fst", "nb0"),))
            ez.op("vector", "tensor_scalar", ax_, ax_, -1.0, None, ALU.mult,
                  rd=(("fst", "nb0"),), wr=(("fst", "nb"),))
            for t2 in range(TT):
                ez.op("scalar", "activation", xn2c[:, t2 * C:(t2 + 1) * C],
                      xcb[:, t2 * C:(t2 + 1) * C], AF.Identity,
                      scale=sq_[:, t2:t2 + 1], bias=ax_[:, t2:t2 + 1],
                      rd=(xck, ("fst", "rs"), ("fst", "nb")),
                      wr=(("xn2c", t2, 0), ("xn2c", t2, 3)))
                ez.op("vector", "tensor_tensor", xn2c[:, t2 * C:(t2 + 1) * C],
                      xn2c[:, t2 * C:(t2 + 1) * C], A["cb1_rep"][:], ALU.mult,
                      rd=(("xn2c", t2, 0), ("cb1_rep",)), wr=(("xn2c", t2, 1),))
                ez.op("vector", "tensor_tensor", xn2c[:, t2 * C:(t2 + 1) * C],
                      xn2c[:, t2 * C:(t2 + 1) * C], A["sb_rep"][:], ALU.add,
                      rd=(("xn2c", t2, 1), ("sb_rep",)), wr=(("xn2c", t2, 2),))
                ez.op("vector", "tensor_reduce",
                      fst[:, 3 * TT + t2:3 * TT + t2 + 1],
                      xn2c[:, t2 * C:(t2 + 1) * C], axis=AX.X, op=ALU.max,
                      apply_absolute_value=True,
                      rd=(("xn2c", t2, 2),), wr=(("fst", "amax", t2),))
            am = fst[:, 3 * TT:4 * TT]
            ez.op("vector", "tensor_scalar", am, am, 1e-5, 1.0 / 127.0,
                  ALU.max, ALU.mult,
                  rd=tuple(("fst", "amax", t2) for t2 in range(TT)),
                  wr=(("fst", "s1inv"),))
            s1 = fst[:, 4 * TT:5 * TT]
            ez.op("vector", "reciprocal", s1, am,
                  rd=(("fst", "s1inv"),), wr=(("fst", "s1"),))
            for t2 in range(TT):
                xsl = xn2c[:, t2 * C:(t2 + 1) * C]
                ez.op("vector", "tensor_scalar", xsl, xsl, s1[:, t2:t2 + 1],
                      M32, ALU.mult, ALU.add,
                      rd=(("xn2c", t2, 2), ("fst", "s1")), wr=(("xn2c", t2, 3),))
                ez.op("scalar", "activation", lnA, xsl, AF.Identity,
                      bias=A["nm32c"][:, 0:1],
                      rd=(("xn2c", t2, 3), ("nm32c",)), wr=(("lnA", 0),))
                ez.op("scalar", "activation", xsl, lnA, AF.Identity,
                      scale=A["hsel"][:, 0:1],
                      rd=(("lnA", 0), ("hsel",)), wr=(("xn2c", t2, 4),))
                ez.op("scalar", "activation", lnA, lnA, AF.Identity,
                      bias=A["m52c"][:, 0:1],
                      rd=(("lnA", 0), ("m52c",)), wr=(("lnA", 1),))
                ez.op("vector", "scalar_tensor_tensor", xsl, lnA, -M52, xsl,
                      ALU.add, ALU.add,
                      rd=(("lnA", 1), ("xn2c", t2, 4)), wr=(("xn2c", t2, 5),))
                ez.op("vector", "tensor_scalar", hqs_tm[:, t2 * C:(t2 + 1) * C],
                      xsl, am[:, t2:t2 + 1], None, ALU.mult,
                      rd=(("xn2c", t2, 5), ("fst", "s1inv")),
                      wr=(("hqs_tm", t2),))
            hb = hqs_v[j % 2]
            hbn = "hqs%d" % (j % 2)
            for t2 in range(TT):
                for g0 in range(0, NCC, 4):
                    bank = (g0 // 4) % 2
                    ng = min(4, NCC - g0)
                    keys2 = pk("psB", bank * 512, ng * 128)
                    for gi in range(ng):
                        cc = g0 + gi
                        ez.op("tensor", "matmul",
                              psB[:, bank * 512 + gi * 128:
                                  bank * 512 + (gi + 1) * 128],
                              hqs_tm[:, t2 * C + cc * 128:
                                     t2 * C + (cc + 1) * 128],
                              A["ident_bf"][:], start=True, stop=True,
                              rd=(("hqs_tm", t2), ("ident_bf",)), wr=keys2,
                              sig=(gi == ng - 1))
                    for gi in range(ng):
                        cc = g0 + gi
                        ez.op("scalar", "activation",
                              hb[:, cc * HW_ + 1 + t2 * 128:
                                 cc * HW_ + 1 + (t2 + 1) * 128],
                              psB[:, bank * 512 + gi * 128:
                                  bank * 512 + (gi + 1) * 128], AF.Identity,
                              rd=keys2, wr=((hbn, "mid", cc, t2),))

        def hqs_edges(j):
            hb_prev = hqs_v[(j - 1) % 2]
            hbn_prev = "hqs%d" % ((j - 1) % 2)
            hb_cur = hqs_v[j % 2]
            hbn_cur = "hqs%d" % (j % 2)
            if j < NFC:
                dst = hb_cur.rearrange("p (cc w) -> p cc w", w=HW_)[:, :, 0:1]
                if j == 0:
                    ez.op("vector", "memset", dst, 0.0, wr=((hbn_cur, "edgeL"),))
                else:
                    ez.op("vector", "tensor_copy", dst,
                          hb_prev.rearrange("p (cc w) -> p cc w", w=HW_)
                          [:, :, HW_ - 2:HW_ - 1],
                          rd=tuple((hbn_prev, "mid", cc, TT - 1)
                                   for cc in range(NCC)),
                          wr=((hbn_cur, "edgeL"),))
            if j >= 1:
                dst = hb_prev.rearrange("p (cc w) -> p cc w", w=HW_) \
                    [:, :, HW_ - 1:HW_]
                if j == NFC:
                    ez.op("vector", "memset", dst, 0.0, wr=((hbn_prev, "edgeR"),))
                else:
                    ez.op("vector", "tensor_copy", dst,
                          hb_cur.rearrange("p (cc w) -> p cc w", w=HW_)
                          [:, :, 1:2],
                          rd=tuple((hbn_cur, "mid", cc, 0) for cc in range(NCC)),
                          wr=((hbn_prev, "edgeR"),))

        def ffn1_conv(j1, j3):
            if j1 is not None:
                hb = hqs_v[j1 % 2]
                hbn = "hqs%d" % (j1 % 2)
                h1pad_1 = h1pad_v[j1 % 2]
                hkeys = tuple((hbn, "mid", cc, t2) for cc in range(NCC)
                              for t2 in range(TT)) + \
                    ((hbn, "edgeL"), (hbn, "edgeR"))
            if j3 is not None:
                h1pad_3 = h1pad_v[j3 % 2]
            for hc in range(NHC):
                if j1 is not None:
                    qsl = ((j1 * NQ) + hc // QH) % 2
                    bank = hc % 2
                    keys = pk("psA", bank * 512, HW_)
                    for cc in range(NCC):
                        ez.op("tensor", "matmul",
                              psA[:, bank * 512: bank * 512 + HW_],
                              wq1r[qsl][:, cc * (HID // NQ) +
                                        (hc % QH) * 128:
                                        cc * (HID // NQ) +
                                        (hc % QH + 1) * 128],
                              hb[:, cc * HW_:(cc + 1) * HW_],
                              start=(cc == 0), stop=(cc == NCC - 1),
                              rd=(("wq1r", qsl),) + hkeys, wr=keys,
                              sig=(cc == NCC - 1))
                    ez.op("scalar", "activation",
                          h1pad_1[:, hc * HW_:(hc + 1) * HW_],
                          psA[:, bank * 512: bank * 512 + HW_], AF.Identity,
                          rd=keys, wr=(("h1pad", j1 % 2, hc),))
                    if (hc + 1) % QH == 0:
                        load_q1()
                if j3 is not None:
                    g = hc // GRP
                    left = h1pad_3[:, hc * HW_: hc * HW_ + TC]
                    dstc = convB[:, hc * TC:(hc + 1) * TC]
                    cclaim = (("hq2", g, "m"), ("hq2", g, "r"),
                              ("convB", hc, "s"))
                    ez.op("scalar", "activation", dstc, left, AF.Identity,
                          scale=A["tapsf"][:, hc * 3:hc * 3 + 1],
                          bias=A["convb_fm"][:, hc:hc + 1],
                          rd=(("h1pad", j3 % 2, hc), ("tapsf",), ("convb_fm",)),
                          wr=(("convB", hc, 0),) + cclaim)
            if j3 is not None:
                for hc in range(NHC):
                    mid = h1pad_3[:, hc * HW_ + 1: hc * HW_ + 1 + TC]
                    right = h1pad_3[:, hc * HW_ + 2: hc * HW_ + 2 + TC]
                    dstc = convB[:, hc * TC:(hc + 1) * TC]
                    ez.op("vector", "scalar_tensor_tensor", dstc, mid,
                          A["tapsf"][:, hc * 3 + 1:hc * 3 + 2], dstc,
                          ALU.mult, ALU.add,
                          rd=(("h1pad", j3 % 2, hc), ("convB", hc, 0),
                              ("tapsf",)),
                          wr=(("convB", hc, 1),))
                    ez.op("vector", "scalar_tensor_tensor", dstc, right,
                          A["tapsf"][:, hc * 3 + 2:hc * 3 + 3], dstc,
                          ALU.mult, ALU.add,
                          rd=(("h1pad", j3 % 2, hc), ("convB", hc, 1),
                              ("tapsf",)),
                          wr=(("convB", hc, 2),))
                for hc in range(NHC):
                    g = hc // GRP
                    dstc = convB[:, hc * TC:(hc + 1) * TC]
                    ez.op("scalar", "activation", dstc, dstc, AF.Silu,
                          rd=(("convB", hc, 2),), wr=(("convB", hc, "s"),))
                    if hc % GRP == GRP - 1:
                        ez.op("vector", "tensor_reduce",
                              runmax_s[:, g * TC:(g + 1) * TC],
                              convB[:, g * GRP * TC:(g + 1) * GRP * TC]
                              .rearrange("p (hc t) -> p t hc", t=TC),
                              axis=AX.X, op=ALU.max, apply_absolute_value=True,
                              rd=tuple(("convB", h2, "s")
                                       for h2 in range(g * GRP, (g + 1) * GRP)),
                              wr=(("run", g), ("runs",)))

        def ffn_out(j):
            run = runmax_s
            for g in range(1, NG):
                ez.op("vector", "tensor_tensor", run[:, 0:TC], run[:, 0:TC],
                      run[:, g * TC:(g + 1) * TC], ALU.max,
                      rd=(("run", 0), ("run", g)),
                      wr=(("run", 0),))
            for t2 in range(TT):
                keys2 = pk("psB", 2 * 512, 128)
                s2claim = (("s2c", "s2inv"), ("s2c", "s2"),
                           ("s2c", "s2w")) if t2 == 0 else ()
                ez.op("tensor", "matmul", psB[:, 2 * 512:2 * 512 + 128],
                      run[:, t2 * 128:(t2 + 1) * 128], A["ident_f32"][:],
                      start=True, stop=True,
                      rd=(("run", 0), ("ident_f32",)), wr=keys2, sig=True)
                ez.op("vector", "tensor_reduce", A["s2c"][:, t2:t2 + 1],
                      psB[:, 2 * 512:2 * 512 + 128], axis=AX.X, op=ALU.max,
                      rd=keys2, wr=(("s2c", "max", t2),) + s2claim)
            ez.op("vector", "tensor_scalar", A["s2c"][:, 0:TT],
                  A["s2c"][:, 0:TT], 1e-5, 1.0 / 127.0, ALU.max, ALU.mult,
                  rd=tuple(("s2c", "max", t2) for t2 in range(TT)),
                  wr=(("s2c", "s2inv"),))
            ez.op("vector", "reciprocal", A["s2c"][:, TT:2 * TT],
                  A["s2c"][:, 0:TT], rd=(("s2c", "s2inv"),),
                  wr=(("s2c", "s2"),))
            for t2 in range(TT):
                keys2 = pk("psB", 2 * 512 + 128 + t2 * 128, 128)
                ez.op("tensor", "matmul",
                      psB[0:1, 2 * 512 + 128 + t2 * 128:
                          2 * 512 + 128 + (t2 + 1) * 128],
                      A["s2c"][:, TT + t2:TT + t2 + 1], A["ident_f32"][:],
                      start=True, stop=True,
                      rd=(("s2c", "s2"), ("ident_f32",)), wr=keys2, sig=True)
            rowkeys = pk("psB", 2 * 512 + 128, TT * 128)
            ez.op("vector", "tensor_copy", A["rowtc"][0:1, :],
                  psB[0:1, 2 * 512 + 128: 2 * 512 + 128 + TC],
                  rd=rowkeys, wr=(("rowtc",),))
            keys3 = pk("psB", 3 * 512, TC)
            for tqr in range((TC + 511) // 512):
                n = min(512, TC - tqr * 512)
                ez.op("tensor", "matmul",
                      psB[:, 3 * 512 + tqr * 512: 3 * 512 + tqr * 512 + n],
                      A["ones_row_f32"][:],
                      A["rowtc"][0:1, tqr * 512:tqr * 512 + n],
                      start=True, stop=True,
                      rd=(("rowtc",), ("ones_row_f32",)), wr=keys3,
                      sig=(tqr == (TC + 511) // 512 - 1))
            ez.op("vector", "tensor_copy", A["s2rep"][:],
                  psB[:, 3 * 512: 3 * 512 + TC],
                  rd=keys3, wr=(("s2rep",),))
            ez.op("vector", "tensor_scalar", A["s2c"][:, 2 * TT:3 * TT],
                  A["s2c"][:, 0:TT], A["wsc"][:, 3:4], None, ALU.mult,
                  rd=(("s2c", "s2inv"), ("wsc", "sinv", "w2T")),
                  wr=(("s2c", "s2w"),))
            f2keys = pk("psA", 0, TT * NO2 * 512)
            h1stage = h1pad_v[j % 2]
            for g in range(NG):
                gsl = convB[:, g * GRP * TC:(g + 1) * GRP * TC]
                hsl = hq2[:, g * GRP * TC:(g + 1) * GRP * TC]
                bsl = h1stage[:, 0:GRP * TC]
                silk = tuple(("convB", h2, "s")
                             for h2 in range(g * GRP, (g + 1) * GRP))
                ez.op("vector", "tensor_tensor",
                      hsl.rearrange("p (hc t) -> p hc t", t=TC),
                      gsl.rearrange("p (hc t) -> p hc t", t=TC),
                      A["s2rep"][:].unsqueeze(1).broadcast_to([128, GRP, TC]),
                      ALU.mult,
                      rd=silk + (("s2rep",), ("run", g)),
                      wr=(("hq2", g, "m"), ("hq2", g, "r")) + silk)
                ez.op("vector", "tensor_scalar", hsl, hsl,
                      M32, M32, ALU.add, ALU.subtract,
                      rd=(("hq2", g, "m"),), wr=(("hq2", g, "a"),))
                ez.op("vector", "tensor_scalar", bsl, hsl,
                      M52, M52, ALU.add, ALU.subtract,
                      rd=(("hq2", g, "a"),),
                      wr=(("hq2", g, "b"),) +
                         tuple(("h1pad", j % 2, h2) for h2 in range(NHC)))
                ez.op("vector", "scalar_tensor_tensor", hsl, hsl,
                      A["hsel"][:, 0:1], bsl, ALU.mult, ALU.add,
                      rd=(("hq2", g, "a"), ("hq2", g, "b"), ("hsel",)),
                      wr=(("hq2", g, "r"),))
                qsl = ((j * NQ) + (g * GRP) // QH) % 2
                for hc in range(g * GRP, (g + 1) * GRP):
                    for t2 in range(TT):
                        for oh in range(NO2):
                            slot = (t2 * NO2 + oh) * 512
                            ez.op("tensor", "matmul",
                                  psA[:, slot:slot + OW],
                                  hq2[:, hc * TC + t2 * 128:
                                      hc * TC + (t2 + 1) * 128],
                                  wq2r[qsl][:, (hc % QH) * C + oh * OW:
                                            (hc % QH) * C + (oh + 1) * OW],
                                  start=(hc == 0), stop=(hc == NHC - 1),
                                  rd=(("hq2", g, "r"), ("wq2r", qsl)),
                                  wr=f2keys,
                                  sig=(hc % GRP == GRP - 1 and t2 == TT - 1
                                       and oh == NO2 - 1))
                for _ in range(((g + 1) * GRP) // QH - (g * GRP) // QH):
                    load_q2()
            for t2 in range(TT):
                psrc = psA[:, t2 * NO2 * 512:(t2 + 1) * NO2 * 512] \
                    .rearrange("p (n w) -> p n w", w=512)[:, :, 0:OW]
                ez.op("scalar", "activation",
                      f2tmp_v.rearrange("p (n w) -> p n w", w=OW), psrc,
                      AF.Identity,
                      scale=A["s2c"][:, 2 * TT + t2:2 * TT + t2 + 1],
                      rd=f2keys + (("s2c", "s2w"),),
                      wr=(("f2t", 0), ("f2t", 1)))
                fo_i = (j * TT + t2) % 2
                fo = fout_v[fo_i]
                fok = ("fout", fo_i)
                ez.op("vector", "tensor_tensor", fo, f2tmp_v,
                      A["gb_rep"][:], ALU.mult,
                      rd=(("f2t", 0), ("gb_rep",)), wr=(fok, ("f2t", 1)))
                ez.dma("gpsimd",
                       out_hbm[j * TC + t2 * 128: j * TC + (t2 + 1) * 128, :],
                       fo, "so%d" % fo_i,
                       rd=(fok,), wr=(("outf", j, t2), ("xmid", j * TT + t2)),
                       accum_op=ALU.add)

        for j in range(NFC + 3):
            j1 = j - 2 if 2 <= j < NFC + 2 else None
            j3 = j - 3 if j >= 3 else None
            if j1 is not None or j3 is not None:
                ffn1_conv(j1, j3)
            if j < NFC:
                ln2_quant(j)
            if j <= NFC:
                hqs_edges(j)
            if j3 is not None:
                ffn_out(j3)

        for _sn in ("so0", "so1", "xm0", "xm1", "wqh"):
            nc.sync.wait_ge(ez.dsem[_sn][0], ez.dsem[_sn][1])

    return nc


def _host_inputs(cfg, inputs, core):
    f32 = np.float32
    C, NCC, NHC = cfg.C, cfg.NCC, cfg.NHC
    ct = np.ascontiguousarray
    p = "L0_"
    m = {
        "x": ct(inputs[p + "attn_x"][core].astype(f32)),
        "wqT": ct(inputs[p + "attn_Wq"].astype(f32).T),
        "wkT": ct(inputs[p + "attn_Wk"].astype(f32).T),
        "wvT": ct(inputs[p + "attn_Wv"].astype(f32).T),
        "woT": ct(inputs[p + "attn_Wo"].astype(f32).T),
        "w1T": ct(inputs[p + "ffn_w1"].astype(f32).T),
        "w2T": ct(inputs[p + "ffn_w2"].astype(f32).T),
        "wmod": ct(inputs[p + "wmod"].astype(f32)),
        "c_fm": ct(inputs[p + "c"][0 if inputs[p + "c"].shape[0] == 1 else core]
                   .astype(f32).reshape(NCC, 128).T),
        "bq_row": ct(inputs[p + "attn_bq"].astype(f32).reshape(1, C)),
        "bv_row": ct(inputs[p + "attn_bv"].astype(f32).reshape(1, C)),
        "bo_row": ct(inputs[p + "attn_bo"].astype(f32).reshape(1, C)),
        "lng_fm": ct(inputs[p + "attn_ln_g"].astype(f32).reshape(NCC, 128).T),
        "lnb_fm": ct(inputs[p + "attn_ln_b"].astype(f32).reshape(NCC, 128).T),
        "convw_fm": ct(inputs[p + "ffn_conv_w"].astype(f32)
                       .reshape(NHC, 128, 3).transpose(1, 0, 2)
                       .reshape(128, 3 * NHC)),
        "convb_fm": ct(inputs[p + "ffn_conv_b"].astype(f32)
                       .reshape(NHC, 128).T),
        "ident_bf": np.eye(128, dtype=ml_dtypes.bfloat16),
        "ident_f32": np.eye(128, dtype=f32),
        "ones_row_bf": np.ones((1, 128), dtype=ml_dtypes.bfloat16),
        "ones_row_f32": np.ones((1, 128), dtype=f32),
        "onesmat_bf": np.ones((128, 128), dtype=ml_dtypes.bfloat16),
    }
    return m


_CACHED = {}


def kernel(**inputs):
    cfg = CFG()
    key = "nc"
    if key not in _CACHED:
        _CACHED[key] = build_nc(cfg)
    nc = _CACHED[key]
    in_maps = [_host_inputs(cfg, inputs, core) for core in range(8)]
    res = run_bass_kernel_spmd(nc, in_maps, core_ids=list(range(8)))
    out = np.stack([np.asarray(res.results[core]["out"])
                    for core in range(8)], axis=0)
    return out.astype(np.float32)


# version 29
# speedup vs baseline: 1.0996x; 1.0996x over previous
import contextlib
import numpy as np
import ml_dtypes
import concourse.bass as bass
import concourse.mybir as mybir
from concourse.bass_utils import run_bass_kernel_spmd

F32 = mybir.dt.float32
BF16 = mybir.dt.bfloat16
FP8 = mybir.dt.float8e4
AF = mybir.ActivationFunctionType
ALU = mybir.AluOpType
AX = mybir.AxisListType

M32 = float(np.float32(1.5 * 2 ** 23))
M52 = float(np.float32(1.5 * 2 ** 52))
EPS = 1e-5


class CFG:
    def __init__(self, T=2048, C=1024, HID=4096, D=64, TC=256):
        self.T = T
        self.C = C
        self.HID = HID
        self.D = D
        self.H = C // D
        self.TC = TC
        self.NT = T // 128
        self.NCC = C // 128
        self.NHC = HID // 128
        self.NFC = T // TC


class EZ:
    CENG = ("tensor", "vector", "scalar", "gpsimd")

    def __init__(self, nc, es):
        self.nc = nc
        self.osem = {}
        self.ocnt = {}
        for e in self.CENG:
            self.osem[e] = es.enter_context(nc.semaphore("o_" + e))
            self.ocnt[e] = 0
        self.dsem = {}
        self.wm = {}
        self.wr = {}
        self.rd = {}
        self.pend = {e: [] for e in self.CENG + ("sync",)}
        self.pend_rd = {e: [] for e in self.CENG + ("sync",)}
        self.es = es

    def mksem(self, name):
        self.dsem[name] = [self.es.enter_context(self.nc.semaphore(name)), 0]

    def _emit_wait(self, engine, evt):
        if evt is None:
            return
        if evt[0] == "e":
            _, peng, val = evt
            key = (engine, "o", peng)
            if self.wm.get(key, 0) >= val:
                return
            getattr(self.nc, engine).wait_ge(self.osem[peng], val)
            self.wm[key] = val
        else:
            _, sname, val = evt
            key = (engine, "d", sname)
            if self.wm.get(key, 0) >= val:
                return
            getattr(self.nc, engine).wait_ge(self.dsem[sname][0], val)
            self.wm[key] = val

    def _pre(self, engine, rd, wr):
        need = {}

        def add(evt):
            if evt is None:
                return
            key = ("o", evt[1]) if evt[0] == "e" else ("d", evt[1])
            if need.get(key, 0) < evt[2]:
                need[key] = evt[2]

        for k in rd:
            add(self.wr.get(k))
        for k in wr:
            add(self.wr.get(k))
            for evt in self.rd.get(k, ()):
                add(evt)
        for key, val in need.items():
            self._emit_wait(engine, ("e", key[1], val) if key[0] == "o"
                            else ("d", key[1], val))

    def _post(self, engine, evt, rd, wr, sig):
        if sig:
            for k in self.pend[engine]:
                self.wr[k] = evt
                self.rd[k] = []
            self.pend[engine] = []
            for k in wr:
                self.wr[k] = evt
                self.rd[k] = []
            for k in self.pend_rd[engine]:
                self.rd.setdefault(k, []).append(evt)
            self.pend_rd[engine] = []
            for k in rd:
                self.rd.setdefault(k, []).append(evt)
        else:
            self.pend[engine].extend(wr)
            self.pend_rd[engine].extend(rd)

    def op(self, engine, fname, *args, rd=(), wr=(), sig=True, **kw):
        self._pre(engine, rd, wr)
        instr = getattr(getattr(self.nc, engine), fname)(*args, **kw)
        evt = None
        if sig:
            self.ocnt[engine] += 1
            instr.then_inc(self.osem[engine], 1)
            evt = ("e", engine, self.ocnt[engine])
        self._post(engine, evt, rd, wr, sig)
        return instr

    def dma(self, engine, out, in_, sname, rd=(), wr=(), transpose=False, **dkw):
        self._pre(engine, rd, wr)
        kw = dict(dkw)
        if transpose:
            kw["transpose"] = True
        instr = getattr(self.nc, engine).dma_start(out=out, in_=in_, **kw)
        s = self.dsem[sname]
        s[1] += 16
        instr.then_inc(s[0], 16)
        self._post(engine, ("d", sname, s[1]), rd, wr, True)
        return instr

    def seal(self, sname):
        tot = self.dsem[sname][1]
        for k, evt in self.wr.items():
            if evt is not None and evt[0] == "d" and evt[1] == sname:
                self.wr[k] = ("d", sname, tot)

    def barrier(self, dmas=()):
        for e in ("tensor", "scalar", "gpsimd"):
            if self.ocnt[e] > 0:
                self._emit_wait("vector", ("e", e, self.ocnt[e]))
        for dn in dmas:
            if self.dsem[dn][1] > 0:
                self._emit_wait("vector", ("d", dn, self.dsem[dn][1]))
        self.ocnt["vector"] += 1
        self.nc.vector.engine_nop().then_inc(self.osem["vector"], 1)
        evt = ("e", "vector", self.ocnt["vector"])
        for e in ("tensor", "scalar", "gpsimd", "sync"):
            self._emit_wait(e, evt)


def rnd32(x):
    return (x + 31) // 32 * 32


def build_nc(cfg: CFG):
    nc = bass.Bass("TRN2", target_bir_lowering=False)
    T, C, HID, D, H = cfg.T, cfg.C, cfg.HID, cfg.D, cfg.H
    NT, NCC, NHC, TC, NFC = cfg.NT, cfg.NCC, cfg.NHC, cfg.TC, cfg.NFC
    HP = H // 2
    TT = TC // 128
    OW = min(512, C)
    NO2 = C // OW
    TQW = min(512, T)
    NT4 = T // TQW
    HW_ = TC + 2
    inv_C = 1.0 / C
    NJ = 6 * NCC
    THALF = min(1024, T)
    NKH = T // THALF
    KHT = THALF // 128
    PW1 = min(1024, HID)
    PW2 = min(1024, C)
    NQ = max(1, NHC // 8)
    QH = NHC // NQ
    QW1 = NCC * HID // NQ
    QW2 = NHC * C // NQ
    GRP = 8 if NHC >= 8 else NHC
    NG = NHC // GRP

    d = {}

    def din(name, shape, dtype=F32):
        d[name] = nc.dram_tensor(name, shape, dtype, kind="ExternalInput").ap()

    din("x", [T, C])
    din("wqT", [C, C]); din("wkT", [C, C]); din("wvT", [C, C]); din("woT", [C, C])
    din("w1T", [C, HID]); din("w2T", [HID, C])
    din("wmod", [C, 6 * C])
    din("c_fm", [128, NCC])
    din("bq_row", [1, C]); din("bv_row", [1, C]); din("bo_row", [1, C])
    din("lng_fm", [128, NCC]); din("lnb_fm", [128, NCC])
    din("convw_fm", [128, 3 * NHC]); din("convb_fm", [128, NHC])
    din("ident_bf", [128, 128], BF16)
    din("ident_f32", [128, 128])
    din("ones_row_bf", [1, 128], BF16)
    din("ones_row_f32", [1, 128])
    din("onesmat_bf", [128, 128], BF16)
    out_hbm = nc.dram_tensor("out", [T, C], F32, kind="ExternalOutput").ap()
    wq1h = nc.dram_tensor("wq1h", [128, NCC * HID], FP8,
                          kind="ExternalOutput").ap()
    wq2h = nc.dram_tensor("wq2h", [128, NHC * C], FP8,
                          kind="ExternalOutput").ap()

    es = contextlib.ExitStack()
    ez = EZ(nc, es)
    A = {}

    def sb(name, shape, dtype):
        A[name] = es.enter_context(nc.sbuf_tensor("s_" + name, shape, dtype))
        return A[name]

    NW1P = NCC * (HID // PW1)
    NW2P = NHC * (C // PW2)
    for nm, shp, dt_ in [
        ("ident_bf", [128, 128], BF16), ("ident_f32", [128, 128], F32),
        ("ones_row_bf", [1, 128], BF16), ("ones_row_f32", [1, 128], F32),
        ("onesmat_bf", [128, 128], BF16),
        ("c_fm", [128, NCC], F32), ("c_sil", [128, NCC], BF16),
        ("bq_row", [1, C], BF16), ("bv_row", [1, C], BF16),
        ("bo_row", [1, C], BF16),
        ("lng_fm", [128, NCC], F32), ("lnb_fm", [128, NCC], F32),
        ("convw_fm", [128, 3 * NHC], F32), ("convb_fm", [128, NHC], F32),
        ("stats", [128, 3 * NT], F32), ("sgn32", [128, NCC], F32),
        ("mod_col", [128, 6 * NCC], F32), ("modpart", [128, 6 * NCC], F32),
        ("ca1_col", [128, NCC], F32), ("sa_col", [128, NCC], F32),
        ("ga_col", [128, NCC], F32), ("gb_col", [128, NCC], F32),
        ("cb1_rep", [128, C], F32), ("sb_rep", [128, C], F32),
        ("ga_rep", [128, C], BF16), ("gb_rep", [128, C], BF16),
        ("kstats", [128, 2 * NCC], F32),
        ("epsv", [128, 1], F32), ("qstat", [128, 2 * H + 1], F32),
        ("state_sb", [128, HP * D], F32), ("state_bf", [128, HP * D], BF16),
        ("fstats", [128, 5 * TT], F32), ("s2c", [128, 3 * TT], F32),
        ("rowtc", [1, TC], F32), ("wsc", [128, 8], F32),
        ("wsums", [128, NW1P + NW2P + 8], F32), ("wrow", [1, 8], F32),
        ("tapsf", [128, 3 * NHC], F32),
        ("s2rep", [128, TC], BF16),
        ("hsel", [128, 1], F32), ("nm32c", [128, 1], F32),
        ("m52c", [128, 1], F32),
    ]:
        sb(nm, shp, dt_)

    RB1E = max(16 * C, NT * C, NCC * T,
               TT * C + 2 * rnd32(NCC * HW_) + NHC * TC + 64)
    RB2E = NCC * T
    SLOT3 = max(NCC * C, rnd32(NHC * HW_))
    RB3E = 2 * SLOT3
    SCBE = max(2 * T, 2 * C)
    ABE = RB1E + RB2E + RB3E + SCBE
    arB = es.enter_context(nc.sbuf_tensor("arB", [128, ABE], BF16))

    RF1E = max(2 * T + 4 * C, 3 * TT * C + 2 * C, 3 * C,
               max(2 * T, 2 * C + 2 * PW1) + 2 * PW1)
    SCW = max(C, NG * TC, PW1)
    SCW2 = SCW
    SCFE = 3 * SCW
    AFE = RF1E + SCFE
    arF = es.enter_context(nc.sbuf_tensor("arF", [128, AFE], F32))

    APE = 2 * QW1 + 2 * QW2
    arP = es.enter_context(nc.sbuf_tensor("arP", [128, APE], FP8))

    def vB(off, n, rows=128):
        assert off + n <= ABE, (off, n, ABE)
        return arB[0:rows, off:off + n]

    def vF(off, n, rows=128):
        assert off + n <= AFE, (off, n, AFE)
        return arF[0:rows, off:off + n]

    def vP(off, n):
        assert off + n <= APE, (off, n, APE)
        return arP[:, off:off + n]

    O1 = 0
    xn4 = [vB(O1 + i * C, C) for i in range(4)]
    wmst_v = [vB(O1 + 4 * C, 6 * C), vB(O1 + 10 * C, 6 * C)]
    v_tm = vB(O1, NT * C)
    q_fm = vB(O1, NCC * T)
    hqs_tm = vB(O1, TT * C)
    _o = O1 + TT * C
    hqs_v = [vB(_o, NCC * HW_), vB(_o + rnd32(NCC * HW_), NCC * HW_)]
    _o += 2 * rnd32(NCC * HW_)
    convB = vB(_o, NHC * TC)
    hq2 = convB
    O2 = RB1E
    h_fm = vB(O2, NCC * T)
    attn_fm = vB(O2, NCC * T)
    O3 = RB1E + RB2E
    wslot = [vB(O3, NCC * C), vB(O3 + SLOT3, NCC * C)]
    h1pad_v = [vB(O3, NHC * HW_), vB(O3 + SLOT3, NHC * HW_)]
    OS = RB1E + RB2E + RB3E
    kexp_fm = vB(OS, T)
    kslice_v = [vB(OS + T, 128), vB(OS + T + 128, 128)]
    sqb_v = [vB(OS, T), vB(OS + T, T)]
    b_rep = vB(OS, T)
    ascr_v = vB(OS + T, T)
    qtb_s = vB(OS, 2 * C)
    a_rep = vB(OS + T, T)

    OF = 0
    WSB2 = max(2 * T, 2 * C + 2 * PW1)
    wstage_v = [vF(OF + 2 * C, PW1), vF(OF + 2 * C + PW1, PW1),
                vF(OF + WSB2, PW1), vF(OF + WSB2 + PW1, PW1)]
    xt_v = [vF(OF + i * C, C) for i in range(4)]
    qe_v = [vF(OF, C), vF(OF + C, C)]
    m_rep = vF(OF, T)
    v_rep = vF(OF + T, T)
    xres_v = [vF(OF + 2 * T, C), vF(OF + 2 * T + C, C)]
    xout_v = [vF(OF + 2 * T + 2 * C, C), vF(OF + 2 * T + 3 * C, C)]
    xc_v = [vF(OF, TT * C), vF(OF + TT * C, TT * C)]
    xn2c = vF(OF + 2 * TT * C, TT * C)
    fout_v = [vF(OF + 3 * TT * C, C), vF(OF + 3 * TT * C + C, C)]
    OSF = OF + RF1E
    runmax_s = vF(OSF, NG * TC)
    f2tmp_v = vF(OSF + SCW, C)
    lnA = vF(OSF + 2 * SCW, C)
    rowtmp = vF(OSF + 2 * SCW, C, rows=1)[0:1, :]
    wqstg = vF(OSF + 2 * SCW, PW1)
    wsgn = vF(OSF + SCW, PW1)
    tern8 = None

    wq1r = [vP(0, QW1), vP(QW1, QW1)]
    wq2r = [vP(2 * QW1, QW2), vP(2 * QW1 + QW2, QW2)]
    tern8 = vP(0, PW1)

    psA = es.enter_context(nc.psum_tensor("psA", [128, 2048], F32))
    psB = es.enter_context(nc.psum_tensor("psB", [128, 2048], F32))
    PS = {"psA": psA, "psB": psB}

    def pk(psn, col0, width):
        b0 = col0 // 512
        b1 = (col0 + width - 1) // 512
        return tuple((psn, b) for b in range(b0, b1 + 1))

    for s in ["cst", "cstg", "xt0", "xt1", "xt2", "xt3", "wm0", "wm1", "wk", "wv",
              "wq", "wo", "wst0", "wst1", "wst2", "wst3", "xr0", "xr1", "xm0", "xm1",
              "so0", "so1", "xc0", "xc1", "wqh", "wr1a", "wr1b", "wr2a", "wr2b"]:
        ez.mksem(s)

    with es:
        es.enter_context(nc.Block())

        ez.op("vector", "memset", A["epsv"][:], EPS, wr=(("epsv",),))
        ez.op("vector", "memset", A["hsel"][:], 1.0, wr=(("hsel", 0),))
        ez.op("vector", "memset", A["nm32c"][:], -M32, wr=(("nm32c",),))
        ez.op("vector", "memset", A["m52c"][:], M52, wr=(("m52c",),))
        ez.op("vector", "tensor_scalar", A["hsel"][:], A["hsel"][:],
              M52, M52, ALU.add, ALU.subtract,
              rd=(("hsel", 0),), wr=(("hsel", 1),))
        ez.op("vector", "tensor_scalar", A["hsel"][:], A["hsel"][:],
              -1.0, 1.0, ALU.mult, ALU.add,
              rd=(("hsel", 1),), wr=(("hsel",),))

        for nm in ["ident_bf", "ident_f32", "ones_row_bf", "ones_row_f32",
                   "onesmat_bf", "c_fm", "lng_fm", "lnb_fm", "convw_fm",
                   "convb_fm"]:
            ez.dma("sync", A[nm][:], d[nm], "cst", wr=((nm,),))
        for nm in ["bq_row", "bv_row", "bo_row"]:
            ez.dma("gpsimd", A[nm][:], d[nm], "cstg", wr=((nm,),))
        ez.seal("cst")
        ez.seal("cstg")

        def load_w(nm, slot, sem, claims=()):
            ez.dma("gpsimd", wslot[slot].rearrange("p (j c) -> p j c", c=C),
                   d[nm].rearrange("(j p) c -> p j c", p=128), sem,
                   wr=((nm,),) + claims)

        def load_wmod(i):
            ez.dma("gpsimd", wmst_v[i % 2][:], d["wmod"][i * 128:(i + 1) * 128, :],
                   "wm%d" % (i % 2), wr=(("wmst", i % 2),))

        for _i in range(min(2, NCC)):
            load_wmod(_i)
        load_w("wvT", 0, "wv")
        load_w("wkT", 1, "wk")

        st = A["stats"]

        def load_xt(i):
            ez.dma("sync", xt_v[i % 4], d["x"][i * 128:(i + 1) * 128, :],
                   "xt%d" % (i % 4), wr=(("xt", i % 4),))

        for _i in range(min(4, NT)):
            load_xt(_i)
        sq_out_bf = vB(OS, C)
        for g0 in range(0, NT, 4):
            ng = min(4, NT - g0)
            for gi in range(ng):
                i = g0 + gi
                xt = xt_v[i % 4]
                xk = ("xt", i % 4)
                ez.op("vector", "reduce_sum", st[:, i:i + 1], xt, axis=AX.X,
                      rd=(xk,), wr=(("st", "sum", i),))
                ez.op("scalar", "activation", sq_out_bf, xt, AF.Square,
                      accum_out=st[:, NT + i:NT + i + 1], rd=(xk,),
                      wr=(("st", "sq", i), ("scrb0",)))
            sm = st[:, g0:g0 + ng]
            sq = st[:, NT + g0:NT + g0 + ng]
            c2 = st[:, 2 * NT + g0:2 * NT + g0 + ng]
            sumk = tuple(("st", "sum", g0 + z) for z in range(ng))
            sqk = tuple(("st", "sq", g0 + z) for z in range(ng))
            ez.op("vector", "tensor_scalar", sm, sm, inv_C, None, ALU.mult,
                  rd=sumk, wr=(("st", "m", g0),))
            ez.op("vector", "tensor_scalar", sq, sq, inv_C, None, ALU.mult,
                  rd=sqk, wr=(("st", "v0", g0),))
            ez.op("vector", "tensor_tensor", c2, sm, sm, ALU.mult,
                  rd=(("st", "m", g0),), wr=(("st", "m2", g0),))
            ez.op("vector", "tensor_tensor", sq, sq, c2, ALU.subtract,
                  rd=(("st", "v0", g0), ("st", "m2", g0)),
                  wr=(("st", "var", g0),))
            ez.op("scalar", "activation", sq, sq, AF.Sqrt,
                  bias=A["epsv"][:, 0:1],
                  rd=(("st", "var", g0), ("epsv",)), wr=(("st", "rs", g0),))
            ez.op("vector", "reciprocal", sq, sq,
                  rd=(("st", "rs", g0),), wr=(("st", "rs2", g0),))
            ez.op("vector", "tensor_tensor", c2, sm, sq, ALU.mult,
                  rd=(("st", "m", g0), ("st", "rs2", g0)),
                  wr=(("st", "nb0", g0),))
            ez.op("vector", "tensor_scalar", c2, c2, -1.0, None, ALU.mult,
                  rd=(("st", "nb0", g0),), wr=(("st", "nb", g0),))
            for gi in range(ng):
                i = g0 + gi
                ez.op("scalar", "activation", xn4[i % 4], xt_v[i % 4],
                      AF.Identity, scale=st[:, NT + i:NT + i + 1],
                      bias=st[:, 2 * NT + i:2 * NT + i + 1],
                      rd=(("xt", i % 4), ("st", "rs2", g0), ("st", "nb", g0)),
                      wr=(("xn4", i % 4),))
                if i + 4 < NT:
                    load_xt(i + 4)
            for cc in range(NCC):
                bank = cc % 4
                keys = pk("psA", bank * 512, ng * 128)
                for gi in range(ng):
                    ez.op("tensor", "matmul",
                          psA[:, bank * 512 + gi * 128:
                              bank * 512 + (gi + 1) * 128],
                          xn4[(g0 + gi) % 4][:, cc * 128:(cc + 1) * 128],
                          A["ident_bf"][:], start=True, stop=True,
                          rd=(("xn4", (g0 + gi) % 4), ("ident_bf",)), wr=keys,
                          sig=(gi == ng - 1))
                ez.op("vector", "tensor_copy",
                      h_fm[:, cc * T + g0 * 128: cc * T + (g0 + ng) * 128],
                      psA[:, bank * 512: bank * 512 + ng * 128],
                      rd=keys,
                      wr=tuple(("xn_fm", cc, g0 + gi) for gi in range(ng)))

        ez.op("scalar", "activation", A["c_sil"][:], A["c_fm"][:], AF.Silu,
              rd=(("c_fm",),), wr=(("c_sil",),))
        ez.op("vector", "memset", A["mod_col"][:], 0.0, wr=(("mod_col", 0),))
        for i in range(NCC):
            for jb in range(NJ):
                bank = (i % 2) * 2 + (jb % 2)
                keys = pk("psB", bank * 512, 512)
                slot = jb // 2
                ez.op("tensor", "matmul",
                      psB[:, bank * 512 + slot:bank * 512 + slot + 1],
                      wmst_v[i % 2][:, jb * 128:(jb + 1) * 128],
                      A["c_sil"][:, i:i + 1],
                      start=True, stop=True,
                      rd=(("wmst", i % 2), ("c_sil",)), wr=keys,
                      sig=(jb >= NJ - 2))
            for b2 in range(2):
                bank = (i % 2) * 2 + b2
                keys = pk("psB", bank * 512, 512)
                dst = A["modpart"][:].rearrange("p (sl two) -> p sl two", two=2) \
                    [:, :, b2:b2 + 1]
                ez.op("vector", "tensor_copy", dst,
                      psB[:, bank * 512:bank * 512 + NJ // 2].unsqueeze(2),
                      rd=keys, wr=(("modpart", b2),))
            if i + 2 < NCC:
                load_wmod(i + 2)
            ez.op("vector", "tensor_tensor", A["mod_col"][:], A["mod_col"][:],
                  A["modpart"][:], ALU.add,
                  rd=tuple(("modpart", b2) for b2 in range(2)) +
                     (("mod_col", i),),
                  wr=(("mod_col", i + 1),))
        MODK = ("mod_col", NCC)
        mc = A["mod_col"]
        g_sa, g_ca, g_ga, g_sb, g_cb, g_gb = range(6)
        ez.op("vector", "tensor_scalar", A["ca1_col"][:],
              mc[:, g_ca * NCC:(g_ca + 1) * NCC], 1.0, None, ALU.add,
              rd=(MODK,), wr=(("ca1_col",),))
        ez.op("vector", "tensor_copy", A["sa_col"][:],
              mc[:, g_sa * NCC:(g_sa + 1) * NCC], rd=(MODK,), wr=(("sa_col",),))
        for gsl, nm in ((g_ga, "ga_col"), (g_gb, "gb_col")):
            sl = mc[:, gsl * NCC:(gsl + 1) * NCC]
            ez.op("scalar", "sign", A["sgn32"][:], sl, rd=(MODK,),
                  wr=(("sgn32",),))
            ez.op("scalar", "activation", A[nm][:], sl, AF.Abs,
                  rd=(MODK,), wr=((nm, 0),))
            ez.op("vector", "tensor_scalar", A[nm][:], A[nm][:], 0.25, None,
                  ALU.max, rd=((nm, 0),), wr=((nm, 1),))
            ez.op("vector", "tensor_tensor", A[nm][:], A[nm][:], A["sgn32"][:],
                  ALU.mult, rd=((nm, 1), ("sgn32",)), wr=((nm,),))

        def col_to_rep(col_ap, col_key, rep_name, add_one):
            keys = pk("psB", 0, C)
            for cc in range(NCC):
                ez.op("tensor", "matmul", psB[0:1, cc * 128:(cc + 1) * 128],
                      col_ap[:, cc:cc + 1], A["ident_f32"][:],
                      start=True, stop=True,
                      rd=(col_key, ("ident_f32",)), wr=keys,
                      sig=(cc == NCC - 1))
            ez.op("vector", "tensor_copy", rowtmp[0:1, :], psB[0:1, 0:C],
                  rd=keys, wr=(("rowtmp",),))
            keys2 = pk("psB", 1024, C)
            for h2 in range(NO2):
                ez.op("tensor", "matmul",
                      psB[:, 1024 + h2 * OW:1024 + (h2 + 1) * OW],
                      A["ones_row_f32"][:], rowtmp[0:1, h2 * OW:(h2 + 1) * OW],
                      start=True, stop=True,
                      rd=(("rowtmp",), ("ones_row_f32",)), wr=keys2,
                      sig=(h2 == NO2 - 1))
            ez.op("vector", "tensor_scalar", A[rep_name][:],
                  psB[:, 1024:1024 + C], 1.0 if add_one else 0.0, None, ALU.add,
                  rd=keys2, wr=((rep_name,),))

        col_to_rep(mc[:, g_cb * NCC:(g_cb + 1) * NCC], MODK, "cb1_rep", True)
        col_to_rep(mc[:, g_sb * NCC:(g_sb + 1) * NCC], MODK, "sb_rep", False)
        col_to_rep(A["ga_col"][:], ("ga_col",), "ga_rep", False)
        col_to_rep(A["gb_col"][:], ("gb_col",), "gb_rep", False)

        for cc in range(NCC):
            ez.op("vector", "tensor_scalar",
                  h_fm[:, cc * T:(cc + 1) * T], h_fm[:, cc * T:(cc + 1) * T],
                  A["ca1_col"][:, cc:cc + 1], A["sa_col"][:, cc:cc + 1],
                  ALU.mult, ALU.add,
                  rd=tuple(("xn_fm", cc, i) for i in range(NT)) +
                     (("ca1_col",), ("sa_col",)),
                  wr=(("h_fm", cc),))

        ez.barrier()

        wq_work = []

        def wqd(n=1):
            for _ in range(n):
                if wq_work:
                    wq_work.pop(0)()

        st_i = [0]

        def quant_weights_pieces(dname, n_chunks, row_len, sumoff, scol,
                                 sinvcol, negcol, dst_hbm, dst_name, mean_n,
                                 auxcol, pw):
            npc = row_len // pw
            np_tot = n_chunks * npc

            def stage(pi):
                ci, pj = divmod(pi, npc)
                bidx = st_i[0] % 4
                buf = wstage_v[bidx]
                src = d[dname][ci * 128:(ci + 1) * 128, pj * pw:(pj + 1) * pw]
                ez.dma("gpsimd", buf[:, 0:pw], src, "wst%d" % bidx,
                       wr=(("wstage", bidx),))
                st_i[0] += 1
                return buf, ("wstage", bidx)

            def mk_mean(pi):
                def go():
                    buf, bk = stage(pi)
                    ez.op("vector", "tensor_reduce",
                          A["wsums"][:, sumoff + pi:sumoff + pi + 1],
                          buf[:, 0:pw], axis=AX.X, op=ALU.add,
                          apply_absolute_value=True,
                          rd=(bk,), wr=(("wsums", dname, pi),))
                return go

            for pi in range(np_tot):
                wq_work.append(mk_mean(pi))

            def finalize():
                allk = tuple(("wsums", dname, pi) for pi in range(np_tot))
                ez.op("vector", "reduce_sum", A["wsc"][:, auxcol:auxcol + 1],
                      A["wsums"][:, sumoff:sumoff + np_tot], axis=AX.X,
                      rd=allk, wr=(("wsc", dname, "ps"),))
                keys = pk("psA", 1024, 128)
                ez.op("tensor", "matmul", psA[0:1, 1024:1152],
                      A["wsc"][:, auxcol:auxcol + 1], A["ident_f32"][:],
                      start=True, stop=True,
                      rd=(("wsc", dname, "ps"), ("ident_f32",)), wr=keys,
                      sig=True)
                ez.op("vector", "tensor_reduce", A["wrow"][0:1, 0:1],
                      psA[0:1, 1024:1152], axis=AX.X, op=ALU.add,
                      rd=keys, wr=(("wrow", dname, "tot"),))
                ez.op("vector", "tensor_scalar", A["wrow"][0:1, 1:2],
                      A["wrow"][0:1, 0:1], 1.0 / mean_n, 1e-5, ALU.mult,
                      ALU.max,
                      rd=(("wrow", dname, "tot"),),
                      wr=(("wrow", dname, "mean"),))
                ez.op("vector", "reciprocal", A["wrow"][0:1, 2:3],
                      A["wrow"][0:1, 1:2],
                      rd=(("wrow", dname, "mean"),), wr=(("wrow", dname, "s"),))
                keys2 = pk("psA", 1280, 2)
                ez.op("tensor", "matmul", psA[:, 1280:1282],
                      A["ones_row_f32"][:], A["wrow"][0:1, 1:3],
                      start=True, stop=True,
                      rd=(("wrow", dname, "mean"), ("wrow", dname, "s"),
                          ("ones_row_f32",)),
                      wr=keys2, sig=True)
                ez.op("vector", "tensor_copy",
                      A["wsc"][:, sinvcol:sinvcol + 1], psA[:, 1280:1281],
                      rd=keys2, wr=(("wsc", "sinv", dname),))
                ez.op("vector", "tensor_copy", A["wsc"][:, scol:scol + 1],
                      psA[:, 1281:1282], rd=keys2, wr=(("wsc", "s", dname),))
                ez.op("vector", "tensor_scalar", A["wsc"][:, negcol:negcol + 1],
                      A["wsc"][:, scol:scol + 1], -1.0, None, ALU.mult,
                      rd=(("wsc", "s", dname),), wr=(("wsc", "negs", dname),))

            wq_work.append(finalize)

            def mk_tern(pi):
                def go():
                    ci, pj = divmod(pi, npc)
                    buf, bk = stage(pi)
                    w_sl = buf[:, 0:pw]
                    ez.op("vector", "tensor_scalar", wqstg[:, 0:pw], w_sl,
                          A["wsc"][:, scol:scol + 1], 0.5, ALU.mult, ALU.is_ge,
                          rd=(bk, ("wsc", "s", dname)),
                          wr=(("wqstg",), ("rowtmp",), ("lnA", 0),
                              ("lnA", 1)))
                    ez.op("vector", "tensor_scalar", w_sl, w_sl,
                          A["wsc"][:, negcol:negcol + 1], 0.5, ALU.mult,
                          ALU.is_ge,
                          rd=(bk, ("wsc", "negs", dname)), wr=(bk,))
                    ez.op("vector", "tensor_tensor", tern8[:, 0:pw],
                          wqstg[:, 0:pw], w_sl, ALU.subtract,
                          rd=(("wqstg",), bk), wr=(("tern8",),))
                    ez.dma("sync",
                           dst_hbm[:, ci * row_len + pj * pw:
                                   ci * row_len + (pj + 1) * pw],
                           tern8[:, 0:pw], "wqh",
                           rd=(("tern8",),), wr=((dst_name, pi),))
                return go

            for pi in range(np_tot):
                wq_work.append(mk_tern(pi))

        quant_weights_pieces("w1T", NCC, HID, 0, 0, 1, 6, wq1h, "wq1h",
                             C * HID, 4, PW1)
        quant_weights_pieces("w2T", NHC, C, NW1P, 2, 3, 7, wq2h, "wq2h",
                             C * HID, 5, PW2)

        def proj_tm_tile(i, wname, wslot_i, brow, psn):
            ps = PS[psn]
            keys = pk(psn, 0, C)
            for ci in range(NCC):
                lhsT = h_fm[:, ci * T + i * 128: ci * T + (i + 1) * 128]
                for oh in range(NO2):
                    ez.op("tensor", "matmul",
                          ps[:, oh * OW:(oh + 1) * OW], lhsT,
                          wslot[wslot_i][:, ci * C + oh * OW:
                                         ci * C + (oh + 1) * OW],
                          start=(ci == 0), stop=False,
                          rd=(("h_fm", ci), (wname,)), wr=keys, sig=False)
            for oh in range(NO2):
                ez.op("tensor", "matmul", ps[:, oh * OW:(oh + 1) * OW],
                      A["ones_row_bf"][:],
                      A[brow][0:1, oh * OW:(oh + 1) * OW],
                      start=False, stop=True,
                      rd=(("ones_row_bf",), (brow,)), wr=keys,
                      sig=(oh == NO2 - 1))
            return ps, keys

        for i in range(NT):
            psn = "psA" if i % 2 == 0 else "psB"
            ps, keys = proj_tm_tile(i, "wvT", 0, "bv_row", psn)
            ez.op("scalar", "activation", v_tm[:, i * C:(i + 1) * C],
                  ps[:, 0:C], AF.Identity, rd=keys, wr=(("v_tm", i),))
            wqd(3)
        load_w("wqT", 0, "wq", claims=(("wvT",),))

        kst = A["kstats"]
        for oc in range(NCC):
            psn = "psA" if oc % 2 == 0 else "psB"
            opn = "psB" if oc % 2 == 0 else "psA"
            ps = PS[psn]
            stk_e = pk(psn, 1536, 64)
            stk_o = pk(opn, 1536, 64)
            for half in range(NKH):
                keys = pk(psn, 0, THALF)
                for ci in range(NCC):
                    lhsT = wslot[1][:, ci * C + oc * 128: ci * C + (oc + 1) * 128]
                    for tq in range(THALF // TQW):
                        t0 = half * THALF + tq * TQW
                        ez.op("tensor", "matmul",
                              ps[:, tq * TQW:(tq + 1) * TQW], lhsT,
                              h_fm[:, ci * T + t0: ci * T + t0 + TQW],
                              start=(ci == 0), stop=(ci == NCC - 1),
                              rd=(("wkT",), ("h_fm", ci)), wr=keys,
                              sig=(ci == NCC - 1 and tq == THALF // TQW - 1))
                scol = (NCC + oc) if half == 0 else oc
                ez.op("scalar", "activation", kexp_fm[:, 0:THALF],
                      ps[:, 0:THALF], AF.Exp,
                      accum_out=kst[:, scol:scol + 1],
                      rd=keys, wr=(("kexp",), ("kst", "sum", oc, half),
                                   ("scrb0",)))
                for it in range(KHT):
                    i = half * KHT + it
                    bank = it % 2
                    keys2 = pk(psn, bank * 512, 128)
                    ksl = kslice_v[i % 2]
                    ez.op("tensor", "matmul",
                          ps[:, bank * 512:bank * 512 + 128],
                          kexp_fm[:, it * 128:(it + 1) * 128],
                          A["ident_bf"][:], start=True, stop=True,
                          rd=(("kexp",), ("ident_bf",)), wr=keys2, sig=True)
                    ez.op("scalar", "activation", ksl,
                          ps[:, bank * 512:bank * 512 + 128], AF.Identity,
                          rd=keys2, wr=(("ksl", i % 2),))
                    ez.op("tensor", "matmul",
                          ps[0:64, 1536:1600],
                          ksl[:, 0:64],
                          v_tm[:, i * C + oc * 128: i * C + oc * 128 + 64],
                          start=(i == 0), stop=(i == NT - 1),
                          tile_position=(0, 0),
                          rd=(("ksl", i % 2), ("v_tm", i)),
                          wr=stk_e, sig=True)
                    ez.op("tensor", "matmul",
                          PS[opn][64:128, 1536:1600],
                          ksl[:, 64:128],
                          v_tm[:, i * C + oc * 128 + 64:
                               i * C + oc * 128 + 128],
                          start=(i == 0), stop=(i == NT - 1),
                          tile_position=(0, 64),
                          rd=(("ksl", i % 2), ("v_tm", i)),
                          wr=stk_o, sig=True)
                wqd(3)
            if NKH > 1:
                ez.op("vector", "tensor_tensor", kst[:, NCC + oc:NCC + oc + 1],
                      kst[:, NCC + oc:NCC + oc + 1], kst[:, oc:oc + 1],
                      ALU.add,
                      rd=(("kst", "sum", oc, 0), ("kst", "sum", oc, 1)),
                      wr=(("kst", "tot", oc),))
            ez.op("scalar", "activation",
                  A["state_sb"][0:64, oc * D:(oc + 1) * D],
                  PS[psn][0:64, 1536:1600], AF.Identity,
                  rd=stk_e, wr=(("state0", oc),))
            ez.op("scalar", "activation",
                  A["state_sb"][64:128, oc * D:(oc + 1) * D],
                  PS[opn][64:128, 1536:1600], AF.Identity,
                  rd=stk_o, wr=(("state1", oc),))
            wqd(2)
        ez.op("vector", "reciprocal", kst[:, NCC:2 * NCC], kst[:, NCC:2 * NCC],
              rd=tuple(("kst", "tot" if NKH > 1 else "sum", oc) +
                       (() if NKH > 1 else (0,))
                       for oc in range(NCC)),
              wr=(("kst", "recip"),))
        for oc in range(NCC):
            for pb in (0, 64):
                ez.op("scalar", "activation",
                      A["state_bf"][pb:pb + 64, oc * D:(oc + 1) * D],
                      A["state_sb"][pb:pb + 64, oc * D:(oc + 1) * D],
                      AF.Identity,
                      scale=kst[pb:pb + 64, NCC + oc:NCC + oc + 1],
                      rd=(("state%d" % (pb // 64), oc), ("kst", "recip")),
                      wr=(("state%d" % (pb // 64), oc, "s"),))
        load_w("woT", 1, "wo", claims=(("wkT",),))

        ez.barrier()

        for i in range(NT):
            psn = "psA" if i % 2 == 0 else "psB"
            ps, keys = proj_tm_tile(i, "wqT", 0, "bq_row", psn)
            qe = qe_v[i % 2]
            qek = ("qe", i % 2)
            qs = A["qstat"]
            ez.op("vector", "tensor_reduce", qs[:, 2 * H:2 * H + 1],
                  ps[:, 0:C], axis=AX.X, op=ALU.max, negate=True,
                  rd=keys, wr=(("qmx", i),))
            ez.op("scalar", "activation", qe, ps[:, 0:C], AF.Exp,
                  bias=qs[:, 2 * H:2 * H + 1],
                  rd=keys + (("qmx", i),), wr=(qek,))
            ez.op("vector", "tensor_reduce", qs[:, 0:H].unsqueeze(2),
                  qe.rearrange("p (h dd) -> p h dd", dd=D), axis=AX.X,
                  op=ALU.add, rd=(qek,), wr=(("qs", i),))
            ez.op("vector", "reciprocal", qs[:, H:2 * H], qs[:, 0:H],
                  rd=(("qs", i),), wr=(("qsr", i),))
            qtb = qtb_s[:, (i % 2) * C:(i % 2 + 1) * C]
            ez.op("vector", "tensor_tensor",
                  qtb.rearrange("p (h dd) -> p h dd", dd=D),
                  qe.rearrange("p (h dd) -> p h dd", dd=D),
                  qs[:, H:2 * H].unsqueeze(2).broadcast_to([128, H, D]),
                  ALU.mult,
                  rd=(qek, ("qsr", i)), wr=(("qtb", i % 2),))
            for gq in range(0, NCC, 4):
                bank = 2 + (gq // 4) % 2
                ngq = min(4, NCC - gq)
                keys2 = pk(psn, bank * 512, ngq * 128)
                for gi in range(ngq):
                    cc = gq + gi
                    ez.op("tensor", "matmul",
                          ps[:, bank * 512 + gi * 128:
                             bank * 512 + (gi + 1) * 128],
                          qtb[:, cc * 128:(cc + 1) * 128],
                          A["ident_bf"][:], start=True, stop=True,
                          rd=(("qtb", i % 2), ("ident_bf",)), wr=keys2,
                          sig=(gi == ngq - 1))
                dst = q_fm.rearrange("p (hp t) -> p hp t", t=T) \
                    [:, gq:gq + ngq, i * 128:(i + 1) * 128]
                ez.op("scalar", "activation", dst,
                      ps[:, bank * 512:bank * 512 + ngq * 128]
                      .rearrange("p (n w) -> p n w", w=128), AF.Identity,
                      rd=keys2,
                      wr=tuple(("q_fm", gq + gi, i) for gi in range(ngq)))
            wqd(3)

        ez.barrier()

        for hp in range(HP):
            psn = "psA" if hp % 2 == 0 else "psB"
            ps = PS[psn]
            keys = pk(psn, 0, T)
            for tq in range(NT4):
                for pb in (0, 64):
                    ez.op("tensor", "matmul",
                          ps[pb:pb + 64, tq * TQW:(tq + 1) * TQW],
                          A["state_bf"][pb:pb + 64, hp * D:(hp + 1) * D],
                          q_fm[pb:pb + 64, hp * T + tq * TQW:
                               hp * T + (tq + 1) * TQW],
                          start=True, stop=True, tile_position=(pb, pb),
                          rd=(("state%d" % (pb // 64), hp, "s"),
                              ("q_fm", hp, 4 * tq)) +
                             tuple(("q_fm", hp, 4 * tq + z)
                                   for z in range(1, TQW // 128)),
                          wr=keys, sig=(pb == 64))
            ez.op("scalar", "activation", attn_fm[:, hp * T:(hp + 1) * T],
                  ps[:, 0:T], AF.Identity, rd=keys, wr=(("attn_fm", hp),))
            wqd(2)
        wqd(len(wq_work))

        for hp in range(HP):
            sl = attn_fm[:, hp * T:(hp + 1) * T]
            sq = sqb_v[hp % 2]
            ez.op("scalar", "activation", sq, sl, AF.Square,
                  rd=(("attn_fm", hp),), wr=(("sqbuf", hp % 2),))
            psn, psq = ("psA", "psB")
            for tq in range(NT4):
                keysS = pk(psn, tq * TQW, TQW)
                keysQ = pk(psq, tq * TQW, TQW)
                ez.op("tensor", "matmul",
                      PS[psn][:, tq * TQW:(tq + 1) * TQW],
                      A["onesmat_bf"][:], sl[:, tq * TQW:(tq + 1) * TQW],
                      start=(hp == 0), stop=(hp == HP - 1),
                      rd=(("attn_fm", hp), ("onesmat_bf",)), wr=keysS,
                      sig=(tq == NT4 - 1))
                ez.op("tensor", "matmul",
                      PS[psq][:, tq * TQW:(tq + 1) * TQW],
                      A["onesmat_bf"][:], sq[:, tq * TQW:(tq + 1) * TQW],
                      start=(hp == 0), stop=(hp == HP - 1),
                      rd=(("sqbuf", hp % 2), ("onesmat_bf",)), wr=keysQ,
                      sig=(tq == NT4 - 1))
        pkS = pk("psA", 0, T)
        pkQ = pk("psB", 0, T)
        ez.op("vector", "tensor_scalar", m_rep, psA[:, 0:T], inv_C, None,
              ALU.mult, rd=pkS, wr=(("m_rep",),))
        ez.op("vector", "tensor_scalar", v_rep, psB[:, 0:T], inv_C, None,
              ALU.mult, rd=pkQ, wr=(("v_rep", 0), ("wstage", 0),
                                    ("wstage", 1)))
        ez.op("vector", "scalar_tensor_tensor", ascr_v, m_rep, -1.0, m_rep,
              ALU.mult, ALU.mult,
              rd=(("m_rep",),),
              wr=(("ascr",), ("sqbuf", 1)))
        ez.op("vector", "tensor_tensor", v_rep, v_rep, ascr_v, ALU.add,
              rd=(("v_rep", 0), ("ascr",)), wr=(("v_rep", 1),))
        ez.op("scalar", "activation", v_rep, v_rep, AF.Sqrt,
              bias=A["epsv"][:, 0:1],
              rd=(("v_rep", 1), ("epsv",)), wr=(("v_rep", "sd"),))
        ez.op("vector", "reciprocal", v_rep, v_rep,
              rd=(("v_rep", "sd"),), wr=(("v_rep", "rs"),))
        ez.op("vector", "tensor_copy", a_rep, v_rep,
              rd=(("v_rep", "rs"),), wr=(("a_rep",), ("ascr",),
                                         ("sqbuf", 1)))
        ez.op("vector", "tensor_tensor", m_rep, m_rep, v_rep, ALU.mult,
              rd=(("m_rep",), ("v_rep", "rs")), wr=(("m_rep", "mrs"),))
        ez.op("vector", "tensor_scalar", b_rep, m_rep, -1.0, None, ALU.mult,
              rd=(("m_rep", "mrs"),), wr=(("b_rep",), ("sqbuf", 0),
                                          ("kexp",)))

        for hp in range(HP):
            sl = attn_fm[:, hp * T:(hp + 1) * T]
            ez.op("vector", "tensor_tensor", sl, sl, a_rep, ALU.mult,
                  rd=(("attn_fm", hp), ("a_rep",)), wr=(("attn_fm", hp, "s"),))
            ez.op("vector", "tensor_tensor", sl, sl, b_rep, ALU.add,
                  rd=(("attn_fm", hp, "s"), ("b_rep",)),
                  wr=(("attn_fm", hp, "b"),))
            ez.op("scalar", "activation", sl, sl, AF.Identity,
                  scale=A["lng_fm"][:, hp:hp + 1], bias=A["lnb_fm"][:, hp:hp + 1],
                  rd=(("attn_fm", hp, "b"), ("lng_fm",), ("lnb_fm",)),
                  wr=(("lnout", hp),))

        def load_xres(i):
            ez.dma("sync", xres_v[i % 2], d["x"][i * 128:(i + 1) * 128, :],
                   "xr%d" % (i % 2),
                   wr=(("xres", i % 2), ("wstage", 0), ("wstage", 1),
                       ("wstage", 2), ("wstage", 3)))

        for _i in range(min(2, NT)):
            load_xres(_i)
        for i in range(NT):
            psn = "psA" if i % 2 == 0 else "psB"
            ps = PS[psn]
            keys = pk(psn, 0, C)
            for ci in range(NCC):
                lhsT = attn_fm[:, ci * T + i * 128: ci * T + (i + 1) * 128]
                for oh in range(NO2):
                    ez.op("tensor", "matmul", ps[:, oh * OW:(oh + 1) * OW],
                          lhsT,
                          wslot[1][:, ci * C + oh * OW: ci * C + (oh + 1) * OW],
                          start=(ci == 0), stop=False,
                          rd=(("lnout", ci), ("woT",)), wr=keys, sig=False)
            for oh in range(NO2):
                ez.op("tensor", "matmul", ps[:, oh * OW:(oh + 1) * OW],
                      A["ones_row_bf"][:], A["bo_row"][0:1, oh * OW:(oh + 1) * OW],
                      start=False, stop=True,
                      rd=(("ones_row_bf",), ("bo_row",)), wr=keys,
                      sig=(oh == NO2 - 1))
            xrk = ("xres", i % 2)
            xok = ("xout", i % 2)
            ez.op("vector", "tensor_tensor", f2tmp_v, ps[:, 0:C],
                  A["ga_rep"][:], ALU.mult,
                  rd=keys + (("ga_rep",),), wr=(("f2s",), ("wsgn",)))
            ez.op("gpsimd", "tensor_tensor", xout_v[i % 2], f2tmp_v,
                  xres_v[i % 2], ALU.add,
                  rd=(("f2s",), xrk), wr=(xok,))
            ez.dma("sync", out_hbm[i * 128:(i + 1) * 128, :], xout_v[i % 2],
                   "xm%d" % (i % 2), rd=(xok,), wr=(("xmid", i),))
            if i + 2 < NT:
                load_xres(i + 2)

        ez.barrier(dmas=("xm0", "xm1", "wqh"))

        ez.op("vector", "tensor_scalar", A["tapsf"][:], A["convw_fm"][:],
              A["wsc"][:, 1:2], None, ALU.mult,
              rd=(("convw_fm",), ("wsc", "sinv", "w1T")), wr=(("tapsf",),))

        q1_idx = [0]
        q2_idx = [0]

        def load_q1():
            idx = q1_idx[0]
            if idx >= NFC * NQ:
                return
            q = idx % NQ
            slot = idx % 2
            src = wq1h.rearrange("p (cc z) -> p cc z", z=HID) \
                [:, :, q * (HID // NQ):(q + 1) * (HID // NQ)]
            ez.dma("sync", wq1r[slot].rearrange("p (cc z) -> p cc z",
                                                z=HID // NQ),
                   src, "wr1a" if slot == 0 else "wr1b",
                   wr=(("wq1r", slot),))
            q1_idx[0] += 1

        def load_q2():
            idx = q2_idx[0]
            if idx >= NFC * NQ:
                return
            q = idx % NQ
            slot = idx % 2
            src = wq2h[:, q * QW2:(q + 1) * QW2]
            ez.dma("sync", wq2r[slot], src, "wr2a" if slot == 0 else "wr2b",
                   wr=(("wq2r", slot),))
            q2_idx[0] += 1

        load_q1()
        load_q1()
        load_q2()
        load_q2()

        def ln2_quant(j):
            xcb = xc_v[j % 2]
            xck = ("xc", j % 2)
            src = out_hbm[j * TC:(j + 1) * TC, :] \
                .rearrange("(j2 p) c -> p j2 c", p=128)
            fstk = tuple(("fst", t, t2) for t in ("sum", "sq", "amax")
                         for t2 in range(TT)) + \
                tuple(("fst", t) for t in ("m", "v0", "m2", "var", "sd", "rs",
                                           "nb0", "nb", "s1inv", "s1"))
            ez.dma("sync", xcb.rearrange("p (j2 c) -> p j2 c", c=C), src,
                   "xc%d" % (j % 2),
                   rd=tuple(("xmid", jj) for jj in range(NT)),
                   wr=(xck,) + fstk)
            fst = A["fstats"]
            for t2 in range(TT):
                xt = xcb[:, t2 * C:(t2 + 1) * C]
                ez.op("vector", "reduce_sum", fst[:, t2:t2 + 1], xt, axis=AX.X,
                      rd=(xck,), wr=(("fst", "sum", t2),))
                ez.op("scalar", "activation", sq_out_bf, xt, AF.Square,
                      accum_out=fst[:, TT + t2:TT + t2 + 1],
                      rd=(xck,),
                      wr=(("fst", "sq", t2), ("scrb0",)) +
                         tuple(("run", g) for g in range(NG)))
            sm_ = fst[:, 0:TT]; sq_ = fst[:, TT:2 * TT]; ax_ = fst[:, 2 * TT:3 * TT]
            kk = tuple(("fst", t, t2) for t in ("sum", "sq") for t2 in range(TT))
            ez.op("vector", "tensor_scalar", sm_, sm_, inv_C, None, ALU.mult,
                  rd=kk, wr=(("fst", "m"),))
            ez.op("vector", "tensor_scalar", sq_, sq_, inv_C, None, ALU.mult,
                  rd=kk + (("fst", "m"),), wr=(("fst", "v0"),))
            ez.op("vector", "tensor_tensor", ax_, sm_, sm_, ALU.mult,
                  rd=(("fst", "m"),), wr=(("fst", "m2"),))
            ez.op("vector", "tensor_tensor", sq_, sq_, ax_, ALU.subtract,
                  rd=(("fst", "v0"), ("fst", "m2")), wr=(("fst", "var"),))
            ez.op("scalar", "activation", sq_, sq_, AF.Sqrt,
                  bias=A["epsv"][:, 0:1],
                  rd=(("fst", "var"), ("epsv",)), wr=(("fst", "sd"),))
            ez.op("vector", "reciprocal", sq_, sq_,
                  rd=(("fst", "sd"),), wr=(("fst", "rs"),))
            ez.op("vector", "tensor_tensor", ax_, sm_, sq_, ALU.mult,
                  rd=(("fst", "m"), ("fst", "rs")), wr=(("fst", "nb0"),))
            ez.op("vector", "tensor_scalar", ax_, ax_, -1.0, None, ALU.mult,
                  rd=(("fst", "nb0"),), wr=(("fst", "nb"),))
            for t2 in range(TT):
                ez.op("scalar", "activation", xn2c[:, t2 * C:(t2 + 1) * C],
                      xcb[:, t2 * C:(t2 + 1) * C], AF.Identity,
                      scale=sq_[:, t2:t2 + 1], bias=ax_[:, t2:t2 + 1],
                      rd=(xck, ("fst", "rs"), ("fst", "nb")),
                      wr=(("xn2c", t2, 0), ("xn2c", t2, 3)))
                ez.op("vector", "tensor_tensor", xn2c[:, t2 * C:(t2 + 1) * C],
                      xn2c[:, t2 * C:(t2 + 1) * C], A["cb1_rep"][:], ALU.mult,
                      rd=(("xn2c", t2, 0), ("cb1_rep",)), wr=(("xn2c", t2, 1),))
                ez.op("vector", "tensor_tensor", xn2c[:, t2 * C:(t2 + 1) * C],
                      xn2c[:, t2 * C:(t2 + 1) * C], A["sb_rep"][:], ALU.add,
                      rd=(("xn2c", t2, 1), ("sb_rep",)), wr=(("xn2c", t2, 2),))
                ez.op("vector", "tensor_reduce",
                      fst[:, 3 * TT + t2:3 * TT + t2 + 1],
                      xn2c[:, t2 * C:(t2 + 1) * C], axis=AX.X, op=ALU.max,
                      apply_absolute_value=True,
                      rd=(("xn2c", t2, 2),), wr=(("fst", "amax", t2),))
            am = fst[:, 3 * TT:4 * TT]
            ez.op("vector", "tensor_scalar", am, am, 1e-5, 1.0 / 127.0,
                  ALU.max, ALU.mult,
                  rd=tuple(("fst", "amax", t2) for t2 in range(TT)),
                  wr=(("fst", "s1inv"),))
            s1 = fst[:, 4 * TT:5 * TT]
            ez.op("vector", "reciprocal", s1, am,
                  rd=(("fst", "s1inv"),), wr=(("fst", "s1"),))
            for t2 in range(TT):
                xsl = xn2c[:, t2 * C:(t2 + 1) * C]
                ez.op("vector", "tensor_scalar", xsl, xsl, s1[:, t2:t2 + 1],
                      M32, ALU.mult, ALU.add,
                      rd=(("xn2c", t2, 2), ("fst", "s1")), wr=(("xn2c", t2, 3),))
                ez.op("scalar", "activation", lnA, xsl, AF.Identity,
                      bias=A["nm32c"][:, 0:1],
                      rd=(("xn2c", t2, 3), ("nm32c",)), wr=(("lnA", 0),))
                ez.op("scalar", "activation", xsl, lnA, AF.Identity,
                      scale=A["hsel"][:, 0:1],
                      rd=(("lnA", 0), ("hsel",)), wr=(("xn2c", t2, 4),))
                ez.op("scalar", "activation", lnA, lnA, AF.Identity,
                      bias=A["m52c"][:, 0:1],
                      rd=(("lnA", 0), ("m52c",)), wr=(("lnA", 1),))
                ez.op("vector", "scalar_tensor_tensor", xsl, lnA, -M52, xsl,
                      ALU.add, ALU.add,
                      rd=(("lnA", 1), ("xn2c", t2, 4)), wr=(("xn2c", t2, 5),))
                ez.op("vector", "tensor_scalar", hqs_tm[:, t2 * C:(t2 + 1) * C],
                      xsl, am[:, t2:t2 + 1], None, ALU.mult,
                      rd=(("xn2c", t2, 5), ("fst", "s1inv")),
                      wr=(("hqs_tm", t2),))
            hb = hqs_v[j % 2]
            hbn = "hqs%d" % (j % 2)
            for t2 in range(TT):
                for g0 in range(0, NCC, 4):
                    bank = (g0 // 4) % 2
                    ng = min(4, NCC - g0)
                    keys2 = pk("psB", bank * 512, ng * 128)
                    for gi in range(ng):
                        cc = g0 + gi
                        ez.op("tensor", "matmul",
                              psB[:, bank * 512 + gi * 128:
                                  bank * 512 + (gi + 1) * 128],
                              hqs_tm[:, t2 * C + cc * 128:
                                     t2 * C + (cc + 1) * 128],
                              A["ident_bf"][:], start=True, stop=True,
                              rd=(("hqs_tm", t2), ("ident_bf",)), wr=keys2,
                              sig=(gi == ng - 1))
                    for gi in range(ng):
                        cc = g0 + gi
                        ez.op("scalar", "activation",
                              hb[:, cc * HW_ + 1 + t2 * 128:
                                 cc * HW_ + 1 + (t2 + 1) * 128],
                              psB[:, bank * 512 + gi * 128:
                                  bank * 512 + (gi + 1) * 128], AF.Identity,
                              rd=keys2, wr=((hbn, "mid", cc, t2),))

        def hqs_edges(j):
            hb_prev = hqs_v[(j - 1) % 2]
            hbn_prev = "hqs%d" % ((j - 1) % 2)
            hb_cur = hqs_v[j % 2]
            hbn_cur = "hqs%d" % (j % 2)
            if j < NFC:
                dst = hb_cur.rearrange("p (cc w) -> p cc w", w=HW_)[:, :, 0:1]
                if j == 0:
                    ez.op("vector", "memset", dst, 0.0, wr=((hbn_cur, "edgeL"),))
                else:
                    ez.op("vector", "tensor_copy", dst,
                          hb_prev.rearrange("p (cc w) -> p cc w", w=HW_)
                          [:, :, HW_ - 2:HW_ - 1],
                          rd=tuple((hbn_prev, "mid", cc, TT - 1)
                                   for cc in range(NCC)),
                          wr=((hbn_cur, "edgeL"),))
            if j >= 1:
                dst = hb_prev.rearrange("p (cc w) -> p cc w", w=HW_) \
                    [:, :, HW_ - 1:HW_]
                if j == NFC:
                    ez.op("vector", "memset", dst, 0.0, wr=((hbn_prev, "edgeR"),))
                else:
                    ez.op("vector", "tensor_copy", dst,
                          hb_cur.rearrange("p (cc w) -> p cc w", w=HW_)
                          [:, :, 1:2],
                          rd=tuple((hbn_cur, "mid", cc, 0) for cc in range(NCC)),
                          wr=((hbn_prev, "edgeR"),))

        def ffn1_conv(j1, j3):
            if j1 is not None:
                hb = hqs_v[j1 % 2]
                hbn = "hqs%d" % (j1 % 2)
                h1pad_1 = h1pad_v[j1 % 2]
                hkeys = tuple((hbn, "mid", cc, t2) for cc in range(NCC)
                              for t2 in range(TT)) + \
                    ((hbn, "edgeL"), (hbn, "edgeR"))
            if j3 is not None:
                h1pad_3 = h1pad_v[j3 % 2]
            for hc in range(NHC):
                if j1 is not None:
                    qsl = ((j1 * NQ) + hc // QH) % 2
                    bank = hc % 2
                    keys = pk("psA", bank * 512, HW_)
                    for cc in range(NCC):
                        ez.op("tensor", "matmul",
                              psA[:, bank * 512: bank * 512 + HW_],
                              wq1r[qsl][:, cc * (HID // NQ) +
                                        (hc % QH) * 128:
                                        cc * (HID // NQ) +
                                        (hc % QH + 1) * 128],
                              hb[:, cc * HW_:(cc + 1) * HW_],
                              start=(cc == 0), stop=(cc == NCC - 1),
                              rd=(("wq1r", qsl),) + hkeys, wr=keys,
                              sig=(cc == NCC - 1))
                    ez.op("scalar", "activation",
                          h1pad_1[:, hc * HW_:(hc + 1) * HW_],
                          psA[:, bank * 512: bank * 512 + HW_], AF.Identity,
                          rd=keys, wr=(("h1pad", j1 % 2, hc),))
                    if (hc + 1) % QH == 0:
                        load_q1()
                if j3 is not None:
                    g = hc // GRP
                    left = h1pad_3[:, hc * HW_: hc * HW_ + TC]
                    dstc = convB[:, hc * TC:(hc + 1) * TC]
                    cclaim = (("hq2", g, "m"), ("hq2", g, "r"),
                              ("convB", hc, "s"))
                    ez.op("scalar", "activation", dstc, left, AF.Identity,
                          scale=A["tapsf"][:, hc * 3:hc * 3 + 1],
                          bias=A["convb_fm"][:, hc:hc + 1],
                          rd=(("h1pad", j3 % 2, hc), ("tapsf",), ("convb_fm",)),
                          wr=(("convB", hc, 0),) + cclaim)
            if j3 is not None:
                for hc in range(NHC):
                    mid = h1pad_3[:, hc * HW_ + 1: hc * HW_ + 1 + TC]
                    right = h1pad_3[:, hc * HW_ + 2: hc * HW_ + 2 + TC]
                    dstc = convB[:, hc * TC:(hc + 1) * TC]
                    ez.op("vector", "scalar_tensor_tensor", dstc, mid,
                          A["tapsf"][:, hc * 3 + 1:hc * 3 + 2], dstc,
                          ALU.mult, ALU.add,
                          rd=(("h1pad", j3 % 2, hc), ("convB", hc, 0),
                              ("tapsf",)),
                          wr=(("convB", hc, 1),))
                    ez.op("vector", "scalar_tensor_tensor", dstc, right,
                          A["tapsf"][:, hc * 3 + 2:hc * 3 + 3], dstc,
                          ALU.mult, ALU.add,
                          rd=(("h1pad", j3 % 2, hc), ("convB", hc, 1),
                              ("tapsf",)),
                          wr=(("convB", hc, 2),))
                for hc in range(NHC):
                    g = hc // GRP
                    dstc = convB[:, hc * TC:(hc + 1) * TC]
                    ez.op("scalar", "activation", dstc, dstc, AF.Silu,
                          rd=(("convB", hc, 2),), wr=(("convB", hc, "s"),))
                    if hc % GRP == GRP - 1:
                        ez.op("vector", "tensor_reduce",
                              runmax_s[:, g * TC:(g + 1) * TC],
                              convB[:, g * GRP * TC:(g + 1) * GRP * TC]
                              .rearrange("p (hc t) -> p t hc", t=TC),
                              axis=AX.X, op=ALU.max, apply_absolute_value=True,
                              rd=tuple(("convB", h2, "s")
                                       for h2 in range(g * GRP, (g + 1) * GRP)),
                              wr=(("run", g), ("runs",)))

        def ffn_out(j):
            run = runmax_s
            for g in range(1, NG):
                ez.op("vector", "tensor_tensor", run[:, 0:TC], run[:, 0:TC],
                      run[:, g * TC:(g + 1) * TC], ALU.max,
                      rd=(("run", 0), ("run", g)),
                      wr=(("run", 0),))
            for t2 in range(TT):
                keys2 = pk("psB", 2 * 512, 128)
                s2claim = (("s2c", "s2inv"), ("s2c", "s2"),
                           ("s2c", "s2w")) if t2 == 0 else ()
                ez.op("tensor", "matmul", psB[:, 2 * 512:2 * 512 + 128],
                      run[:, t2 * 128:(t2 + 1) * 128], A["ident_f32"][:],
                      start=True, stop=True,
                      rd=(("run", 0), ("ident_f32",)), wr=keys2, sig=True)
                ez.op("vector", "tensor_reduce", A["s2c"][:, t2:t2 + 1],
                      psB[:, 2 * 512:2 * 512 + 128], axis=AX.X, op=ALU.max,
                      rd=keys2, wr=(("s2c", "max", t2),) + s2claim)
            ez.op("vector", "tensor_scalar", A["s2c"][:, 0:TT],
                  A["s2c"][:, 0:TT], 1e-5, 1.0 / 127.0, ALU.max, ALU.mult,
                  rd=tuple(("s2c", "max", t2) for t2 in range(TT)),
                  wr=(("s2c", "s2inv"),))
            ez.op("vector", "reciprocal", A["s2c"][:, TT:2 * TT],
                  A["s2c"][:, 0:TT], rd=(("s2c", "s2inv"),),
                  wr=(("s2c", "s2"),))
            for t2 in range(TT):
                keys2 = pk("psB", 2 * 512 + 128 + t2 * 128, 128)
                ez.op("tensor", "matmul",
                      psB[0:1, 2 * 512 + 128 + t2 * 128:
                          2 * 512 + 128 + (t2 + 1) * 128],
                      A["s2c"][:, TT + t2:TT + t2 + 1], A["ident_f32"][:],
                      start=True, stop=True,
                      rd=(("s2c", "s2"), ("ident_f32",)), wr=keys2, sig=True)
            rowkeys = pk("psB", 2 * 512 + 128, TT * 128)
            ez.op("vector", "tensor_copy", A["rowtc"][0:1, :],
                  psB[0:1, 2 * 512 + 128: 2 * 512 + 128 + TC],
                  rd=rowkeys, wr=(("rowtc",),))
            keys3 = pk("psB", 3 * 512, TC)
            for tqr in range((TC + 511) // 512):
                n = min(512, TC - tqr * 512)
                ez.op("tensor", "matmul",
                      psB[:, 3 * 512 + tqr * 512: 3 * 512 + tqr * 512 + n],
                      A["ones_row_f32"][:],
                      A["rowtc"][0:1, tqr * 512:tqr * 512 + n],
                      start=True, stop=True,
                      rd=(("rowtc",), ("ones_row_f32",)), wr=keys3,
                      sig=(tqr == (TC + 511) // 512 - 1))
            ez.op("vector", "tensor_copy", A["s2rep"][:],
                  psB[:, 3 * 512: 3 * 512 + TC],
                  rd=keys3, wr=(("s2rep",),))
            ez.op("vector", "tensor_scalar", A["s2c"][:, 2 * TT:3 * TT],
                  A["s2c"][:, 0:TT], A["wsc"][:, 3:4], None, ALU.mult,
                  rd=(("s2c", "s2inv"), ("wsc", "sinv", "w2T")),
                  wr=(("s2c", "s2w"),))
            f2keys = pk("psA", 0, TT * NO2 * 512)
            h1stage = h1pad_v[j % 2]
            for g in range(NG):
                gsl = convB[:, g * GRP * TC:(g + 1) * GRP * TC]
                hsl = hq2[:, g * GRP * TC:(g + 1) * GRP * TC]
                bsl = h1stage[:, 0:GRP * TC]
                silk = tuple(("convB", h2, "s")
                             for h2 in range(g * GRP, (g + 1) * GRP))
                ez.op("vector", "tensor_tensor",
                      hsl.rearrange("p (hc t) -> p hc t", t=TC),
                      gsl.rearrange("p (hc t) -> p hc t", t=TC),
                      A["s2rep"][:].unsqueeze(1).broadcast_to([128, GRP, TC]),
                      ALU.mult,
                      rd=silk + (("s2rep",), ("run", g)),
                      wr=(("hq2", g, "m"), ("hq2", g, "r")) + silk)
                ez.op("vector", "tensor_scalar", hsl, hsl,
                      M32, M32, ALU.add, ALU.subtract,
                      rd=(("hq2", g, "m"),), wr=(("hq2", g, "a"),))
                ez.op("vector", "tensor_scalar", bsl, hsl,
                      M52, M52, ALU.add, ALU.subtract,
                      rd=(("hq2", g, "a"),),
                      wr=(("hq2", g, "b"),) +
                         tuple(("h1pad", j % 2, h2) for h2 in range(NHC)))
                ez.op("vector", "scalar_tensor_tensor", hsl, hsl,
                      A["hsel"][:, 0:1], bsl, ALU.mult, ALU.add,
                      rd=(("hq2", g, "a"), ("hq2", g, "b"), ("hsel",)),
                      wr=(("hq2", g, "r"),))
                qsl = ((j * NQ) + (g * GRP) // QH) % 2
                for hc in range(g * GRP, (g + 1) * GRP):
                    for t2 in range(TT):
                        for oh in range(NO2):
                            slot = (t2 * NO2 + oh) * 512
                            ez.op("tensor", "matmul",
                                  psA[:, slot:slot + OW],
                                  hq2[:, hc * TC + t2 * 128:
                                      hc * TC + (t2 + 1) * 128],
                                  wq2r[qsl][:, (hc % QH) * C + oh * OW:
                                            (hc % QH) * C + (oh + 1) * OW],
                                  start=(hc == 0), stop=(hc == NHC - 1),
                                  rd=(("hq2", g, "r"), ("wq2r", qsl)),
                                  wr=f2keys,
                                  sig=(hc % GRP == GRP - 1 and t2 == TT - 1
                                       and oh == NO2 - 1))
                for _ in range(((g + 1) * GRP) // QH - (g * GRP) // QH):
                    load_q2()
            for t2 in range(TT):
                psrc = psA[:, t2 * NO2 * 512:(t2 + 1) * NO2 * 512] \
                    .rearrange("p (n w) -> p n w", w=512)[:, :, 0:OW]
                ez.op("scalar", "activation",
                      f2tmp_v.rearrange("p (n w) -> p n w", w=OW), psrc,
                      AF.Identity,
                      scale=A["s2c"][:, 2 * TT + t2:2 * TT + t2 + 1],
                      rd=f2keys + (("s2c", "s2w"),),
                      wr=(("f2t", 0), ("f2t", 1)))
                fo_i = (j * TT + t2) % 2
                fo = fout_v[fo_i]
                fok = ("fout", fo_i)
                ez.op("vector", "tensor_tensor", fo, f2tmp_v,
                      A["gb_rep"][:], ALU.mult,
                      rd=(("f2t", 0), ("gb_rep",)), wr=(fok, ("f2t", 1)))
                ez.dma("gpsimd",
                       out_hbm[j * TC + t2 * 128: j * TC + (t2 + 1) * 128, :],
                       fo, "so%d" % fo_i,
                       rd=(fok,), wr=(("outf", j, t2), ("xmid", j * TT + t2)),
                       accum_op=ALU.add)

        for j in range(NFC + 3):
            j1 = j - 2 if 2 <= j < NFC + 2 else None
            j3 = j - 3 if j >= 3 else None
            if j1 is not None or j3 is not None:
                ffn1_conv(j1, j3)
            if j < NFC:
                ln2_quant(j)
            if j <= NFC:
                hqs_edges(j)
            if j3 is not None:
                ffn_out(j3)

        for _sn in ("so0", "so1", "xm0", "xm1", "wqh"):
            nc.sync.wait_ge(ez.dsem[_sn][0], ez.dsem[_sn][1])

    return nc


def _host_inputs(cfg, inputs, core):
    f32 = np.float32
    C, NCC, NHC = cfg.C, cfg.NCC, cfg.NHC
    ct = np.ascontiguousarray
    p = "L0_"
    m = {
        "x": ct(inputs[p + "attn_x"][core].astype(f32)),
        "wqT": ct(inputs[p + "attn_Wq"].astype(f32).T),
        "wkT": ct(inputs[p + "attn_Wk"].astype(f32).T),
        "wvT": ct(inputs[p + "attn_Wv"].astype(f32).T),
        "woT": ct(inputs[p + "attn_Wo"].astype(f32).T),
        "w1T": ct(inputs[p + "ffn_w1"].astype(f32).T),
        "w2T": ct(inputs[p + "ffn_w2"].astype(f32).T),
        "wmod": ct(inputs[p + "wmod"].astype(f32)),
        "c_fm": ct(inputs[p + "c"][0 if inputs[p + "c"].shape[0] == 1 else core]
                   .astype(f32).reshape(NCC, 128).T),
        "bq_row": ct(inputs[p + "attn_bq"].astype(f32).reshape(1, C)),
        "bv_row": ct(inputs[p + "attn_bv"].astype(f32).reshape(1, C)),
        "bo_row": ct(inputs[p + "attn_bo"].astype(f32).reshape(1, C)),
        "lng_fm": ct(inputs[p + "attn_ln_g"].astype(f32).reshape(NCC, 128).T),
        "lnb_fm": ct(inputs[p + "attn_ln_b"].astype(f32).reshape(NCC, 128).T),
        "convw_fm": ct(inputs[p + "ffn_conv_w"].astype(f32)
                       .reshape(NHC, 128, 3).transpose(1, 0, 2)
                       .reshape(128, 3 * NHC)),
        "convb_fm": ct(inputs[p + "ffn_conv_b"].astype(f32)
                       .reshape(NHC, 128).T),
        "ident_bf": np.eye(128, dtype=ml_dtypes.bfloat16),
        "ident_f32": np.eye(128, dtype=f32),
        "ones_row_bf": np.ones((1, 128), dtype=ml_dtypes.bfloat16),
        "ones_row_f32": np.ones((1, 128), dtype=f32),
        "onesmat_bf": np.ones((128, 128), dtype=ml_dtypes.bfloat16),
    }
    return m


_CACHED = {}


def kernel(**inputs):
    cfg = CFG()
    key = "nc"
    if key not in _CACHED:
        _CACHED[key] = build_nc(cfg)
    nc = _CACHED[key]
    in_maps = [_host_inputs(cfg, inputs, core) for core in range(8)]
    res = run_bass_kernel_spmd(nc, in_maps, core_ids=list(range(8)))
    out = np.stack([np.asarray(res.results[core]["out"])
                    for core in range(8)], axis=0)
    return out.astype(np.float32)


# version 30
# speedup vs baseline: 1.1097x; 1.0092x over previous
import contextlib
import numpy as np
import ml_dtypes
import concourse.bass as bass
import concourse.mybir as mybir
from concourse.bass_utils import run_bass_kernel_spmd

F32 = mybir.dt.float32
BF16 = mybir.dt.bfloat16
FP8 = mybir.dt.float8e4
AF = mybir.ActivationFunctionType
ALU = mybir.AluOpType
AX = mybir.AxisListType

M32 = float(np.float32(1.5 * 2 ** 23))
M52 = float(np.float32(1.5 * 2 ** 52))
EPS = 1e-5


class CFG:
    def __init__(self, T=2048, C=1024, HID=4096, D=64, TC=256):
        self.T = T
        self.C = C
        self.HID = HID
        self.D = D
        self.H = C // D
        self.TC = TC
        self.NT = T // 128
        self.NCC = C // 128
        self.NHC = HID // 128
        self.NFC = T // TC


class EZ:
    CENG = ("tensor", "vector", "scalar", "gpsimd")

    def __init__(self, nc, es):
        self.nc = nc
        self.osem = {}
        self.ocnt = {}
        for e in self.CENG:
            self.osem[e] = es.enter_context(nc.semaphore("o_" + e))
            self.ocnt[e] = 0
        self.dsem = {}
        self.wm = {}
        self.wr = {}
        self.rd = {}
        self.pend = {e: [] for e in self.CENG + ("sync",)}
        self.pend_rd = {e: [] for e in self.CENG + ("sync",)}
        self.es = es

    def mksem(self, name):
        self.dsem[name] = [self.es.enter_context(self.nc.semaphore(name)), 0]

    def _emit_wait(self, engine, evt):
        if evt is None:
            return
        if evt[0] == "e":
            _, peng, val = evt
            key = (engine, "o", peng)
            if self.wm.get(key, 0) >= val:
                return
            getattr(self.nc, engine).wait_ge(self.osem[peng], val)
            self.wm[key] = val
        else:
            _, sname, val = evt
            key = (engine, "d", sname)
            if self.wm.get(key, 0) >= val:
                return
            getattr(self.nc, engine).wait_ge(self.dsem[sname][0], val)
            self.wm[key] = val

    def _pre(self, engine, rd, wr):
        need = {}

        def add(evt):
            if evt is None:
                return
            key = ("o", evt[1]) if evt[0] == "e" else ("d", evt[1])
            if need.get(key, 0) < evt[2]:
                need[key] = evt[2]

        for k in rd:
            add(self.wr.get(k))
        for k in wr:
            add(self.wr.get(k))
            for evt in self.rd.get(k, ()):
                add(evt)
        for key, val in need.items():
            self._emit_wait(engine, ("e", key[1], val) if key[0] == "o"
                            else ("d", key[1], val))

    def _post(self, engine, evt, rd, wr, sig):
        if sig:
            for k in self.pend[engine]:
                self.wr[k] = evt
                self.rd[k] = []
            self.pend[engine] = []
            for k in wr:
                self.wr[k] = evt
                self.rd[k] = []
            for k in self.pend_rd[engine]:
                self.rd.setdefault(k, []).append(evt)
            self.pend_rd[engine] = []
            for k in rd:
                self.rd.setdefault(k, []).append(evt)
        else:
            self.pend[engine].extend(wr)
            self.pend_rd[engine].extend(rd)

    def op(self, engine, fname, *args, rd=(), wr=(), sig=True, **kw):
        self._pre(engine, rd, wr)
        instr = getattr(getattr(self.nc, engine), fname)(*args, **kw)
        evt = None
        if sig:
            self.ocnt[engine] += 1
            instr.then_inc(self.osem[engine], 1)
            evt = ("e", engine, self.ocnt[engine])
        self._post(engine, evt, rd, wr, sig)
        return instr

    def dma(self, engine, out, in_, sname, rd=(), wr=(), transpose=False, **dkw):
        self._pre(engine, rd, wr)
        kw = dict(dkw)
        if transpose:
            kw["transpose"] = True
        instr = getattr(self.nc, engine).dma_start(out=out, in_=in_, **kw)
        s = self.dsem[sname]
        s[1] += 16
        instr.then_inc(s[0], 16)
        self._post(engine, ("d", sname, s[1]), rd, wr, True)
        return instr

    def seal(self, sname):
        tot = self.dsem[sname][1]
        for k, evt in self.wr.items():
            if evt is not None and evt[0] == "d" and evt[1] == sname:
                self.wr[k] = ("d", sname, tot)

    def barrier(self, dmas=()):
        for e in ("tensor", "scalar", "gpsimd"):
            if self.ocnt[e] > 0:
                self._emit_wait("vector", ("e", e, self.ocnt[e]))
        for dn in dmas:
            if self.dsem[dn][1] > 0:
                self._emit_wait("vector", ("d", dn, self.dsem[dn][1]))
        self.ocnt["vector"] += 1
        self.nc.vector.engine_nop().then_inc(self.osem["vector"], 1)
        evt = ("e", "vector", self.ocnt["vector"])
        for e in ("tensor", "scalar", "gpsimd", "sync"):
            self._emit_wait(e, evt)


def rnd32(x):
    return (x + 31) // 32 * 32


def build_nc(cfg: CFG):
    nc = bass.Bass("TRN2", target_bir_lowering=False)
    T, C, HID, D, H = cfg.T, cfg.C, cfg.HID, cfg.D, cfg.H
    NT, NCC, NHC, TC, NFC = cfg.NT, cfg.NCC, cfg.NHC, cfg.TC, cfg.NFC
    HP = H // 2
    TT = TC // 128
    OW = min(512, C)
    NO2 = C // OW
    TQW = min(512, T)
    NT4 = T // TQW
    HW_ = TC + 2
    inv_C = 1.0 / C
    NJ = 6 * NCC
    THALF = min(1024, T)
    NKH = T // THALF
    KHT = THALF // 128
    PW1 = min(1024, HID)
    PW2 = min(1024, C)
    NQ = max(1, NHC // 8)
    QH = NHC // NQ
    QW1 = NCC * HID // NQ
    QW2 = NHC * C // NQ
    GRP = 8 if NHC >= 8 else NHC
    NG = NHC // GRP

    d = {}

    def din(name, shape, dtype=F32):
        d[name] = nc.dram_tensor(name, shape, dtype, kind="ExternalInput").ap()

    din("x", [T, C])
    din("wqT", [C, C]); din("wkT", [C, C]); din("wvT", [C, C]); din("woT", [C, C])
    din("w1T", [C, HID]); din("w2T", [HID, C])
    din("wmod", [C, 6 * C])
    din("c_fm", [128, NCC])
    din("bq_row", [1, C]); din("bv_row", [1, C]); din("bo_row", [1, C])
    din("lng_fm", [128, NCC]); din("lnb_fm", [128, NCC])
    din("convw_fm", [128, 3 * NHC]); din("convb_fm", [128, NHC])
    din("ident_bf", [128, 128], BF16)
    din("ident_f32", [128, 128])
    din("ones_row_bf", [1, 128], BF16)
    din("ones_row_f32", [1, 128])
    din("onesmat_bf", [128, 128], BF16)
    out_hbm = nc.dram_tensor("out", [T, C], F32, kind="ExternalOutput").ap()
    wq1h = nc.dram_tensor("wq1h", [128, NCC * HID], FP8,
                          kind="ExternalOutput").ap()
    wq2h = nc.dram_tensor("wq2h", [128, NHC * C], FP8,
                          kind="ExternalOutput").ap()

    es = contextlib.ExitStack()
    ez = EZ(nc, es)
    A = {}

    def sb(name, shape, dtype):
        A[name] = es.enter_context(nc.sbuf_tensor("s_" + name, shape, dtype))
        return A[name]

    NW1P = NCC * (HID // PW1)
    NW2P = NHC * (C // PW2)
    for nm, shp, dt_ in [
        ("ident_bf", [128, 128], BF16), ("ident_f32", [128, 128], F32),
        ("ones_row_bf", [1, 128], BF16), ("ones_row_f32", [1, 128], F32),
        ("onesmat_bf", [128, 128], BF16),
        ("c_fm", [128, NCC], F32), ("c_sil", [128, NCC], BF16),
        ("bq_row", [1, C], BF16), ("bv_row", [1, C], BF16),
        ("bo_row", [1, C], BF16),
        ("lng_fm", [128, NCC], F32), ("lnb_fm", [128, NCC], F32),
        ("convw_fm", [128, 3 * NHC], F32), ("convb_fm", [128, NHC], F32),
        ("stats", [128, 3 * NT], F32), ("sgn32", [128, NCC], F32),
        ("mod_col", [128, 6 * NCC], F32), ("modpart", [128, 6 * NCC], F32),
        ("ca1_col", [128, NCC], F32), ("sa_col", [128, NCC], F32),
        ("ga_col", [128, NCC], F32), ("gb_col", [128, NCC], F32),
        ("cb1_rep", [128, C], F32), ("sb_rep", [128, C], F32),
        ("ga_rep", [128, C], BF16), ("gb_rep", [128, C], BF16),
        ("kstats", [128, 2 * NCC], F32),
        ("epsv", [128, 1], F32), ("qstat", [128, 2 * H + 1], F32),
        ("state_sb", [128, HP * D], F32), ("state_bf", [128, HP * D], BF16),
        ("fstats", [128, 5 * TT], F32), ("s2c", [128, 3 * TT], F32),
        ("rowtc", [1, TC], F32), ("wsc", [128, 8], F32),
        ("wsums", [128, NW1P + NW2P + 8], F32), ("wrow", [1, 8], F32),
        ("tapsf", [128, 3 * NHC], F32),
        ("s2rep", [128, TC], BF16),
        ("hsel", [128, 1], F32), ("nm32c", [128, 1], F32),
        ("m52c", [128, 1], F32),
    ]:
        sb(nm, shp, dt_)

    RB1E = max(16 * C, NT * C, NCC * T,
               TT * C + 2 * rnd32(NCC * HW_) + NHC * TC + 64)
    RB2E = NCC * T
    SLOT3 = max(NCC * C, rnd32(NHC * HW_))
    RB3E = 2 * SLOT3
    SCBE = max(2 * T, 2 * C)
    ABE = RB1E + RB2E + RB3E + SCBE
    arB = es.enter_context(nc.sbuf_tensor("arB", [128, ABE], BF16))

    RF1E = max(2 * T + 4 * C, 3 * TT * C + 2 * C, 3 * C,
               max(2 * T, 2 * C + 2 * PW1) + 2 * PW1)
    SCW = max(C, NG * TC, PW1)
    SCW2 = SCW
    SCFE = 3 * SCW
    AFE = RF1E + SCFE
    arF = es.enter_context(nc.sbuf_tensor("arF", [128, AFE], F32))

    APE = 2 * QW1 + 2 * QW2
    arP = es.enter_context(nc.sbuf_tensor("arP", [128, APE], FP8))

    def vB(off, n, rows=128):
        assert off + n <= ABE, (off, n, ABE)
        return arB[0:rows, off:off + n]

    def vF(off, n, rows=128):
        assert off + n <= AFE, (off, n, AFE)
        return arF[0:rows, off:off + n]

    def vP(off, n):
        assert off + n <= APE, (off, n, APE)
        return arP[:, off:off + n]

    O1 = 0
    xn4 = [vB(O1 + i * C, C) for i in range(4)]
    wmst_v = [vB(O1 + 4 * C, 6 * C), vB(O1 + 10 * C, 6 * C)]
    v_tm = vB(O1, NT * C)
    q_fm = vB(O1, NCC * T)
    hqs_tm = vB(O1, TT * C)
    _o = O1 + TT * C
    hqs_v = [vB(_o, NCC * HW_), vB(_o + rnd32(NCC * HW_), NCC * HW_)]
    _o += 2 * rnd32(NCC * HW_)
    convB = vB(_o, NHC * TC)
    hq2 = convB
    O2 = RB1E
    h_fm = vB(O2, NCC * T)
    attn_fm = vB(O2, NCC * T)
    O3 = RB1E + RB2E
    wslot = [vB(O3, NCC * C), vB(O3 + SLOT3, NCC * C)]
    h1pad_v = [vB(O3, NHC * HW_), vB(O3 + SLOT3, NHC * HW_)]
    OS = RB1E + RB2E + RB3E
    kexp_fm = vB(OS, T)
    kslice_v = [vB(OS + T, 128), vB(OS + T + 128, 128)]
    sqb_v = [vB(OS, T), vB(OS + T, T)]
    b_rep = vB(OS, T)
    ascr_v = vB(OS + T, T)
    qtb_s = vB(OS, 2 * C)
    a_rep = vB(OS + T, T)

    OF = 0
    WSB2 = max(2 * T, 2 * C + 2 * PW1)
    wstage_v = [vF(OF + 2 * C, PW1), vF(OF + 2 * C + PW1, PW1),
                vF(OF + WSB2, PW1), vF(OF + WSB2 + PW1, PW1)]
    xt_v = [vF(OF + i * C, C) for i in range(4)]
    qe_v = [vF(OF, C), vF(OF + C, C)]
    m_rep = vF(OF, T)
    v_rep = vF(OF + T, T)
    xres_v = [vF(OF + 2 * T, C), vF(OF + 2 * T + C, C)]
    xout_v = [vF(OF + 2 * T + 2 * C, C), vF(OF + 2 * T + 3 * C, C)]
    xc_v = [vF(OF, TT * C), vF(OF + TT * C, TT * C)]
    xn2c = vF(OF + 2 * TT * C, TT * C)
    fout_v = [vF(OF + 3 * TT * C, C), vF(OF + 3 * TT * C + C, C)]
    OSF = OF + RF1E
    runmax_s = vF(OSF, NG * TC)
    f2tmp_v = vF(OSF + SCW, C)
    lnA = vF(OSF + 2 * SCW, C)
    rowtmp = vF(OSF + 2 * SCW, C, rows=1)[0:1, :]
    wqstg = vF(OSF + 2 * SCW, PW1)
    wsgn = vF(OSF + SCW, PW1)
    tern8 = None

    wq1r = [vP(0, QW1), vP(QW1, QW1)]
    wq2r = [vP(2 * QW1, QW2), vP(2 * QW1 + QW2, QW2)]
    tern8 = vP(0, PW1)

    psA = es.enter_context(nc.psum_tensor("psA", [128, 2048], F32))
    psB = es.enter_context(nc.psum_tensor("psB", [128, 2048], F32))
    PS = {"psA": psA, "psB": psB}

    def pk(psn, col0, width):
        b0 = col0 // 512
        b1 = (col0 + width - 1) // 512
        return tuple((psn, b) for b in range(b0, b1 + 1))

    for s in ["cst", "cstg", "xt0", "xt1", "xt2", "xt3", "wm0", "wm1", "wk", "wv",
              "wq", "wo", "wst0", "wst1", "wst2", "wst3", "xr0", "xr1", "xm0", "xm1",
              "so0", "so1", "xc0", "xc1", "wqh", "wr1a", "wr1b", "wr2a", "wr2b"]:
        ez.mksem(s)

    with es:
        es.enter_context(nc.Block())

        ez.op("vector", "memset", A["epsv"][:], EPS, wr=(("epsv",),))
        ez.op("vector", "memset", A["hsel"][:], 1.0, wr=(("hsel", 0),))
        ez.op("vector", "memset", A["nm32c"][:], -M32, wr=(("nm32c",),))
        ez.op("vector", "memset", A["m52c"][:], M52, wr=(("m52c",),))
        ez.op("vector", "tensor_scalar", A["hsel"][:], A["hsel"][:],
              M52, M52, ALU.add, ALU.subtract,
              rd=(("hsel", 0),), wr=(("hsel", 1),))
        ez.op("vector", "tensor_scalar", A["hsel"][:], A["hsel"][:],
              -1.0, 1.0, ALU.mult, ALU.add,
              rd=(("hsel", 1),), wr=(("hsel",),))

        for nm in ["ident_bf", "ident_f32", "ones_row_bf", "ones_row_f32",
                   "onesmat_bf", "c_fm", "lng_fm", "lnb_fm", "convw_fm",
                   "convb_fm"]:
            ez.dma("sync", A[nm][:], d[nm], "cst", wr=((nm,),))
        for nm in ["bq_row", "bv_row", "bo_row"]:
            ez.dma("gpsimd", A[nm][:], d[nm], "cstg", wr=((nm,),))
        ez.seal("cst")
        ez.seal("cstg")

        def load_w(nm, slot, sem, claims=()):
            ez.dma("gpsimd", wslot[slot].rearrange("p (j c) -> p j c", c=C),
                   d[nm].rearrange("(j p) c -> p j c", p=128), sem,
                   wr=((nm,),) + claims)

        load_w("wvT", 0, "wv")
        load_w("wkT", 1, "wk")

        def load_wmod(i):
            ez.dma("gpsimd", wmst_v[i % 2][:], d["wmod"][i * 128:(i + 1) * 128, :],
                   "wm%d" % (i % 2), wr=(("wmst", i % 2),))

        for _i in range(min(2, NCC)):
            load_wmod(_i)

        st = A["stats"]

        def load_xt(i):
            ez.dma("sync", xt_v[i % 4], d["x"][i * 128:(i + 1) * 128, :],
                   "xt%d" % (i % 4), wr=(("xt", i % 4),))

        for _i in range(min(4, NT)):
            load_xt(_i)
        sq_out_bf = vB(OS, C)
        for g0 in range(0, NT, 4):
            ng = min(4, NT - g0)
            for gi in range(ng):
                i = g0 + gi
                xt = xt_v[i % 4]
                xk = ("xt", i % 4)
                ez.op("vector", "reduce_sum", st[:, i:i + 1], xt, axis=AX.X,
                      rd=(xk,), wr=(("st", "sum", i),))
                ez.op("scalar", "activation", sq_out_bf, xt, AF.Square,
                      accum_out=st[:, NT + i:NT + i + 1], rd=(xk,),
                      wr=(("st", "sq", i), ("scrb0",)))
            sm = st[:, g0:g0 + ng]
            sq = st[:, NT + g0:NT + g0 + ng]
            c2 = st[:, 2 * NT + g0:2 * NT + g0 + ng]
            sumk = tuple(("st", "sum", g0 + z) for z in range(ng))
            sqk = tuple(("st", "sq", g0 + z) for z in range(ng))
            ez.op("vector", "tensor_scalar", sm, sm, inv_C, None, ALU.mult,
                  rd=sumk, wr=(("st", "m", g0),))
            ez.op("vector", "tensor_scalar", sq, sq, inv_C, None, ALU.mult,
                  rd=sqk, wr=(("st", "v0", g0),))
            ez.op("vector", "tensor_tensor", c2, sm, sm, ALU.mult,
                  rd=(("st", "m", g0),), wr=(("st", "m2", g0),))
            ez.op("vector", "tensor_tensor", sq, sq, c2, ALU.subtract,
                  rd=(("st", "v0", g0), ("st", "m2", g0)),
                  wr=(("st", "var", g0),))
            ez.op("scalar", "activation", sq, sq, AF.Sqrt,
                  bias=A["epsv"][:, 0:1],
                  rd=(("st", "var", g0), ("epsv",)), wr=(("st", "rs", g0),))
            ez.op("vector", "reciprocal", sq, sq,
                  rd=(("st", "rs", g0),), wr=(("st", "rs2", g0),))
            ez.op("vector", "tensor_tensor", c2, sm, sq, ALU.mult,
                  rd=(("st", "m", g0), ("st", "rs2", g0)),
                  wr=(("st", "nb0", g0),))
            ez.op("vector", "tensor_scalar", c2, c2, -1.0, None, ALU.mult,
                  rd=(("st", "nb0", g0),), wr=(("st", "nb", g0),))
            for gi in range(ng):
                i = g0 + gi
                ez.op("scalar", "activation", xn4[i % 4], xt_v[i % 4],
                      AF.Identity, scale=st[:, NT + i:NT + i + 1],
                      bias=st[:, 2 * NT + i:2 * NT + i + 1],
                      rd=(("xt", i % 4), ("st", "rs2", g0), ("st", "nb", g0)),
                      wr=(("xn4", i % 4),))
                if i + 4 < NT:
                    load_xt(i + 4)
            for cc in range(NCC):
                bank = cc % 4
                keys = pk("psA", bank * 512, ng * 128)
                for gi in range(ng):
                    ez.op("tensor", "matmul",
                          psA[:, bank * 512 + gi * 128:
                              bank * 512 + (gi + 1) * 128],
                          xn4[(g0 + gi) % 4][:, cc * 128:(cc + 1) * 128],
                          A["ident_bf"][:], start=True, stop=True,
                          rd=(("xn4", (g0 + gi) % 4), ("ident_bf",)), wr=keys,
                          sig=(gi == ng - 1))
                ez.op("vector", "tensor_copy",
                      h_fm[:, cc * T + g0 * 128: cc * T + (g0 + ng) * 128],
                      psA[:, bank * 512: bank * 512 + ng * 128],
                      rd=keys,
                      wr=tuple(("xn_fm", cc, g0 + gi) for gi in range(ng)))

        ez.op("scalar", "activation", A["c_sil"][:], A["c_fm"][:], AF.Silu,
              rd=(("c_fm",),), wr=(("c_sil",),))
        ez.op("vector", "memset", A["mod_col"][:], 0.0, wr=(("mod_col", 0),))
        for i in range(NCC):
            for jb in range(NJ):
                bank = (i % 2) * 2 + (jb % 2)
                keys = pk("psB", bank * 512, 512)
                slot = jb // 2
                ez.op("tensor", "matmul",
                      psB[:, bank * 512 + slot:bank * 512 + slot + 1],
                      wmst_v[i % 2][:, jb * 128:(jb + 1) * 128],
                      A["c_sil"][:, i:i + 1],
                      start=True, stop=True,
                      rd=(("wmst", i % 2), ("c_sil",)), wr=keys,
                      sig=(jb >= NJ - 2))
            for b2 in range(2):
                bank = (i % 2) * 2 + b2
                keys = pk("psB", bank * 512, 512)
                dst = A["modpart"][:].rearrange("p (sl two) -> p sl two", two=2) \
                    [:, :, b2:b2 + 1]
                ez.op("vector", "tensor_copy", dst,
                      psB[:, bank * 512:bank * 512 + NJ // 2].unsqueeze(2),
                      rd=keys, wr=(("modpart", b2),))
            if i + 2 < NCC:
                load_wmod(i + 2)
            ez.op("vector", "tensor_tensor", A["mod_col"][:], A["mod_col"][:],
                  A["modpart"][:], ALU.add,
                  rd=tuple(("modpart", b2) for b2 in range(2)) +
                     (("mod_col", i),),
                  wr=(("mod_col", i + 1),))
        MODK = ("mod_col", NCC)
        mc = A["mod_col"]
        g_sa, g_ca, g_ga, g_sb, g_cb, g_gb = range(6)
        ez.op("vector", "tensor_scalar", A["ca1_col"][:],
              mc[:, g_ca * NCC:(g_ca + 1) * NCC], 1.0, None, ALU.add,
              rd=(MODK,), wr=(("ca1_col",),))
        ez.op("vector", "tensor_copy", A["sa_col"][:],
              mc[:, g_sa * NCC:(g_sa + 1) * NCC], rd=(MODK,), wr=(("sa_col",),))
        for gsl, nm in ((g_ga, "ga_col"), (g_gb, "gb_col")):
            sl = mc[:, gsl * NCC:(gsl + 1) * NCC]
            ez.op("scalar", "sign", A["sgn32"][:], sl, rd=(MODK,),
                  wr=(("sgn32",),))
            ez.op("scalar", "activation", A[nm][:], sl, AF.Abs,
                  rd=(MODK,), wr=((nm, 0),))
            ez.op("vector", "tensor_scalar", A[nm][:], A[nm][:], 0.25, None,
                  ALU.max, rd=((nm, 0),), wr=((nm, 1),))
            ez.op("vector", "tensor_tensor", A[nm][:], A[nm][:], A["sgn32"][:],
                  ALU.mult, rd=((nm, 1), ("sgn32",)), wr=((nm,),))

        def col_to_rep(col_ap, col_key, rep_name, add_one):
            keys = pk("psB", 0, C)
            for cc in range(NCC):
                ez.op("tensor", "matmul", psB[0:1, cc * 128:(cc + 1) * 128],
                      col_ap[:, cc:cc + 1], A["ident_f32"][:],
                      start=True, stop=True,
                      rd=(col_key, ("ident_f32",)), wr=keys,
                      sig=(cc == NCC - 1))
            ez.op("vector", "tensor_copy", rowtmp[0:1, :], psB[0:1, 0:C],
                  rd=keys, wr=(("rowtmp",),))
            keys2 = pk("psB", 1024, C)
            for h2 in range(NO2):
                ez.op("tensor", "matmul",
                      psB[:, 1024 + h2 * OW:1024 + (h2 + 1) * OW],
                      A["ones_row_f32"][:], rowtmp[0:1, h2 * OW:(h2 + 1) * OW],
                      start=True, stop=True,
                      rd=(("rowtmp",), ("ones_row_f32",)), wr=keys2,
                      sig=(h2 == NO2 - 1))
            ez.op("vector", "tensor_scalar", A[rep_name][:],
                  psB[:, 1024:1024 + C], 1.0 if add_one else 0.0, None, ALU.add,
                  rd=keys2, wr=((rep_name,),))

        col_to_rep(mc[:, g_cb * NCC:(g_cb + 1) * NCC], MODK, "cb1_rep", True)
        col_to_rep(mc[:, g_sb * NCC:(g_sb + 1) * NCC], MODK, "sb_rep", False)
        col_to_rep(A["ga_col"][:], ("ga_col",), "ga_rep", False)
        col_to_rep(A["gb_col"][:], ("gb_col",), "gb_rep", False)

        for cc in range(NCC):
            ez.op("vector", "tensor_scalar",
                  h_fm[:, cc * T:(cc + 1) * T], h_fm[:, cc * T:(cc + 1) * T],
                  A["ca1_col"][:, cc:cc + 1], A["sa_col"][:, cc:cc + 1],
                  ALU.mult, ALU.add,
                  rd=tuple(("xn_fm", cc, i) for i in range(NT)) +
                     (("ca1_col",), ("sa_col",)),
                  wr=(("h_fm", cc),))

        ez.barrier()

        wq_work = []

        def wqd(n=1):
            for _ in range(n):
                if wq_work:
                    wq_work.pop(0)()

        st_i = [0]

        def quant_weights_pieces(dname, n_chunks, row_len, sumoff, scol,
                                 sinvcol, negcol, dst_hbm, dst_name, mean_n,
                                 auxcol, pw):
            npc = row_len // pw
            np_tot = n_chunks * npc

            def stage(pi):
                ci, pj = divmod(pi, npc)
                bidx = st_i[0] % 4
                buf = wstage_v[bidx]
                src = d[dname][ci * 128:(ci + 1) * 128, pj * pw:(pj + 1) * pw]
                ez.dma("gpsimd", buf[:, 0:pw], src, "wst%d" % bidx,
                       wr=(("wstage", bidx),))
                st_i[0] += 1
                return buf, ("wstage", bidx)

            def mk_mean(pi):
                def go():
                    buf, bk = stage(pi)
                    ez.op("vector", "tensor_reduce",
                          A["wsums"][:, sumoff + pi:sumoff + pi + 1],
                          buf[:, 0:pw], axis=AX.X, op=ALU.add,
                          apply_absolute_value=True,
                          rd=(bk,), wr=(("wsums", dname, pi),))
                return go

            for pi in range(np_tot):
                wq_work.append(mk_mean(pi))

            def finalize():
                allk = tuple(("wsums", dname, pi) for pi in range(np_tot))
                ez.op("vector", "reduce_sum", A["wsc"][:, auxcol:auxcol + 1],
                      A["wsums"][:, sumoff:sumoff + np_tot], axis=AX.X,
                      rd=allk, wr=(("wsc", dname, "ps"),))
                keys = pk("psA", 1024, 128)
                ez.op("tensor", "matmul", psA[0:1, 1024:1152],
                      A["wsc"][:, auxcol:auxcol + 1], A["ident_f32"][:],
                      start=True, stop=True,
                      rd=(("wsc", dname, "ps"), ("ident_f32",)), wr=keys,
                      sig=True)
                ez.op("vector", "tensor_reduce", A["wrow"][0:1, 0:1],
                      psA[0:1, 1024:1152], axis=AX.X, op=ALU.add,
                      rd=keys, wr=(("wrow", dname, "tot"),))
                ez.op("vector", "tensor_scalar", A["wrow"][0:1, 1:2],
                      A["wrow"][0:1, 0:1], 1.0 / mean_n, 1e-5, ALU.mult,
                      ALU.max,
                      rd=(("wrow", dname, "tot"),),
                      wr=(("wrow", dname, "mean"),))
                ez.op("vector", "reciprocal", A["wrow"][0:1, 2:3],
                      A["wrow"][0:1, 1:2],
                      rd=(("wrow", dname, "mean"),), wr=(("wrow", dname, "s"),))
                keys2 = pk("psA", 1280, 2)
                ez.op("tensor", "matmul", psA[:, 1280:1282],
                      A["ones_row_f32"][:], A["wrow"][0:1, 1:3],
                      start=True, stop=True,
                      rd=(("wrow", dname, "mean"), ("wrow", dname, "s"),
                          ("ones_row_f32",)),
                      wr=keys2, sig=True)
                ez.op("vector", "tensor_copy",
                      A["wsc"][:, sinvcol:sinvcol + 1], psA[:, 1280:1281],
                      rd=keys2, wr=(("wsc", "sinv", dname),))
                ez.op("vector", "tensor_copy", A["wsc"][:, scol:scol + 1],
                      psA[:, 1281:1282], rd=keys2, wr=(("wsc", "s", dname),))
                ez.op("vector", "tensor_scalar", A["wsc"][:, negcol:negcol + 1],
                      A["wsc"][:, scol:scol + 1], -1.0, None, ALU.mult,
                      rd=(("wsc", "s", dname),), wr=(("wsc", "negs", dname),))

            wq_work.append(finalize)

            def mk_tern(pi):
                def go():
                    ci, pj = divmod(pi, npc)
                    buf, bk = stage(pi)
                    w_sl = buf[:, 0:pw]
                    ez.op("vector", "tensor_scalar", wqstg[:, 0:pw], w_sl,
                          A["wsc"][:, scol:scol + 1], 0.5, ALU.mult, ALU.is_ge,
                          rd=(bk, ("wsc", "s", dname)),
                          wr=(("wqstg",), ("rowtmp",), ("lnA", 0),
                              ("lnA", 1)))
                    ez.op("vector", "tensor_scalar", w_sl, w_sl,
                          A["wsc"][:, negcol:negcol + 1], 0.5, ALU.mult,
                          ALU.is_ge,
                          rd=(bk, ("wsc", "negs", dname)), wr=(bk,))
                    ez.op("vector", "tensor_tensor", tern8[:, 0:pw],
                          wqstg[:, 0:pw], w_sl, ALU.subtract,
                          rd=(("wqstg",), bk), wr=(("tern8",),))
                    ez.dma("sync",
                           dst_hbm[:, ci * row_len + pj * pw:
                                   ci * row_len + (pj + 1) * pw],
                           tern8[:, 0:pw], "wqh",
                           rd=(("tern8",),), wr=((dst_name, pi),))
                return go

            for pi in range(np_tot):
                wq_work.append(mk_tern(pi))

        quant_weights_pieces("w1T", NCC, HID, 0, 0, 1, 6, wq1h, "wq1h",
                             C * HID, 4, PW1)
        quant_weights_pieces("w2T", NHC, C, NW1P, 2, 3, 7, wq2h, "wq2h",
                             C * HID, 5, PW2)

        def proj_tm_tile(i, wname, wslot_i, brow, psn):
            ps = PS[psn]
            keys = pk(psn, 0, C)
            for ci in range(NCC):
                lhsT = h_fm[:, ci * T + i * 128: ci * T + (i + 1) * 128]
                for oh in range(NO2):
                    ez.op("tensor", "matmul",
                          ps[:, oh * OW:(oh + 1) * OW], lhsT,
                          wslot[wslot_i][:, ci * C + oh * OW:
                                         ci * C + (oh + 1) * OW],
                          start=(ci == 0), stop=False,
                          rd=(("h_fm", ci), (wname,)), wr=keys, sig=False)
            for oh in range(NO2):
                ez.op("tensor", "matmul", ps[:, oh * OW:(oh + 1) * OW],
                      A["ones_row_bf"][:],
                      A[brow][0:1, oh * OW:(oh + 1) * OW],
                      start=False, stop=True,
                      rd=(("ones_row_bf",), (brow,)), wr=keys,
                      sig=(oh == NO2 - 1))
            return ps, keys

        for i in range(NT):
            psn = "psA" if i % 2 == 0 else "psB"
            ps, keys = proj_tm_tile(i, "wvT", 0, "bv_row", psn)
            ez.op("scalar", "activation", v_tm[:, i * C:(i + 1) * C],
                  ps[:, 0:C], AF.Identity, rd=keys, wr=(("v_tm", i),))
            wqd(3)
        load_w("wqT", 0, "wq", claims=(("wvT",),))

        kst = A["kstats"]
        for oc in range(NCC):
            psn = "psA" if oc % 2 == 0 else "psB"
            opn = "psB" if oc % 2 == 0 else "psA"
            ps = PS[psn]
            stk_e = pk(psn, 1536, 64)
            stk_o = pk(opn, 1536, 64)
            for half in range(NKH):
                keys = pk(psn, 0, THALF)
                for ci in range(NCC):
                    lhsT = wslot[1][:, ci * C + oc * 128: ci * C + (oc + 1) * 128]
                    for tq in range(THALF // TQW):
                        t0 = half * THALF + tq * TQW
                        ez.op("tensor", "matmul",
                              ps[:, tq * TQW:(tq + 1) * TQW], lhsT,
                              h_fm[:, ci * T + t0: ci * T + t0 + TQW],
                              start=(ci == 0), stop=(ci == NCC - 1),
                              rd=(("wkT",), ("h_fm", ci)), wr=keys,
                              sig=(ci == NCC - 1 and tq == THALF // TQW - 1))
                scol = (NCC + oc) if half == 0 else oc
                ez.op("scalar", "activation", kexp_fm[:, 0:THALF],
                      ps[:, 0:THALF], AF.Exp,
                      accum_out=kst[:, scol:scol + 1],
                      rd=keys, wr=(("kexp",), ("kst", "sum", oc, half),
                                   ("scrb0",)))
                for it in range(KHT):
                    i = half * KHT + it
                    bank = it % 2
                    keys2 = pk(psn, bank * 512, 128)
                    ksl = kslice_v[i % 2]
                    ez.op("tensor", "matmul",
                          ps[:, bank * 512:bank * 512 + 128],
                          kexp_fm[:, it * 128:(it + 1) * 128],
                          A["ident_bf"][:], start=True, stop=True,
                          rd=(("kexp",), ("ident_bf",)), wr=keys2, sig=True)
                    ez.op("scalar", "activation", ksl,
                          ps[:, bank * 512:bank * 512 + 128], AF.Identity,
                          rd=keys2, wr=(("ksl", i % 2),))
                    ez.op("tensor", "matmul",
                          ps[0:64, 1536:1600],
                          ksl[:, 0:64],
                          v_tm[:, i * C + oc * 128: i * C + oc * 128 + 64],
                          start=(i == 0), stop=(i == NT - 1),
                          tile_position=(0, 0),
                          rd=(("ksl", i % 2), ("v_tm", i)),
                          wr=stk_e, sig=True)
                    ez.op("tensor", "matmul",
                          PS[opn][64:128, 1536:1600],
                          ksl[:, 64:128],
                          v_tm[:, i * C + oc * 128 + 64:
                               i * C + oc * 128 + 128],
                          start=(i == 0), stop=(i == NT - 1),
                          tile_position=(0, 64),
                          rd=(("ksl", i % 2), ("v_tm", i)),
                          wr=stk_o, sig=True)
                wqd(3)
            if NKH > 1:
                ez.op("vector", "tensor_tensor", kst[:, NCC + oc:NCC + oc + 1],
                      kst[:, NCC + oc:NCC + oc + 1], kst[:, oc:oc + 1],
                      ALU.add,
                      rd=(("kst", "sum", oc, 0), ("kst", "sum", oc, 1)),
                      wr=(("kst", "tot", oc),))
            ez.op("scalar", "activation",
                  A["state_sb"][0:64, oc * D:(oc + 1) * D],
                  PS[psn][0:64, 1536:1600], AF.Identity,
                  rd=stk_e, wr=(("state0", oc),))
            ez.op("scalar", "activation",
                  A["state_sb"][64:128, oc * D:(oc + 1) * D],
                  PS[opn][64:128, 1536:1600], AF.Identity,
                  rd=stk_o, wr=(("state1", oc),))
            wqd(2)
        ez.op("vector", "reciprocal", kst[:, NCC:2 * NCC], kst[:, NCC:2 * NCC],
              rd=tuple(("kst", "tot" if NKH > 1 else "sum", oc) +
                       (() if NKH > 1 else (0,))
                       for oc in range(NCC)),
              wr=(("kst", "recip"),))
        for oc in range(NCC):
            for pb in (0, 64):
                ez.op("scalar", "activation",
                      A["state_bf"][pb:pb + 64, oc * D:(oc + 1) * D],
                      A["state_sb"][pb:pb + 64, oc * D:(oc + 1) * D],
                      AF.Identity,
                      scale=kst[pb:pb + 64, NCC + oc:NCC + oc + 1],
                      rd=(("state%d" % (pb // 64), oc), ("kst", "recip")),
                      wr=(("state%d" % (pb // 64), oc, "s"),))
        load_w("woT", 1, "wo", claims=(("wkT",),))

        ez.barrier()

        for i in range(NT):
            psn = "psA" if i % 2 == 0 else "psB"
            ps, keys = proj_tm_tile(i, "wqT", 0, "bq_row", psn)
            qe = qe_v[i % 2]
            qek = ("qe", i % 2)
            qs = A["qstat"]
            ez.op("vector", "tensor_reduce", qs[:, 2 * H:2 * H + 1],
                  ps[:, 0:C], axis=AX.X, op=ALU.max, negate=True,
                  rd=keys, wr=(("qmx", i),))
            ez.op("scalar", "activation", qe, ps[:, 0:C], AF.Exp,
                  bias=qs[:, 2 * H:2 * H + 1],
                  rd=keys + (("qmx", i),), wr=(qek,))
            ez.op("vector", "tensor_reduce", qs[:, 0:H].unsqueeze(2),
                  qe.rearrange("p (h dd) -> p h dd", dd=D), axis=AX.X,
                  op=ALU.add, rd=(qek,), wr=(("qs", i),))
            ez.op("vector", "reciprocal", qs[:, H:2 * H], qs[:, 0:H],
                  rd=(("qs", i),), wr=(("qsr", i),))
            qtb = qtb_s[:, (i % 2) * C:(i % 2 + 1) * C]
            ez.op("vector", "tensor_tensor",
                  qtb.rearrange("p (h dd) -> p h dd", dd=D),
                  qe.rearrange("p (h dd) -> p h dd", dd=D),
                  qs[:, H:2 * H].unsqueeze(2).broadcast_to([128, H, D]),
                  ALU.mult,
                  rd=(qek, ("qsr", i)), wr=(("qtb", i % 2),))
            for gq in range(0, NCC, 4):
                bank = 2 + (gq // 4) % 2
                ngq = min(4, NCC - gq)
                keys2 = pk(psn, bank * 512, ngq * 128)
                for gi in range(ngq):
                    cc = gq + gi
                    ez.op("tensor", "matmul",
                          ps[:, bank * 512 + gi * 128:
                             bank * 512 + (gi + 1) * 128],
                          qtb[:, cc * 128:(cc + 1) * 128],
                          A["ident_bf"][:], start=True, stop=True,
                          rd=(("qtb", i % 2), ("ident_bf",)), wr=keys2,
                          sig=(gi == ngq - 1))
                dst = q_fm.rearrange("p (hp t) -> p hp t", t=T) \
                    [:, gq:gq + ngq, i * 128:(i + 1) * 128]
                ez.op("scalar", "activation", dst,
                      ps[:, bank * 512:bank * 512 + ngq * 128]
                      .rearrange("p (n w) -> p n w", w=128), AF.Identity,
                      rd=keys2,
                      wr=tuple(("q_fm", gq + gi, i) for gi in range(ngq)))
            wqd(3)

        ez.barrier()

        for hp in range(HP):
            psn = "psA" if hp % 2 == 0 else "psB"
            ps = PS[psn]
            keys = pk(psn, 0, T)
            for tq in range(NT4):
                for pb in (0, 64):
                    ez.op("tensor", "matmul",
                          ps[pb:pb + 64, tq * TQW:(tq + 1) * TQW],
                          A["state_bf"][pb:pb + 64, hp * D:(hp + 1) * D],
                          q_fm[pb:pb + 64, hp * T + tq * TQW:
                               hp * T + (tq + 1) * TQW],
                          start=True, stop=True, tile_position=(pb, pb),
                          rd=(("state%d" % (pb // 64), hp, "s"),
                              ("q_fm", hp, 4 * tq)) +
                             tuple(("q_fm", hp, 4 * tq + z)
                                   for z in range(1, TQW // 128)),
                          wr=keys, sig=(pb == 64))
            ez.op("scalar", "activation", attn_fm[:, hp * T:(hp + 1) * T],
                  ps[:, 0:T], AF.Identity, rd=keys, wr=(("attn_fm", hp),))
            wqd(2)
        wqd(len(wq_work))

        for hp in range(HP):
            sl = attn_fm[:, hp * T:(hp + 1) * T]
            sq = sqb_v[hp % 2]
            ez.op("scalar", "activation", sq, sl, AF.Square,
                  rd=(("attn_fm", hp),), wr=(("sqbuf", hp % 2),))
            psn, psq = ("psA", "psB")
            for tq in range(NT4):
                keysS = pk(psn, tq * TQW, TQW)
                keysQ = pk(psq, tq * TQW, TQW)
                ez.op("tensor", "matmul",
                      PS[psn][:, tq * TQW:(tq + 1) * TQW],
                      A["onesmat_bf"][:], sl[:, tq * TQW:(tq + 1) * TQW],
                      start=(hp == 0), stop=(hp == HP - 1),
                      rd=(("attn_fm", hp), ("onesmat_bf",)), wr=keysS,
                      sig=(tq == NT4 - 1))
                ez.op("tensor", "matmul",
                      PS[psq][:, tq * TQW:(tq + 1) * TQW],
                      A["onesmat_bf"][:], sq[:, tq * TQW:(tq + 1) * TQW],
                      start=(hp == 0), stop=(hp == HP - 1),
                      rd=(("sqbuf", hp % 2), ("onesmat_bf",)), wr=keysQ,
                      sig=(tq == NT4 - 1))
        pkS = pk("psA", 0, T)
        pkQ = pk("psB", 0, T)
        ez.op("vector", "tensor_scalar", m_rep, psA[:, 0:T], inv_C, None,
              ALU.mult, rd=pkS, wr=(("m_rep",),))
        ez.op("vector", "tensor_scalar", v_rep, psB[:, 0:T], inv_C, None,
              ALU.mult, rd=pkQ, wr=(("v_rep", 0), ("wstage", 0),
                                    ("wstage", 1)))
        ez.op("vector", "scalar_tensor_tensor", ascr_v, m_rep, -1.0, m_rep,
              ALU.mult, ALU.mult,
              rd=(("m_rep",),),
              wr=(("ascr",), ("sqbuf", 1)))
        ez.op("vector", "tensor_tensor", v_rep, v_rep, ascr_v, ALU.add,
              rd=(("v_rep", 0), ("ascr",)), wr=(("v_rep", 1),))
        ez.op("scalar", "activation", v_rep, v_rep, AF.Sqrt,
              bias=A["epsv"][:, 0:1],
              rd=(("v_rep", 1), ("epsv",)), wr=(("v_rep", "sd"),))
        ez.op("vector", "reciprocal", v_rep, v_rep,
              rd=(("v_rep", "sd"),), wr=(("v_rep", "rs"),))
        ez.op("vector", "tensor_copy", a_rep, v_rep,
              rd=(("v_rep", "rs"),), wr=(("a_rep",), ("ascr",),
                                         ("sqbuf", 1)))
        ez.op("vector", "tensor_tensor", m_rep, m_rep, v_rep, ALU.mult,
              rd=(("m_rep",), ("v_rep", "rs")), wr=(("m_rep", "mrs"),))
        ez.op("vector", "tensor_scalar", b_rep, m_rep, -1.0, None, ALU.mult,
              rd=(("m_rep", "mrs"),), wr=(("b_rep",), ("sqbuf", 0),
                                          ("kexp",)))

        for hp in range(HP):
            sl = attn_fm[:, hp * T:(hp + 1) * T]
            ez.op("vector", "tensor_tensor", sl, sl, a_rep, ALU.mult,
                  rd=(("attn_fm", hp), ("a_rep",)), wr=(("attn_fm", hp, "s"),))
            ez.op("vector", "tensor_tensor", sl, sl, b_rep, ALU.add,
                  rd=(("attn_fm", hp, "s"), ("b_rep",)),
                  wr=(("attn_fm", hp, "b"),))
            ez.op("scalar", "activation", sl, sl, AF.Identity,
                  scale=A["lng_fm"][:, hp:hp + 1], bias=A["lnb_fm"][:, hp:hp + 1],
                  rd=(("attn_fm", hp, "b"), ("lng_fm",), ("lnb_fm",)),
                  wr=(("lnout", hp),))

        def load_xres(i):
            ez.dma("sync", xres_v[i % 2], d["x"][i * 128:(i + 1) * 128, :],
                   "xr%d" % (i % 2),
                   wr=(("xres", i % 2), ("wstage", 0), ("wstage", 1),
                       ("wstage", 2), ("wstage", 3)))

        for _i in range(min(2, NT)):
            load_xres(_i)
        for i in range(NT):
            psn = "psA" if i % 2 == 0 else "psB"
            ps = PS[psn]
            keys = pk(psn, 0, C)
            for ci in range(NCC):
                lhsT = attn_fm[:, ci * T + i * 128: ci * T + (i + 1) * 128]
                for oh in range(NO2):
                    ez.op("tensor", "matmul", ps[:, oh * OW:(oh + 1) * OW],
                          lhsT,
                          wslot[1][:, ci * C + oh * OW: ci * C + (oh + 1) * OW],
                          start=(ci == 0), stop=False,
                          rd=(("lnout", ci), ("woT",)), wr=keys, sig=False)
            for oh in range(NO2):
                ez.op("tensor", "matmul", ps[:, oh * OW:(oh + 1) * OW],
                      A["ones_row_bf"][:], A["bo_row"][0:1, oh * OW:(oh + 1) * OW],
                      start=False, stop=True,
                      rd=(("ones_row_bf",), ("bo_row",)), wr=keys,
                      sig=(oh == NO2 - 1))
            xrk = ("xres", i % 2)
            xok = ("xout", i % 2)
            ez.op("vector", "tensor_tensor", f2tmp_v, ps[:, 0:C],
                  A["ga_rep"][:], ALU.mult,
                  rd=keys + (("ga_rep",),), wr=(("f2s",), ("wsgn",)))
            ez.op("gpsimd", "tensor_tensor", xout_v[i % 2], f2tmp_v,
                  xres_v[i % 2], ALU.add,
                  rd=(("f2s",), xrk), wr=(xok,))
            ez.dma("sync", out_hbm[i * 128:(i + 1) * 128, :], xout_v[i % 2],
                   "xm%d" % (i % 2), rd=(xok,), wr=(("xmid", i),))
            if i + 2 < NT:
                load_xres(i + 2)

        ez.barrier(dmas=("xm0", "xm1", "wqh"))

        ez.op("vector", "tensor_scalar", A["tapsf"][:], A["convw_fm"][:],
              A["wsc"][:, 1:2], None, ALU.mult,
              rd=(("convw_fm",), ("wsc", "sinv", "w1T")), wr=(("tapsf",),))

        q1_idx = [0]
        q2_idx = [0]

        def load_q1():
            idx = q1_idx[0]
            if idx >= NFC * NQ:
                return
            q = idx % NQ
            slot = idx % 2
            src = wq1h.rearrange("p (cc z) -> p cc z", z=HID) \
                [:, :, q * (HID // NQ):(q + 1) * (HID // NQ)]
            ez.dma("sync", wq1r[slot].rearrange("p (cc z) -> p cc z",
                                                z=HID // NQ),
                   src, "wr1a" if slot == 0 else "wr1b",
                   wr=(("wq1r", slot),))
            q1_idx[0] += 1

        def load_q2():
            idx = q2_idx[0]
            if idx >= NFC * NQ:
                return
            q = idx % NQ
            slot = idx % 2
            src = wq2h[:, q * QW2:(q + 1) * QW2]
            ez.dma("scalar", wq2r[slot], src,
                   "wr2a" if slot == 0 else "wr2b",
                   wr=(("wq2r", slot),))
            q2_idx[0] += 1

        load_q1()
        load_q1()
        load_q2()
        load_q2()

        def ln2_quant(j):
            xcb = xc_v[j % 2]
            xck = ("xc", j % 2)
            src = out_hbm[j * TC:(j + 1) * TC, :] \
                .rearrange("(j2 p) c -> p j2 c", p=128)
            fstk = tuple(("fst", t, t2) for t in ("sum", "sq", "amax")
                         for t2 in range(TT)) + \
                tuple(("fst", t) for t in ("m", "v0", "m2", "var", "sd", "rs",
                                           "nb0", "nb", "s1inv", "s1"))
            ez.dma("sync", xcb.rearrange("p (j2 c) -> p j2 c", c=C), src,
                   "xc%d" % (j % 2),
                   rd=tuple(("xmid", jj) for jj in range(NT)),
                   wr=(xck,) + fstk)
            fst = A["fstats"]
            for t2 in range(TT):
                xt = xcb[:, t2 * C:(t2 + 1) * C]
                ez.op("vector", "reduce_sum", fst[:, t2:t2 + 1], xt, axis=AX.X,
                      rd=(xck,), wr=(("fst", "sum", t2),))
                ez.op("scalar", "activation", sq_out_bf, xt, AF.Square,
                      accum_out=fst[:, TT + t2:TT + t2 + 1],
                      rd=(xck,),
                      wr=(("fst", "sq", t2), ("scrb0",)) +
                         tuple(("run", g) for g in range(NG)))
            sm_ = fst[:, 0:TT]; sq_ = fst[:, TT:2 * TT]; ax_ = fst[:, 2 * TT:3 * TT]
            kk = tuple(("fst", t, t2) for t in ("sum", "sq") for t2 in range(TT))
            ez.op("vector", "tensor_scalar", sm_, sm_, inv_C, None, ALU.mult,
                  rd=kk, wr=(("fst", "m"),))
            ez.op("vector", "tensor_scalar", sq_, sq_, inv_C, None, ALU.mult,
                  rd=kk + (("fst", "m"),), wr=(("fst", "v0"),))
            ez.op("vector", "tensor_tensor", ax_, sm_, sm_, ALU.mult,
                  rd=(("fst", "m"),), wr=(("fst", "m2"),))
            ez.op("vector", "tensor_tensor", sq_, sq_, ax_, ALU.subtract,
                  rd=(("fst", "v0"), ("fst", "m2")), wr=(("fst", "var"),))
            ez.op("scalar", "activation", sq_, sq_, AF.Sqrt,
                  bias=A["epsv"][:, 0:1],
                  rd=(("fst", "var"), ("epsv",)), wr=(("fst", "sd"),))
            ez.op("vector", "reciprocal", sq_, sq_,
                  rd=(("fst", "sd"),), wr=(("fst", "rs"),))
            ez.op("vector", "tensor_tensor", ax_, sm_, sq_, ALU.mult,
                  rd=(("fst", "m"), ("fst", "rs")), wr=(("fst", "nb0"),))
            ez.op("vector", "tensor_scalar", ax_, ax_, -1.0, None, ALU.mult,
                  rd=(("fst", "nb0"),), wr=(("fst", "nb"),))
            for t2 in range(TT):
                ez.op("scalar", "activation", xn2c[:, t2 * C:(t2 + 1) * C],
                      xcb[:, t2 * C:(t2 + 1) * C], AF.Identity,
                      scale=sq_[:, t2:t2 + 1], bias=ax_[:, t2:t2 + 1],
                      rd=(xck, ("fst", "rs"), ("fst", "nb")),
                      wr=(("xn2c", t2, 0), ("xn2c", t2, 3)))
                ez.op("vector", "tensor_tensor", xn2c[:, t2 * C:(t2 + 1) * C],
                      xn2c[:, t2 * C:(t2 + 1) * C], A["cb1_rep"][:], ALU.mult,
                      rd=(("xn2c", t2, 0), ("cb1_rep",)), wr=(("xn2c", t2, 1),))
                ez.op("vector", "tensor_tensor", xn2c[:, t2 * C:(t2 + 1) * C],
                      xn2c[:, t2 * C:(t2 + 1) * C], A["sb_rep"][:], ALU.add,
                      rd=(("xn2c", t2, 1), ("sb_rep",)), wr=(("xn2c", t2, 2),))
                ez.op("vector", "tensor_reduce",
                      fst[:, 3 * TT + t2:3 * TT + t2 + 1],
                      xn2c[:, t2 * C:(t2 + 1) * C], axis=AX.X, op=ALU.max,
                      apply_absolute_value=True,
                      rd=(("xn2c", t2, 2),), wr=(("fst", "amax", t2),))
            am = fst[:, 3 * TT:4 * TT]
            ez.op("vector", "tensor_scalar", am, am, 1e-5, 1.0 / 127.0,
                  ALU.max, ALU.mult,
                  rd=tuple(("fst", "amax", t2) for t2 in range(TT)),
                  wr=(("fst", "s1inv"),))
            s1 = fst[:, 4 * TT:5 * TT]
            ez.op("vector", "reciprocal", s1, am,
                  rd=(("fst", "s1inv"),), wr=(("fst", "s1"),))
            for t2 in range(TT):
                xsl = xn2c[:, t2 * C:(t2 + 1) * C]
                ez.op("vector", "tensor_scalar", xsl, xsl, s1[:, t2:t2 + 1],
                      M32, ALU.mult, ALU.add,
                      rd=(("xn2c", t2, 2), ("fst", "s1")), wr=(("xn2c", t2, 3),))
                ez.op("scalar", "activation", lnA, xsl, AF.Identity,
                      bias=A["nm32c"][:, 0:1],
                      rd=(("xn2c", t2, 3), ("nm32c",)), wr=(("lnA", 0),))
                ez.op("scalar", "activation", xsl, lnA, AF.Identity,
                      scale=A["hsel"][:, 0:1],
                      rd=(("lnA", 0), ("hsel",)), wr=(("xn2c", t2, 4),))
                ez.op("scalar", "activation", lnA, lnA, AF.Identity,
                      bias=A["m52c"][:, 0:1],
                      rd=(("lnA", 0), ("m52c",)), wr=(("lnA", 1),))
                ez.op("vector", "scalar_tensor_tensor", xsl, lnA, -M52, xsl,
                      ALU.add, ALU.add,
                      rd=(("lnA", 1), ("xn2c", t2, 4)), wr=(("xn2c", t2, 5),))
                ez.op("vector", "tensor_scalar", hqs_tm[:, t2 * C:(t2 + 1) * C],
                      xsl, am[:, t2:t2 + 1], None, ALU.mult,
                      rd=(("xn2c", t2, 5), ("fst", "s1inv")),
                      wr=(("hqs_tm", t2),))
            hb = hqs_v[j % 2]
            hbn = "hqs%d" % (j % 2)
            for t2 in range(TT):
                for g0 in range(0, NCC, 4):
                    bank = (g0 // 4) % 2
                    ng = min(4, NCC - g0)
                    keys2 = pk("psB", bank * 512, ng * 128)
                    for gi in range(ng):
                        cc = g0 + gi
                        ez.op("tensor", "matmul",
                              psB[:, bank * 512 + gi * 128:
                                  bank * 512 + (gi + 1) * 128],
                              hqs_tm[:, t2 * C + cc * 128:
                                     t2 * C + (cc + 1) * 128],
                              A["ident_bf"][:], start=True, stop=True,
                              rd=(("hqs_tm", t2), ("ident_bf",)), wr=keys2,
                              sig=(gi == ng - 1))
                    for gi in range(ng):
                        cc = g0 + gi
                        ez.op("scalar", "activation",
                              hb[:, cc * HW_ + 1 + t2 * 128:
                                 cc * HW_ + 1 + (t2 + 1) * 128],
                              psB[:, bank * 512 + gi * 128:
                                  bank * 512 + (gi + 1) * 128], AF.Identity,
                              rd=keys2, wr=((hbn, "mid", cc, t2),))

        def hqs_edges(j):
            hb_prev = hqs_v[(j - 1) % 2]
            hbn_prev = "hqs%d" % ((j - 1) % 2)
            hb_cur = hqs_v[j % 2]
            hbn_cur = "hqs%d" % (j % 2)
            if j < NFC:
                dst = hb_cur.rearrange("p (cc w) -> p cc w", w=HW_)[:, :, 0:1]
                if j == 0:
                    ez.op("vector", "memset", dst, 0.0, wr=((hbn_cur, "edgeL"),))
                else:
                    ez.op("vector", "tensor_copy", dst,
                          hb_prev.rearrange("p (cc w) -> p cc w", w=HW_)
                          [:, :, HW_ - 2:HW_ - 1],
                          rd=tuple((hbn_prev, "mid", cc, TT - 1)
                                   for cc in range(NCC)),
                          wr=((hbn_cur, "edgeL"),))
            if j >= 1:
                dst = hb_prev.rearrange("p (cc w) -> p cc w", w=HW_) \
                    [:, :, HW_ - 1:HW_]
                if j == NFC:
                    ez.op("vector", "memset", dst, 0.0, wr=((hbn_prev, "edgeR"),))
                else:
                    ez.op("vector", "tensor_copy", dst,
                          hb_cur.rearrange("p (cc w) -> p cc w", w=HW_)
                          [:, :, 1:2],
                          rd=tuple((hbn_cur, "mid", cc, 0) for cc in range(NCC)),
                          wr=((hbn_prev, "edgeR"),))

        def ffn1_conv(j1, j3):
            if j1 is not None:
                hb = hqs_v[j1 % 2]
                hbn = "hqs%d" % (j1 % 2)
                h1pad_1 = h1pad_v[j1 % 2]
                hkeys = tuple((hbn, "mid", cc, t2) for cc in range(NCC)
                              for t2 in range(TT)) + \
                    ((hbn, "edgeL"), (hbn, "edgeR"))
            if j3 is not None:
                h1pad_3 = h1pad_v[j3 % 2]
            for hc in range(NHC):
                if j1 is not None:
                    qsl = ((j1 * NQ) + hc // QH) % 2
                    bank = hc % 2
                    keys = pk("psA", bank * 512, HW_)
                    for cc in range(NCC):
                        ez.op("tensor", "matmul",
                              psA[:, bank * 512: bank * 512 + HW_],
                              wq1r[qsl][:, cc * (HID // NQ) +
                                        (hc % QH) * 128:
                                        cc * (HID // NQ) +
                                        (hc % QH + 1) * 128],
                              hb[:, cc * HW_:(cc + 1) * HW_],
                              start=(cc == 0), stop=(cc == NCC - 1),
                              rd=(("wq1r", qsl),) + hkeys, wr=keys,
                              sig=(cc == NCC - 1))
                    ez.op("scalar", "activation",
                          h1pad_1[:, hc * HW_:(hc + 1) * HW_],
                          psA[:, bank * 512: bank * 512 + HW_], AF.Identity,
                          rd=keys, wr=(("h1pad", j1 % 2, hc),))
                    if (hc + 1) % QH == 0:
                        load_q1()
                if j3 is not None:
                    g = hc // GRP
                    left = h1pad_3[:, hc * HW_: hc * HW_ + TC]
                    dstc = convB[:, hc * TC:(hc + 1) * TC]
                    cclaim = (("hq2", g, "m"), ("hq2", g, "r"),
                              ("convB", hc, "s"))
                    ez.op("scalar", "activation", dstc, left, AF.Identity,
                          scale=A["tapsf"][:, hc * 3:hc * 3 + 1],
                          bias=A["convb_fm"][:, hc:hc + 1],
                          rd=(("h1pad", j3 % 2, hc), ("tapsf",), ("convb_fm",)),
                          wr=(("convB", hc, 0),) + cclaim)
            if j3 is not None:
                for hc in range(NHC):
                    mid = h1pad_3[:, hc * HW_ + 1: hc * HW_ + 1 + TC]
                    right = h1pad_3[:, hc * HW_ + 2: hc * HW_ + 2 + TC]
                    dstc = convB[:, hc * TC:(hc + 1) * TC]
                    ez.op("vector", "scalar_tensor_tensor", dstc, mid,
                          A["tapsf"][:, hc * 3 + 1:hc * 3 + 2], dstc,
                          ALU.mult, ALU.add,
                          rd=(("h1pad", j3 % 2, hc), ("convB", hc, 0),
                              ("tapsf",)),
                          wr=(("convB", hc, 1),))
                    ez.op("vector", "scalar_tensor_tensor", dstc, right,
                          A["tapsf"][:, hc * 3 + 2:hc * 3 + 3], dstc,
                          ALU.mult, ALU.add,
                          rd=(("h1pad", j3 % 2, hc), ("convB", hc, 1),
                              ("tapsf",)),
                          wr=(("convB", hc, 2),))
                for hc in range(NHC):
                    g = hc // GRP
                    dstc = convB[:, hc * TC:(hc + 1) * TC]
                    ez.op("scalar", "activation", dstc, dstc, AF.Silu,
                          rd=(("convB", hc, 2),), wr=(("convB", hc, "s"),))
                    if hc % GRP == GRP - 1:
                        ez.op("vector", "tensor_reduce",
                              runmax_s[:, g * TC:(g + 1) * TC],
                              convB[:, g * GRP * TC:(g + 1) * GRP * TC]
                              .rearrange("p (hc t) -> p t hc", t=TC),
                              axis=AX.X, op=ALU.max, apply_absolute_value=True,
                              rd=tuple(("convB", h2, "s")
                                       for h2 in range(g * GRP, (g + 1) * GRP)),
                              wr=(("run", g), ("runs",)))

        def ffn_out(j):
            run = runmax_s
            for g in range(1, NG):
                ez.op("vector", "tensor_tensor", run[:, 0:TC], run[:, 0:TC],
                      run[:, g * TC:(g + 1) * TC], ALU.max,
                      rd=(("run", 0), ("run", g)),
                      wr=(("run", 0),))
            for t2 in range(TT):
                keys2 = pk("psB", 2 * 512, 128)
                s2claim = (("s2c", "s2inv"), ("s2c", "s2"),
                           ("s2c", "s2w")) if t2 == 0 else ()
                ez.op("tensor", "matmul", psB[:, 2 * 512:2 * 512 + 128],
                      run[:, t2 * 128:(t2 + 1) * 128], A["ident_f32"][:],
                      start=True, stop=True,
                      rd=(("run", 0), ("ident_f32",)), wr=keys2, sig=True)
                ez.op("vector", "tensor_reduce", A["s2c"][:, t2:t2 + 1],
                      psB[:, 2 * 512:2 * 512 + 128], axis=AX.X, op=ALU.max,
                      rd=keys2, wr=(("s2c", "max", t2),) + s2claim)
            ez.op("vector", "tensor_scalar", A["s2c"][:, 0:TT],
                  A["s2c"][:, 0:TT], 1e-5, 1.0 / 127.0, ALU.max, ALU.mult,
                  rd=tuple(("s2c", "max", t2) for t2 in range(TT)),
                  wr=(("s2c", "s2inv"),))
            ez.op("vector", "reciprocal", A["s2c"][:, TT:2 * TT],
                  A["s2c"][:, 0:TT], rd=(("s2c", "s2inv"),),
                  wr=(("s2c", "s2"),))
            for t2 in range(TT):
                keys2 = pk("psB", 2 * 512 + 128 + t2 * 128, 128)
                ez.op("tensor", "matmul",
                      psB[0:1, 2 * 512 + 128 + t2 * 128:
                          2 * 512 + 128 + (t2 + 1) * 128],
                      A["s2c"][:, TT + t2:TT + t2 + 1], A["ident_f32"][:],
                      start=True, stop=True,
                      rd=(("s2c", "s2"), ("ident_f32",)), wr=keys2, sig=True)
            rowkeys = pk("psB", 2 * 512 + 128, TT * 128)
            ez.op("vector", "tensor_copy", A["rowtc"][0:1, :],
                  psB[0:1, 2 * 512 + 128: 2 * 512 + 128 + TC],
                  rd=rowkeys, wr=(("rowtc",),))
            keys3 = pk("psB", 3 * 512, TC)
            for tqr in range((TC + 511) // 512):
                n = min(512, TC - tqr * 512)
                ez.op("tensor", "matmul",
                      psB[:, 3 * 512 + tqr * 512: 3 * 512 + tqr * 512 + n],
                      A["ones_row_f32"][:],
                      A["rowtc"][0:1, tqr * 512:tqr * 512 + n],
                      start=True, stop=True,
                      rd=(("rowtc",), ("ones_row_f32",)), wr=keys3,
                      sig=(tqr == (TC + 511) // 512 - 1))
            ez.op("vector", "tensor_copy", A["s2rep"][:],
                  psB[:, 3 * 512: 3 * 512 + TC],
                  rd=keys3, wr=(("s2rep",),))
            ez.op("vector", "tensor_scalar", A["s2c"][:, 2 * TT:3 * TT],
                  A["s2c"][:, 0:TT], A["wsc"][:, 3:4], None, ALU.mult,
                  rd=(("s2c", "s2inv"), ("wsc", "sinv", "w2T")),
                  wr=(("s2c", "s2w"),))
            f2keys = pk("psA", 0, TT * NO2 * 512)
            h1stage = h1pad_v[j % 2]
            for g in range(NG):
                gsl = convB[:, g * GRP * TC:(g + 1) * GRP * TC]
                hsl = hq2[:, g * GRP * TC:(g + 1) * GRP * TC]
                bsl = h1stage[:, 0:GRP * TC]
                silk = tuple(("convB", h2, "s")
                             for h2 in range(g * GRP, (g + 1) * GRP))
                ez.op("vector", "tensor_tensor",
                      hsl.rearrange("p (hc t) -> p hc t", t=TC),
                      gsl.rearrange("p (hc t) -> p hc t", t=TC),
                      A["s2rep"][:].unsqueeze(1).broadcast_to([128, GRP, TC]),
                      ALU.mult,
                      rd=silk + (("s2rep",), ("run", g)),
                      wr=(("hq2", g, "m"), ("hq2", g, "r")) + silk)
                ez.op("vector", "tensor_scalar", hsl, hsl,
                      M32, M32, ALU.add, ALU.subtract,
                      rd=(("hq2", g, "m"),), wr=(("hq2", g, "a"),))
                ez.op("vector", "tensor_scalar", bsl, hsl,
                      M52, M52, ALU.add, ALU.subtract,
                      rd=(("hq2", g, "a"),),
                      wr=(("hq2", g, "b"),) +
                         tuple(("h1pad", j % 2, h2) for h2 in range(NHC)))
                ez.op("vector", "scalar_tensor_tensor", hsl, hsl,
                      A["hsel"][:, 0:1], bsl, ALU.mult, ALU.add,
                      rd=(("hq2", g, "a"), ("hq2", g, "b"), ("hsel",)),
                      wr=(("hq2", g, "r"),))
                qsl = ((j * NQ) + (g * GRP) // QH) % 2
                for hc in range(g * GRP, (g + 1) * GRP):
                    for t2 in range(TT):
                        for oh in range(NO2):
                            slot = (t2 * NO2 + oh) * 512
                            ez.op("tensor", "matmul",
                                  psA[:, slot:slot + OW],
                                  hq2[:, hc * TC + t2 * 128:
                                      hc * TC + (t2 + 1) * 128],
                                  wq2r[qsl][:, (hc % QH) * C + oh * OW:
                                            (hc % QH) * C + (oh + 1) * OW],
                                  start=(hc == 0), stop=(hc == NHC - 1),
                                  rd=(("hq2", g, "r"), ("wq2r", qsl)),
                                  wr=f2keys,
                                  sig=(hc % GRP == GRP - 1 and t2 == TT - 1
                                       and oh == NO2 - 1))
                for _ in range(((g + 1) * GRP) // QH - (g * GRP) // QH):
                    load_q2()
            for t2 in range(TT):
                psrc = psA[:, t2 * NO2 * 512:(t2 + 1) * NO2 * 512] \
                    .rearrange("p (n w) -> p n w", w=512)[:, :, 0:OW]
                ez.op("scalar", "activation",
                      f2tmp_v.rearrange("p (n w) -> p n w", w=OW), psrc,
                      AF.Identity,
                      scale=A["s2c"][:, 2 * TT + t2:2 * TT + t2 + 1],
                      rd=f2keys + (("s2c", "s2w"),),
                      wr=(("f2t", 0), ("f2t", 1)))
                fo_i = (j * TT + t2) % 2
                fo = fout_v[fo_i]
                fok = ("fout", fo_i)
                ez.op("vector", "tensor_tensor", fo, f2tmp_v,
                      A["gb_rep"][:], ALU.mult,
                      rd=(("f2t", 0), ("gb_rep",)), wr=(fok, ("f2t", 1)))
                ez.dma("gpsimd",
                       out_hbm[j * TC + t2 * 128: j * TC + (t2 + 1) * 128, :],
                       fo, "so%d" % fo_i,
                       rd=(fok,), wr=(("outf", j, t2), ("xmid", j * TT + t2)),
                       accum_op=ALU.add)

        for j in range(NFC + 3):
            j1 = j - 2 if 2 <= j < NFC + 2 else None
            j3 = j - 3 if j >= 3 else None
            if j1 is not None or j3 is not None:
                ffn1_conv(j1, j3)
            if j < NFC:
                ln2_quant(j)
            if j <= NFC:
                hqs_edges(j)
            if j3 is not None:
                ffn_out(j3)

        for _sn in ("so0", "so1", "xm0", "xm1", "wqh"):
            nc.sync.wait_ge(ez.dsem[_sn][0], ez.dsem[_sn][1])

    return nc


def _host_inputs(cfg, inputs, core):
    f32 = np.float32
    C, NCC, NHC = cfg.C, cfg.NCC, cfg.NHC
    ct = np.ascontiguousarray
    p = "L0_"
    m = {
        "x": ct(inputs[p + "attn_x"][core].astype(f32)),
        "wqT": ct(inputs[p + "attn_Wq"].astype(f32).T),
        "wkT": ct(inputs[p + "attn_Wk"].astype(f32).T),
        "wvT": ct(inputs[p + "attn_Wv"].astype(f32).T),
        "woT": ct(inputs[p + "attn_Wo"].astype(f32).T),
        "w1T": ct(inputs[p + "ffn_w1"].astype(f32).T),
        "w2T": ct(inputs[p + "ffn_w2"].astype(f32).T),
        "wmod": ct(inputs[p + "wmod"].astype(f32)),
        "c_fm": ct(inputs[p + "c"][0 if inputs[p + "c"].shape[0] == 1 else core]
                   .astype(f32).reshape(NCC, 128).T),
        "bq_row": ct(inputs[p + "attn_bq"].astype(f32).reshape(1, C)),
        "bv_row": ct(inputs[p + "attn_bv"].astype(f32).reshape(1, C)),
        "bo_row": ct(inputs[p + "attn_bo"].astype(f32).reshape(1, C)),
        "lng_fm": ct(inputs[p + "attn_ln_g"].astype(f32).reshape(NCC, 128).T),
        "lnb_fm": ct(inputs[p + "attn_ln_b"].astype(f32).reshape(NCC, 128).T),
        "convw_fm": ct(inputs[p + "ffn_conv_w"].astype(f32)
                       .reshape(NHC, 128, 3).transpose(1, 0, 2)
                       .reshape(128, 3 * NHC)),
        "convb_fm": ct(inputs[p + "ffn_conv_b"].astype(f32)
                       .reshape(NHC, 128).T),
        "ident_bf": np.eye(128, dtype=ml_dtypes.bfloat16),
        "ident_f32": np.eye(128, dtype=f32),
        "ones_row_bf": np.ones((1, 128), dtype=ml_dtypes.bfloat16),
        "ones_row_f32": np.ones((1, 128), dtype=f32),
        "onesmat_bf": np.ones((128, 128), dtype=ml_dtypes.bfloat16),
    }
    return m


_CACHED = {}


def kernel(**inputs):
    cfg = CFG()
    key = "nc"
    if key not in _CACHED:
        _CACHED[key] = build_nc(cfg)
    nc = _CACHED[key]
    in_maps = [_host_inputs(cfg, inputs, core) for core in range(8)]
    res = run_bass_kernel_spmd(nc, in_maps, core_ids=list(range(8)))
    out = np.stack([np.asarray(res.results[core]["out"])
                    for core in range(8)], axis=0)
    return out.astype(np.float32)
